# version 17
# baseline (speedup 1.0000x reference)
import numpy as np
from contextlib import ExitStack

import concourse.bass as bass
import concourse.mybir as mybir
from concourse.bass_utils import run_bass_kernel_spmd

F32 = mybir.dt.float32
BF16 = mybir.dt.bfloat16
U8 = mybir.dt.uint8
I8 = mybir.dt.int8
I32 = mybir.dt.int32
AF = mybir.ActivationFunctionType
ALU = mybir.AluOpType
AX = mybir.AxisListType.X

B = 2
T = 2048
C = 1024
NH = 16
DC = C // 128
NT = T // 128
F = 4096
FC = F // 128
EPS_Q = 1e-8
EPS_LN = 1e-5
NEG = -1e30

ENGS = ("sync", "tensor", "vector", "scalar", "gpsimd")


class KCtx:
    def __init__(self, nc, es):
        self.nc = nc
        self.sems = {}
        self.cnt = {}
        for name in ("ld", "st", "t", "v", "s", "g", "cc"):
            self.sems[name] = es.enter_context(nc.semaphore(name))
            self.cnt[name] = 0

    def inc(self, name, by=1):
        self.cnt[name] += by
        return (name, self.cnt[name])

    def evt(self, name):
        return (name, self.cnt[name])


class Ph:

    def __init__(self, k):
        self.k = k
        self.q = {e: [] for e in ENGS}
        self.marks = {e: {} for e in ENGS}

    def wait(self, eng, evt):
        if evt is None:
            return
        name, val = evt
        if val <= 0 or self.marks[eng].get(name, 0) >= val:
            return
        self.marks[eng][name] = val
        sem = self.k.sems[name]
        self.q[eng].append(lambda e, s=sem, v=val: e.wait_ge(s, v))

    def op(self, eng, fn, inc=None, incby=1):
        if inc is not None:
            sem = self.k.sems[inc]
            evt = self.k.inc(inc, incby)

            def g(e, fn=fn, sem=sem, incby=incby):
                fn(e).then_inc(sem, incby)

            self.q[eng].append(g)
            return evt
        self.q[eng].append(lambda e, fn=fn: fn(e))
        return None

    def dma(self, eng, out, in_):
        name = "ld" if eng == "sync" else "st"
        return self.op(eng, lambda e, o=out, i=in_: e.dma_start(out=o, in_=i),
                       inc=name, incby=16)

    def run(self, nc):
        with nc.Block() as block:
            for eng in ENGS:
                if self.q[eng]:
                    def body(e, lst=tuple(self.q[eng])):
                        for f in lst:
                            f(e)
                    getattr(block, eng)(body)


def build_nc():
    nc = bass.Bass("TRN2", target_bir_lowering=False, debug=False,
                   detect_race_conditions=False)

    x_d = nc.dram_tensor("x", [B, T, C], F32, kind="ExternalInput").ap()
    wq_d = nc.dram_tensor("wq", [C, C], F32, kind="ExternalInput").ap()
    wk_d = nc.dram_tensor("wk", [C, 256], F32, kind="ExternalInput").ap()
    wv_d = nc.dram_tensor("wv", [C, 256], F32, kind="ExternalInput").ap()
    wo_d = nc.dram_tensor("wo", [C, C], F32, kind="ExternalInput").ap()
    rows_d = nc.dram_tensor("rows", [1, 2560], F32, kind="ExternalInput").ap()
    w1_d = nc.dram_tensor("w1", [F, C], F32, kind="ExternalInput").ap()
    w2_d = nc.dram_tensor("w2", [C, F], F32, kind="ExternalInput").ap()
    b1f_d = nc.dram_tensor("b1f", [128, FC], F32, kind="ExternalInput").ap()
    b2f_d = nc.dram_tensor("b2f", [128, DC], F32, kind="ExternalInput").ap()
    y_d = nc.dram_tensor("y", [B, T, C], F32, kind="ExternalOutput").ap()

    r1_d = nc.dram_tensor("r1sc", [B * T, C], F32).ap()
    h_d = nc.dram_tensor("hsc", [FC, 128, B * T], F32).ap()
    m1_d = nc.dram_tensor("m1sc", [128], F32).ap()
    m2_d = nc.dram_tensor("m2sc", [128], F32).ap()
    s1_d = nc.dram_tensor("s1sc", [1], F32).ap()
    s2_d = nc.dram_tensor("s2sc", [1], F32).ap()
    cc1i_d = nc.dram_tensor("cc1i", [1], F32).ap()
    cc1o_d = nc.dram_tensor("cc1o", [8], F32).ap()
    cc2i_d = nc.dram_tensor("cc2i", [1], F32).ap()
    cc2o_d = nc.dram_tensor("cc2o", [8], F32).ap()
    c2r_d = nc.dram_tensor("c2r", [128, DC], F32).ap()
    b2qr_d = nc.dram_tensor("b2qr", [128, DC], F32).ap()
    c2row_d = nc.dram_tensor("c2row", [1024], F32).ap()
    b2row_d = nc.dram_tensor("b2row", [1024], F32).ap()

    es = ExitStack()
    sb = lambda n, s, d: es.enter_context(nc.sbuf_tensor(n, s, d))
    ps = lambda n, s, d: es.enter_context(nc.psum_tensor(n, s, d))
    k = KCtx(nc, es)

    pq = ps("pq", [128, 1024], F32)
    pv2 = ps("pv2", [128, 1024], F32)
    psc = [ps("psc0", [128, 512], F32), ps("psc1", [128, 512], F32)]
    ppt = [ps("ppt0", [128, 512], BF16), ps("ppt1", [128, 512], BF16)]
    SCORE = [psc[0][:, 0:384], psc[1][:, 0:384],
             pv2[:, 0:384], pv2[:, 512:896]]
    AV = [psc[0][:, 384:449], psc[1][:, 384:449],
          pv2[:, 384:449], pv2[:, 896:961]]

    idb = sb("idb", [128, 128], BF16)
    mask2 = sb("mask2", [128, 384], BF16)
    maskT3 = sb("maskT3", [128, 384], BF16)
    mask1 = sb("mask1", [128, 256], BF16)
    mask0 = sb("mask0", [128, 128], BF16)
    ones_bf = sb("ones_bf", [1, 512], BF16)
    rows_f = sb("rows_f", [1, 2560], F32)
    rows_b = sb("rows_b", [1, 2560], BF16)
    b2q_row = sb("b2q_row", [1, 1024], BF16)
    sq_scr = sb("sq_scr", [128, 1024], F32)
    c2t = sb("c2t", [128, 1024], F32)

    m1acc = sb("m1acc", [128, 1], F32)
    m2acc = sb("m2acc", [128, 1], F32)
    trow = sb("trow", [1, 128], F32)
    lmax = sb("lmax", [1, 1], F32)
    g8 = sb("g8", [1, 8], F32)
    gg = sb("gg", [1, 1], F32)
    s1bc = sb("s1bc", [128, 1], F32)
    rs1 = sb("rs1", [128, 1], F32)
    s2bc = sb("s2bc", [128, 1], F32)
    rs2 = sb("rs2", [128, 1], F32)
    sw1 = sb("sw1", [128, FC], F32)
    rsw1 = sb("rsw1", [128, FC], F32)
    c1 = sb("c1", [128, FC], F32)
    b1q = sb("b1q", [128, FC], F32)
    b1f_s = sb("b1f_s", [128, FC], F32)
    sw2 = sb("sw2", [128, DC], F32)
    rsw2 = sb("rsw2", [128, DC], F32)
    c2 = sb("c2", [128, DC], F32)
    b2qf = sb("b2qf", [128, DC], F32)
    b2f_s = sb("b2f_s", [128, DC], F32)
    tq32 = sb("tq32", [128, FC], I32)
    tqf = sb("tqf", [128, FC], F32)
    ln_sm = sb("ln_sm", [128, 2], F32)
    ln_ssq = sb("ln_ssq", [128, 2], F32)
    ln_mean = sb("ln_mean", [128, 2], F32)
    ln_var = sb("ln_var", [128, 2], F32)
    ln_lnv = sb("ln_lnv", [128, 2], F32)
    ln_rstd = sb("ln_rstd", [128, 2], F32)
    ln_t1 = sb("ln_t1", [128, 2], F32)
    ln_nmr = sb("ln_nmr", [128, 2], F32)

    def ln_a(ph, par, rpre_ap, n_free, sv_evt):
        p = slice(par, par + 1)
        ph.wait("vector", sv_evt)
        ev = ph.op("vector", lambda e: e.reduce_sum(
            ln_sm[:, p], rpre_ap, axis=AX), inc="v")
        ph.wait("vector", ev)
        ev = ph.op("vector", lambda e: e.tensor_scalar_mul(
            ln_mean[:, p], ln_sm[:, p], 1.0 / n_free), inc="v")
        ph.wait("scalar", sv_evt)
        ev_sq = ph.op("scalar", lambda e: e.activation(
            sq_scr[:, 0:n_free], rpre_ap, AF.Square, accum_out=ln_ssq[:, p]),
            inc="s")
        ph.wait("vector", ev_sq)
        ph.wait("vector", ev)
        ev = ph.op("vector", lambda e: e.tensor_scalar_mul(
            ln_ssq[:, p], ln_ssq[:, p], 1.0 / n_free), inc="v")
        evm = ph.op("vector", lambda e: e.tensor_mul(
            ln_var[:, p], ln_mean[:, p], ln_mean[:, p]), inc="v")
        ph.wait("vector", ev)
        ph.wait("vector", evm)
        ev = ph.op("vector", lambda e: e.tensor_sub(
            ln_var[:, p], ln_ssq[:, p], ln_var[:, p]), inc="v")
        ph.wait("vector", ev)
        ev_var = ph.op("vector", lambda e: e.tensor_scalar_add(
            ln_var[:, p], ln_var[:, p], EPS_LN), inc="v")
        return ev_var

    def ln_b(ph, par, ev_var):
        p = slice(par, par + 1)
        ph.wait("scalar", ev_var)
        ev = ph.op("scalar", lambda e: e.activation(
            ln_lnv[:, p], ln_var[:, p], AF.Ln), inc="s")
        ph.wait("scalar", ev)
        return ph.op("scalar", lambda e: e.activation(
            ln_rstd[:, p], ln_lnv[:, p], AF.Exp, scale=-0.5), inc="s")

    def ln_c(ph, par, rpre_ap, out_ap, ev_rstd):
        p = slice(par, par + 1)
        ph.wait("vector", ev_rstd)
        ev = ph.op("vector", lambda e: e.tensor_mul(
            ln_t1[:, p], ln_rstd[:, p], ln_rstd[:, p]), inc="v")
        ph.wait("vector", ev)
        ev = ph.op("vector", lambda e: e.tensor_mul(
            ln_t1[:, p], ln_t1[:, p], ln_var[:, p]), inc="v")
        ph.wait("vector", ev)
        ev = ph.op("vector", lambda e: e.tensor_scalar(
            ln_t1[:, p], ln_t1[:, p], -0.5, 1.5, ALU.mult, ALU.add), inc="v")
        ph.wait("vector", ev)
        ev = ph.op("vector", lambda e: e.tensor_mul(
            ln_rstd[:, p], ln_rstd[:, p], ln_t1[:, p]), inc="v")
        ph.wait("vector", ev)
        ev = ph.op("vector", lambda e: e.scalar_tensor_tensor(
            ln_nmr[:, p], ln_mean[:, p], -1.0, ln_rstd[:, p],
            ALU.mult, ALU.mult), inc="v")
        ph.wait("vector", ev)
        return ph.op("vector", lambda e: e.tensor_scalar(
            out_ap, rpre_ap, ln_rstd[:, par:par + 1], ln_nmr[:, par:par + 1],
            ALU.mult, ALU.add), inc="v")

    def gather_scale(ph, macc, md, cci, cco, sd, sbc, rsb):
        ev1 = ph.dma("gpsimd", md[:], macc[:, 0:1])
        ph.wait("gpsimd", ev1)
        ev2 = ph.dma("gpsimd", trow[0:1, :], md[None, :])
        ph.wait("vector", ev2)
        ev_lm = ph.op("vector", lambda e: e.reduce_max(
            lmax[:, :], trow[:, :], axis=AX), inc="v")
        ph.wait("gpsimd", ev_lm)
        ev3 = ph.dma("gpsimd", cci[:], lmax[0:1, 0:1])
        ph.wait("gpsimd", ev3)
        ev_cc = ph.op("gpsimd", lambda e: e.collective_compute(
            "AllGather", ALU.bypass, replica_groups=[list(range(8))],
            ins=[cci[:].opt()], outs=[cco[:].opt()]), inc="cc")
        ph.wait("gpsimd", ev_cc)
        ev4 = ph.dma("gpsimd", g8[0:1, :], cco[None, :])
        ph.wait("vector", ev4)
        evr = ph.op("vector", lambda e: e.reduce_max(gg[:, :], g8[:, :], axis=AX),
                    inc="v")
        ph.wait("vector", evr)
        ev_s = ph.op("vector", lambda e: e.tensor_scalar(
            gg[:, :], gg[:, :], EPS_Q, 1.0 / 255.0, ALU.max, ALU.mult), inc="v")
        ph.wait("gpsimd", ev_s)
        ev5 = ph.dma("gpsimd", sd[:], gg[0:1, 0:1])
        ph.wait("gpsimd", ev5)
        ev6 = ph.dma("gpsimd", sbc[:, :], sd[None, :].partition_broadcast(128))
        ph.wait("vector", ev6)
        return ph.op("vector", lambda e: e.reciprocal(rsb[:, :], sbc[:, :]),
                     inc="v")

    with es:
        a_es = ExitStack()
        sba = lambda n, s, d: a_es.enter_context(nc.sbuf_tensor(n, s, d))
        x_fm = sba("x_fm", [128, DC * T], BF16)
        q_fm = sba("q_fm", [128, DC * T], BF16)
        k2 = sba("k2", [128, 4 * T], BF16)
        v_tm = sba("v_tm", [128, NT * 260], BF16)
        wqb = sba("wqb", [128, DC * 1024], BF16)
        wkb = sba("wkb", [128, DC * 256], BF16)
        wvb = sba("wvb", [128, DC * 256], BF16)
        wob = sba("wob", [128, DC * 1024], BF16)
        xld = [sba("xld0", [128, 1024], F32), sba("xld1", [128, 1024], F32)]
        xbf = sba("xbf", [128, 1024], BF16)
        PT = [sba("PT%d" % i, [128, 384], BF16) for i in range(4)]
        negmb = sba("negmb", [128, 4], BF16)
        nr = sba("nr", [1, 512], BF16)
        rl = sba("rl", [128, 4], F32)
        y_tm = sba("y_tm", [128, 1024], BF16)
        y_fm = sba("y_fm", [128, 1024], BF16)
        xres = [sba("xres0", [128, 1024], F32), sba("xres1", [128, 1024], F32)]
        rpre = [sba("rpre0", [128, 1024], F32), sba("rpre1", [128, 1024], F32)]
        r1t = [sba("r1t0", [128, 1024], F32), sba("r1t1", [128, 1024], F32)]
        tmx = sba("tmx", [128, 1], F32)
        wst = xld

        ph = Ph(k)

        def g_init(e):
            e.memset(idb[:, :], 1.0)
            e.affine_select(out=idb[:, :], in_=idb[:, :], pattern=[[-1, 128]],
                            compare_op=ALU.is_equal, fill=0.0, base=0,
                            channel_multiplier=1)
            e.memset(maskT3[:, :], 0.0)
            e.affine_select(out=maskT3[:, 0:128], in_=maskT3[:, 0:128],
                            pattern=[[-1, 128]], compare_op=ALU.is_ge,
                            fill=NEG, base=0, channel_multiplier=1)
            e.affine_select(out=maskT3[:, 128:256], in_=maskT3[:, 128:256],
                            pattern=[[-1, 128]], compare_op=ALU.is_ge,
                            fill=NEG, base=128, channel_multiplier=1)
            e.affine_select(out=maskT3[:, 128:256], in_=maskT3[:, 128:256],
                            pattern=[[1, 128]], compare_op=ALU.is_ge,
                            fill=NEG, base=128, channel_multiplier=-1)
            e.affine_select(out=maskT3[:, 256:384], in_=maskT3[:, 256:384],
                            pattern=[[1, 128]], compare_op=ALU.is_ge,
                            fill=NEG, base=0, channel_multiplier=-1)
            e.memset(mask2[:, :], 0.0)
            e.affine_select(out=mask2[:, :], in_=mask2[:, :], pattern=[[1, 384]],
                            compare_op=ALU.is_ge, fill=NEG, base=0,
                            channel_multiplier=-1)
            e.affine_select(out=mask2[:, :], in_=mask2[:, :], pattern=[[-1, 384]],
                            compare_op=ALU.is_ge, fill=NEG, base=256,
                            channel_multiplier=1)
            e.memset(mask1[:, :], 0.0)
            e.affine_select(out=mask1[:, :], in_=mask1[:, :], pattern=[[-1, 256]],
                            compare_op=ALU.is_ge, fill=NEG, base=128,
                            channel_multiplier=1)
            e.memset(mask0[:, :], 0.0)
            e.affine_select(out=mask0[:, :], in_=mask0[:, :], pattern=[[-1, 128]],
                            compare_op=ALU.is_ge, fill=NEG, base=0,
                            channel_multiplier=1)
            e.memset(ones_bf[:, :], 1.0)
            e.memset(m1acc[:, :], NEG)
            e.memset(m2acc[:, :], 0.0)
            vap = v_tm[:, :].rearrange("p (t g c) -> p t g c", t=NT, g=4)
            e.memset(vap[:, :, :, 64:65], 1.0)
            return e.engine_nop()
        ev_g = ph.op("gpsimd", g_init, inc="g")

        ev_rows = ph.dma("sync", rows_f[:, :], rows_d[:, :])
        ph.wait("vector", ev_rows)
        ev_rowsb = ph.op("vector", lambda e: e.tensor_copy(
            rows_b[:, :], rows_f[:, :]), inc="v")

        wsrc = [(wq_d, wqb, 1024), (wo_d, wob, 1024), (wk_d, wkb, 256),
                (wv_d, wvb, 256)]
        cast_evts = []
        i = 0
        for (src, dst, w) in wsrc:
            for dc in range(DC):
                buf = wst[i % 2]
                if i >= 2:
                    ph.wait("sync", cast_evts[i - 2])
                ev_l = ph.dma("sync", buf[:, 0:w], src[dc * 128:(dc + 1) * 128, :])
                ph.wait("vector", ev_l)
                ev_c = ph.op("vector", lambda e, d=dst, dc=dc, b=buf, w=w:
                             e.tensor_copy(d[:, dc * w:(dc + 1) * w], b[:, 0:w]),
                             inc="v")
                cast_evts.append(ev_c)
                i += 1
        ph.run(nc)

        ph = Ph(k)
        for eng in ENGS:
            ph.wait(eng, ev_g)
            ph.wait(eng, ev_rowsb)
            ph.wait(eng, cast_evts[-1])

        pt_free = {0: None, 1: None}
        r1_store = {}
        r1_mx = {}
        sl_exp = {i: None for i in range(4)}
        sl_nm = {i: None for i in range(4)}
        sl_ye = {i: None for i in range(4)}
        sl_pte = {i: None for i in range(4)}
        outproj_vread = None
        pend_stat = None
        pend_bc = None
        pending_tail = None
        ev_kd = None

        def flush_ln_c(ph):
            nonlocal pend_bc
            if pend_bc is None:
                return
            (gq, par, ev_var), ev_rstd = pend_bc
            pend_bc = None
            rb = gq % 2
            ph.wait("vector", r1_store.get(gq - 2))
            ph.wait("vector", r1_mx.get(gq - 2))
            ev_r1 = ln_c(ph, par, rpre[par][:, :], r1t[rb][:, :], ev_rstd)
            ph.wait("vector", ev_r1)
            ev_tm = ph.op("vector", lambda e, rb=rb: e.reduce_max(
                tmx[:, :], r1t[rb][:, :], axis=AX), inc="v")
            ph.wait("vector", ev_tm)
            r1_mx[gq] = ph.op("vector", lambda e: e.tensor_max(
                m1acc[:, :], m1acc[:, :], tmx[:, :]), inc="v")
            ph.wait("gpsimd", ev_r1)
            r1_store[gq] = ph.dma("gpsimd", r1_d[gq * 128:(gq + 1) * 128, :],
                                  r1t[rb][:, :])

        def flush_ln_b(ph):
            nonlocal pend_stat, pend_bc
            if pend_stat is None:
                return
            pend_bc = (pend_stat, ln_b(ph, pend_stat[1], pend_stat[2]))
            pend_stat = None

        for b in range(B):
            xfm_ready = None
            xbf_cons = None
            xld_cons = {}
            for tt in range(NT):
                buf = xld[tt % 2]
                if tt >= 2:
                    ph.wait("sync", xld_cons[tt - 2])
                ev_l = ph.dma("sync", buf[:, :],
                              x_d[b, tt * 128:(tt + 1) * 128, :])
                ph.wait("vector", ev_l)
                if xbf_cons is not None:
                    ph.wait("vector", xbf_cons)
                ev_cast = ph.op("vector", lambda e, buf=buf: e.tensor_copy(
                    xbf[:, :], buf[:, :]), inc="v")
                xld_cons[tt] = ev_cast
                for half in range(2):
                    bank = ppt[half]
                    ph.wait("tensor", ev_cast)
                    ph.wait("tensor", pt_free[half])
                    def tfn(e, half=half, bank=bank):
                        for j in range(4):
                            dc = half * 4 + j
                            ins = e.transpose(bank[:, j * 128:(j + 1) * 128],
                                              xbf[:, dc * 128:(dc + 1) * 128],
                                              idb[:, :])
                        return ins
                    ev_t = ph.op("tensor", tfn, inc="t")
                    ph.wait("vector", ev_t)
                    dst = x_fm[:, :].rearrange("p (d t) -> p d t", d=DC)[
                        :, half * 4:(half + 1) * 4, tt * 128:(tt + 1) * 128]
                    ev_e = ph.op("vector", lambda e, bank=bank, dst=dst:
                                 e.tensor_copy(dst, bank[:, :].rearrange(
                                     "p (d t) -> p d t", d=4)), inc="v")
                    pt_free[half] = ev_e
                    xfm_ready = ev_e
                xbf_cons = ev_t
            kv_ready = None
            q_ready = None
            for tw in range(4):
                for mc in range(2):
                    pslc = pq[:, mc * 512:(mc + 1) * 512]
                    ph.wait("tensor", xfm_ready)
                    ph.wait("tensor", outproj_vread)
                    if kv_ready is not None:
                        ph.wait("tensor", kv_ready)
                    def kfn(e, tw=tw, mc=mc, pslc=pslc):
                        e.matmul(pslc,
                                 rows_b[0:1, 2048 + mc * 128:2048 + (mc + 1) * 128],
                                 ones_bf[0:1, 0:512], start=True, stop=False)
                        for dc in range(DC):
                            ins = e.matmul(
                                pslc,
                                wkb[:, dc * 256 + mc * 128:dc * 256 + (mc + 1) * 128],
                                x_fm[:, dc * T + tw * 512:dc * T + (tw + 1) * 512],
                                start=False, stop=(dc == DC - 1))
                        return ins
                    ev_k = ph.op("tensor", kfn, inc="t")
                    ph.wait("vector", ev_k)
                    ph.wait("vector", ev_kd)
                    g0, g1 = 2 * mc, 2 * mc + 1
                    def kev(e, pslc=pslc, g0=g0, g1=g1, tw=tw):
                        e.tensor_copy(
                            k2[0:64, g0 * T + tw * 512:g0 * T + (tw + 1) * 512],
                            pslc[0:64, :])
                        return e.tensor_copy(
                            k2[64:128, g1 * T + tw * 512:g1 * T + (tw + 1) * 512],
                            pslc[64:128, :])
                    kv_ready = ph.op("vector", kev, inc="v")
                for mc in range(DC):
                    pslc = pq[:, (mc % 2) * 512:((mc % 2) + 1) * 512]
                    ph.wait("tensor", kv_ready if mc < 2 else q_ready)
                    def qfn(e, tw=tw, mc=mc, pslc=pslc):
                        e.matmul(pslc, rows_b[0:1, mc * 128:(mc + 1) * 128],
                                 ones_bf[0:1, 0:512], start=True, stop=False)
                        for dc in range(DC):
                            ins = e.matmul(
                                pslc,
                                wqb[:, dc * 1024 + mc * 128:dc * 1024 + (mc + 1) * 128],
                                x_fm[:, dc * T + tw * 512:dc * T + (tw + 1) * 512],
                                start=False, stop=(dc == DC - 1))
                        return ins
                    ev_q = ph.op("tensor", qfn, inc="t")
                    ph.wait("scalar", ev_q)
                    q_ready = ph.op("scalar", lambda e, tw=tw, mc=mc, pslc=pslc:
                                    e.activation(
                                        q_fm[:, mc * T + tw * 512:mc * T + (tw + 1) * 512],
                                        pslc, AF.Identity, scale=0.125), inc="s")
            ph.wait("gpsimd", kv_ready)
            ph.dma("gpsimd", k2[64:128, 0:T], k2[0:64, 0:T])
            ph.dma("gpsimd", k2[64:128, 2 * T:3 * T], k2[0:64, 2 * T:3 * T])
            ph.dma("gpsimd", k2[0:64, T:2 * T], k2[64:128, T:2 * T])
            ev_kd = ph.dma("gpsimd", k2[0:64, 3 * T:4 * T],
                           k2[64:128, 3 * T:4 * T])
            v_ready = None
            for tt in range(NT):
                pslc = pv2[:, 0:256]
                ph.wait("tensor", xfm_ready)
                ph.wait("tensor", sl_exp[2])
                ph.wait("tensor", sl_exp[3])
                ph.wait("tensor", sl_ye[2])
                ph.wait("tensor", sl_ye[3])
                if v_ready is not None:
                    ph.wait("tensor", v_ready)
                def vfn(e, tt=tt, pslc=pslc):
                    e.matmul(pslc, ones_bf[0:1, 0:128],
                             rows_b[0:1, 2304:2560], start=True, stop=False)
                    for dc in range(DC):
                        ins = e.matmul(
                            pslc, x_fm[:, dc * T + tt * 128:dc * T + (tt + 1) * 128],
                            wvb[:, dc * 256:(dc + 1) * 256],
                            start=False, stop=(dc == DC - 1))
                    return ins
                ev_v = ph.op("tensor", vfn, inc="t")
                ph.wait("vector", ev_v)
                def vev(e, tt=tt, pslc=pslc):
                    for g in range(4):
                        ins = e.tensor_copy(
                            v_tm[:, tt * 260 + g * 65:tt * 260 + g * 65 + 64],
                            pslc[:, g * 64:(g + 1) * 64])
                    return ins
                v_ready = ph.op("vector", vev, inc="v")

            xres_cons = {}
            for qt in range(NT):
                kts = min(qt + 1, 3)
                klen = kts * 128
                k0 = (qt - kts + 1) * 128
                mask = {1: mask0, 2: mask1, 3: mask2}[kts]
                gq = b * NT + qt
                par = gq % 2
                if qt >= 2:
                    ph.wait("sync", xres_cons.get(qt - 2))
                ev_xr = ph.dma("sync", xres[qt % 2][:, :],
                               x_d[b, qt * 128:(qt + 1) * 128, :])

                for step in range(NH + 1):
                    if step < NH:
                        h = step
                        sl = h % 4
                        g = h // 4
                        qoff = (h % 2) * 64
                        ph.wait("tensor", q_ready)
                        ph.wait("tensor", kv_ready)
                        ph.wait("tensor", ev_kd)
                        ph.wait("tensor", v_ready)
                        ph.wait("tensor", sl_exp[sl])
                        ph.wait("tensor", sl_nm[sl])
                        qsl = q_fm[qoff:qoff + 64,
                                   (h // 2) * T + qt * 128:(h // 2) * T + (qt + 1) * 128]
                        def dfn(e, sl=sl, g=g, qoff=qoff, qt=qt, qsl=qsl):
                            return e.matmul(
                                SCORE[sl][:, 0:128], qsl,
                                k2[qoff:qoff + 64, g * T + qt * 128:g * T + (qt + 1) * 128],
                                start=True, stop=True)
                        ev_d = ph.op("tensor", dfn, inc="t")
                        ph.wait("vector", ev_d)
                        ev_nm = ph.op("vector", lambda e, sl=sl:
                                      e.reduce_max(negmb[:, sl:sl + 1],
                                                   SCORE[sl][:, 0:128],
                                                   axis=AX, negate=True),
                                      inc="v")
                        sl_nm[sl] = ev_nm
                        ph.wait("tensor", ev_nm)
                        ph.wait("tensor", pt_free[sl % 2])
                        ev_ntr = ph.op("tensor", lambda e, sl=sl: e.transpose(
                            ppt[sl % 2][0:1, 0:128], negmb[:, sl:sl + 1],
                            idb[:, :]), inc="t")
                        ph.wait("scalar", ev_ntr)
                        ev_nre = ph.op("scalar", lambda e, sl=sl: e.activation(
                            nr[0:1, sl * 128:(sl + 1) * 128],
                            ppt[sl % 2][0:1, 0:128], AF.Identity), inc="s")
                        pt_free[sl % 2] = ev_nre
                        ph.wait("tensor", ev_nre)
                        def stfn(e, sl=sl, g=g, qoff=qoff, qt=qt, k0=k0,
                                 kts=kts, qsl=qsl):
                            for jb in range(kts):
                                e.matmul(
                                    SCORE[sl][:, jb * 128:(jb + 1) * 128],
                                    k2[qoff:qoff + 64,
                                       g * T + k0 + jb * 128:g * T + k0 + (jb + 1) * 128],
                                    qsl, start=True, stop=False)
                                e.matmul(SCORE[sl][:, jb * 128:(jb + 1) * 128],
                                         idb[:, :],
                                         maskT3[:, (3 - kts + jb) * 128:
                                                (4 - kts + jb) * 128],
                                         start=False, stop=False)
                                ins = e.matmul(
                                    SCORE[sl][:, jb * 128:(jb + 1) * 128],
                                    ones_bf[0:1, 0:128],
                                    nr[0:1, sl * 128:(sl + 1) * 128],
                                    start=False, stop=True)
                            return ins
                        ev_st = ph.op("tensor", stfn, inc="t")
                        ph.wait("scalar", ev_st)
                        ev_exp = ph.op("scalar", lambda e, sl=sl, klen=klen:
                                       e.activation(PT[sl][:, 0:klen],
                                                    SCORE[sl][:, 0:klen],
                                                    AF.Exp), inc="s")
                        sl_exp[sl] = ev_exp
                        if step == 3:
                            flush_ln_b(ph)
                        if step == 8 and pending_tail is not None:
                            pending_tail(ph)
                            pending_tail = None
                    if step >= 1:
                        h2 = step - 1
                        sl2 = h2 % 4
                        g2 = h2 // 4
                        ph.wait("tensor", sl_exp[sl2])
                        ph.wait("tensor", sl_ye[sl2])
                        def avfn(e, sl2=sl2, kts=kts, k0=k0, g2=g2):
                            for kt in range(kts):
                                ktok = k0 // 128 + kt
                                ins = e.matmul(
                                    AV[sl2],
                                    PT[sl2][:, kt * 128:(kt + 1) * 128],
                                    v_tm[:, ktok * 260 + g2 * 65:ktok * 260 + (g2 + 1) * 65],
                                    start=(kt == 0), stop=(kt == kts - 1))
                            return ins
                        ev_av = ph.op("tensor", avfn, inc="t")
                        ph.wait("vector", ev_av)
                        ev_rl = ph.op("vector", lambda e, sl2=sl2: e.reciprocal(
                            rl[:, sl2:sl2 + 1], AV[sl2][:, 64:65]), inc="v")
                        ph.wait("vector", ev_rl)
                        ev_ye = ph.op("vector", lambda e, sl2=sl2, h2=h2:
                                      e.tensor_scalar_mul(
                                          y_tm[:, h2 * 64:(h2 + 1) * 64],
                                          AV[sl2][:, 0:64], rl[:, sl2:sl2 + 1]),
                                      inc="v")
                        sl_ye[sl2] = ev_ye
                        yev_last = ev_ye
                flush_ln_c(ph)

                for half in range(2):
                    bank = ppt[half]
                    ph.wait("tensor", yev_last)
                    ph.wait("tensor", pt_free[half])
                    def ytr(e, half=half, bank=bank):
                        for j in range(4):
                            c = half * 4 + j
                            ins = e.transpose(bank[:, j * 128:(j + 1) * 128],
                                              y_tm[:, c * 128:(c + 1) * 128],
                                              idb[:, :])
                        return ins
                    ev_yt = ph.op("tensor", ytr, inc="t")
                    ph.wait("vector", ev_yt)
                    ev_yfe = ph.op("vector", lambda e, half=half, bank=bank:
                                   e.tensor_copy(
                                       y_fm[:, half * 512:(half + 1) * 512],
                                       bank[:, :]), inc="v")
                    pt_free[half] = ev_yfe
                    yfm_ready = ev_yfe
                ph.wait("tensor", yfm_ready)
                ph.wait("tensor", outproj_vread)
                def ofn(e):
                    for n2 in range(2):
                        pslc = pq[:, n2 * 512:(n2 + 1) * 512]
                        e.matmul(pslc, ones_bf[0:1, 0:128],
                                 rows_b[0:1, 1024 + n2 * 512:1024 + (n2 + 1) * 512],
                                 start=True, stop=False)
                        for dc in range(DC):
                            ins = e.matmul(
                                pslc, y_fm[:, dc * 128:(dc + 1) * 128],
                                wob[:, dc * 1024 + n2 * 512:dc * 1024 + (n2 + 1) * 512],
                                start=False, stop=(dc == DC - 1))
                    return ins
                ev_o = ph.op("tensor", ofn, inc="t")
                ph.wait("vector", ev_o)
                ph.wait("vector", ev_xr)
                ph.wait("vector", r1_store.get(gq - 2))
                ev_rp = ph.op("vector", lambda e, par=par, qt=qt: e.tensor_add(
                    rpre[par][:, :], pq[:, :], xres[qt % 2][:, :]), inc="v")
                xres_cons[qt] = ev_rp
                outproj_vread = ev_rp
                def tail(ph, gq=gq, par=par, ev_rp=ev_rp):
                    nonlocal pend_stat
                    pend_stat = (gq, par,
                                 ln_a(ph, par, rpre[par][:, :], 1024, ev_rp))
                pending_tail = tail
            flush_ln_b(ph)
            flush_ln_c(ph)
            if pending_tail is not None:
                pending_tail(ph)
                pending_tail = None
            flush_ln_b(ph)
            flush_ln_c(ph)

        ph.wait("gpsimd", k.evt("v"))
        ph.wait("gpsimd", r1_store[B * NT - 1])
        ev_rs1 = gather_scale(ph, m1acc, m1_d, cc1i_d, cc1o_d, s1_d, s1bc, rs1)
        ph.run(nc)
        a_es.close()

        b_es = ExitStack()
        sbb = lambda n, s, d: b_es.enter_context(nc.sbuf_tensor(n, s, d))
        w1T = sbb("w1T", [128, DC * F], BF16)
        w1st = [sbb("w1st0", [128, 1024], F32), sbb("w1st1", [128, 1024], F32)]
        w1i8 = sbb("w1i8", [128, 1024], I8)
        w1cb = sbb("w1cb", [128, 1024], BF16)
        r1ld = [sbb("r1ld0", [128, 4096], F32), sbb("r1ld1", [128, 4096], F32)]
        xq_u8 = sbb("xq_u8", [128, 4096], U8)
        xq_bf = sbb("xq_bf", [128, 4096], BF16)
        xq_fm = sbb("xq_fm", [128, DC * 512], BF16)
        h_t = [sbb("h_t0", [128, 512], F32), sbb("h_t1", [128, 512], F32)]
        hmx = sbb("hmx", [128, 1], F32)
        amax_t = sbb("amax_t", [128, 1], F32)

        ph = Ph(k)
        pt_free = {0: None, 1: None}
        w1_cons = {}
        w1_amax = {}
        w1_trd = {}
        w1T_ready = None
        for fc in range(FC):
            buf = w1st[fc % 2]
            if fc >= 2:
                ph.wait("sync", w1_cons[fc - 2])
                ph.wait("sync", w1_amax[fc - 2])
            ev_l = ph.dma("sync", buf[:, :], w1_d[fc * 128:(fc + 1) * 128, :])
            ph.wait("vector", ev_l)
            ev_a = ph.op("vector", lambda e, buf=buf: e.reduce_max(
                amax_t[:, :], buf[:, :], axis=AX,
                apply_absolute_value=True), inc="v")
            ph.wait("vector", ev_a)
            ev_a = ph.op("vector", lambda e, fc=fc: e.tensor_scalar(
                sw1[:, fc:fc + 1], amax_t[:, :], EPS_Q,
                1.0 / 127.0, ALU.max, ALU.mult), inc="v")
            ph.wait("vector", ev_a)
            ev_a = ph.op("vector", lambda e, fc=fc: e.reciprocal(
                rsw1[:, fc:fc + 1], sw1[:, fc:fc + 1]), inc="v")
            w1_amax[fc] = ev_a
            ph.wait("scalar", ev_a)
            ph.wait("scalar", w1_trd.get(fc - 1))
            def w1qz(e, fc=fc, buf=buf):
                e.activation(w1i8[:, :], buf[:, :], AF.Identity,
                             scale=rsw1[:, fc:fc + 1])
                return e.activation(w1cb[:, :], w1i8[:, :], AF.Identity)
            ev_qz = ph.op("scalar", w1qz, inc="s")
            w1_cons[fc] = ev_qz
            for half in range(2):
                bank = ppt[half]
                ph.wait("tensor", ev_qz)
                ph.wait("tensor", pt_free[half])
                def w1tr(e, half=half, bank=bank):
                    for j in range(4):
                        dc = half * 4 + j
                        ins = e.transpose(bank[:, j * 128:(j + 1) * 128],
                                          w1cb[:, dc * 128:(dc + 1) * 128],
                                          idb[:, :])
                    return ins
                ev_t = ph.op("tensor", w1tr, inc="t")
                w1_trd[fc] = ev_t
                ph.wait("vector", ev_t)
                dst = w1T[:, :].rearrange("p (d f) -> p d f", d=DC)[
                    :, half * 4:(half + 1) * 4, fc * 128:(fc + 1) * 128]
                ev_e = ph.op("vector", lambda e, bank=bank, dst=dst:
                             e.tensor_copy(dst, bank[:, :].rearrange(
                                 "p (d t) -> p d t", d=4)), inc="v")
                pt_free[half] = ev_e
                w1T_ready = ev_e

        ev_b1 = ph.dma("sync", b1f_s[:, :], b1f_d[:, :])
        ph.wait("vector", ev_rs1)
        ph.wait("vector", ev_b1)
        ph.wait("vector", w1_amax[FC - 1])
        ev_c1 = ph.op("vector", lambda e: e.tensor_scalar_mul(
            c1[:, :], sw1[:, :], s1bc[:, 0:1]), inc="v")
        ph.wait("vector", ev_c1)
        ev_c1 = ph.op("vector", lambda e: e.reciprocal(
            tqf[:, :], c1[:, :]), inc="v")
        ph.wait("vector", ev_c1)
        ev_c1 = ph.op("vector", lambda e: e.tensor_mul(
            tqf[:, :], b1f_s[:, :], tqf[:, :]), inc="v")
        ph.wait("scalar", ev_c1)
        ev_b1r = ph.op("scalar", lambda e: e.activation(
            tq32[:, :], tqf[:, :], AF.Identity), inc="s")
        ph.wait("scalar", ev_b1r)
        ev_b1r = ph.op("scalar", lambda e: e.activation(
            tqf[:, :], tq32[:, :], AF.Identity), inc="s")
        ph.wait("vector", ev_b1r)
        ev_b1q = ph.op("vector", lambda e: e.tensor_mul(
            b1q[:, :], tqf[:, :], c1[:, :]), inc="v")

        ht_hm = {}
        ht_ev = {}
        ht_st = {}
        r1ld_cons = {}
        xq_trdone = None
        for it in range(8):
            buf = r1ld[it % 2]
            if it >= 2:
                ph.wait("sync", r1ld_cons[it - 2])
            src = r1_d[it * 512:(it + 1) * 512, :].rearrange(
                "(q p) c -> p q c", p=128)
            ev_l = ph.dma("sync", buf[:, :].rearrange("p (q c) -> p q c", q=4),
                          src)
            ph.wait("scalar", ev_l)
            ph.wait("scalar", ev_rs1)
            ph.wait("scalar", xq_trdone)
            def bq8(e, buf=buf):
                e.activation(xq_u8[:, :], buf[:, :], AF.Identity,
                             scale=rs1[:, 0:1])
                return e.activation(xq_bf[:, :], xq_u8[:, :], AF.Identity)
            ev_q8 = ph.op("scalar", bq8, inc="s")
            r1ld_cons[it] = ev_q8
            xq_ready = None
            for dc in range(DC):
                bank = ppt[dc % 2]
                ph.wait("tensor", ev_q8)
                ph.wait("tensor", pt_free[dc % 2])
                def xtr(e, dc=dc, bank=bank):
                    for j in range(4):
                        ins = e.transpose(
                            bank[:, j * 128:(j + 1) * 128],
                            xq_bf[:, j * 1024 + dc * 128:j * 1024 + (dc + 1) * 128],
                            idb[:, :])
                    return ins
                ev_t = ph.op("tensor", xtr, inc="t")
                ph.wait("vector", ev_t)
                ev_e = ph.op("vector", lambda e, dc=dc, bank=bank: e.tensor_copy(
                    xq_fm[:, dc * 512:(dc + 1) * 512], bank[:, :]), inc="v")
                pt_free[dc % 2] = ev_e
                xq_ready = ev_e
            xq_trdone = k.evt("t")
            for fc in range(FC):
                gidx = it * FC + fc
                pslc = pq[:, (fc % 2) * 512:((fc % 2) + 1) * 512]
                ph.wait("tensor", xq_ready)
                ph.wait("tensor", w1T_ready)
                ph.wait("tensor", ht_ev.get(gidx - 2))
                def mm1(e, fc=fc, pslc=pslc):
                    for dc in range(DC):
                        ins = e.matmul(
                            pslc, w1T[:, dc * F + fc * 128:dc * F + (fc + 1) * 128],
                            xq_fm[:, dc * 512:(dc + 1) * 512],
                            start=(dc == 0), stop=(dc == DC - 1))
                    return ins
                ev_m = ph.op("tensor", mm1, inc="t")
                hb = gidx % 2
                ph.wait("scalar", ev_m)
                ph.wait("scalar", ev_b1q)
                ph.wait("scalar", ht_hm.get(gidx - 2))
                ph.wait("scalar", ht_st.get(gidx - 2))
                ev_h = ph.op("scalar", lambda e, fc=fc, hb=hb, pslc=pslc:
                             e.activation(h_t[hb][:, :], pslc, AF.Relu,
                                          scale=c1[:, fc:fc + 1],
                                          bias=b1q[:, fc:fc + 1]), inc="s")
                ht_ev[gidx] = ev_h
                ph.wait("vector", ev_h)
                ev_hx = ph.op("vector", lambda e, hb=hb: e.reduce_max(
                    hmx[:, :], h_t[hb][:, :], axis=AX), inc="v")
                ph.wait("vector", ev_hx)
                ht_hm[gidx] = ph.op("vector", lambda e: e.tensor_max(
                    m2acc[:, :], m2acc[:, :], hmx[:, :]), inc="v")
                ph.wait("gpsimd", ev_h)
                ht_st[gidx] = ph.dma("gpsimd",
                                     h_d[fc, :, it * 512:(it + 1) * 512],
                                     h_t[hb][:, :])

        ph.wait("gpsimd", k.evt("v"))
        ph.wait("gpsimd", ht_st[8 * FC - 1])
        ev_rs2 = gather_scale(ph, m2acc, m2_d, cc2i_d, cc2o_d, s2_d, s2bc, rs2)
        ph.run(nc)
        b_es.close()

        c_es = ExitStack()
        sbc = lambda n, s, d: c_es.enter_context(nc.sbuf_tensor(n, s, d))
        w2T = sbc("w2T", [128, FC * 1024], BF16)
        t_es = ExitStack()
        sbt = lambda n, s, d: t_es.enter_context(nc.sbuf_tensor(n, s, d))
        w2st = [sbt("w2st0", [128, 4096], F32), sbt("w2st1", [128, 4096], F32)]
        w2i8 = sbt("w2i8", [128, 4096], I8)
        w2cb = sbt("w2cb", [128, 4096], BF16)
        amax2 = sbt("amax2", [128, 1], F32)

        ph = Ph(k)
        pt_free = {0: None, 1: None}
        w2_cons = {}
        w2_amax = {}
        w2_trd = {}
        w2T_ready = None
        for dc in range(DC):
            buf = w2st[dc % 2]
            if dc >= 2:
                ph.wait("sync", w2_cons[dc - 2])
                ph.wait("sync", w2_amax[dc - 2])
            ev_l = ph.dma("sync", buf[:, :], w2_d[dc * 128:(dc + 1) * 128, :])
            ph.wait("vector", ev_l)
            ev_a = ph.op("vector", lambda e, buf=buf: e.reduce_max(
                amax2[:, :], buf[:, :], axis=AX,
                apply_absolute_value=True), inc="v")
            ph.wait("vector", ev_a)
            ev_a = ph.op("vector", lambda e, dc=dc: e.tensor_scalar(
                sw2[:, dc:dc + 1], amax2[:, :], EPS_Q,
                1.0 / 127.0, ALU.max, ALU.mult), inc="v")
            ph.wait("vector", ev_a)
            ev_a = ph.op("vector", lambda e, dc=dc: e.reciprocal(
                rsw2[:, dc:dc + 1], sw2[:, dc:dc + 1]), inc="v")
            w2_amax[dc] = ev_a
            ph.wait("scalar", ev_a)
            ph.wait("scalar", w2_trd.get(dc - 1))
            def w2qz(e, dc=dc, buf=buf):
                e.activation(w2i8[:, :], buf[:, :], AF.Identity,
                             scale=rsw2[:, dc:dc + 1])
                return e.activation(w2cb[:, :], w2i8[:, :], AF.Identity)
            ev_qz = ph.op("scalar", w2qz, inc="s")
            w2_cons[dc] = ev_qz
            for grp in range(DC):
                bank = ppt[grp % 2]
                ph.wait("tensor", ev_qz)
                ph.wait("tensor", pt_free[grp % 2])
                def w2tr(e, grp=grp, bank=bank):
                    for j in range(4):
                        fc = grp * 4 + j
                        ins = e.transpose(bank[:, j * 128:(j + 1) * 128],
                                          w2cb[:, fc * 128:(fc + 1) * 128],
                                          idb[:, :])
                    return ins
                ev_t = ph.op("tensor", w2tr, inc="t")
                w2_trd[dc] = ev_t
                ph.wait("vector", ev_t)
                dst = w2T[:, :].rearrange("p (f d) -> p f d", f=FC)[
                    :, grp * 4:(grp + 1) * 4, dc * 128:(dc + 1) * 128]
                ev_e = ph.op("vector", lambda e, bank=bank, dst=dst:
                             e.tensor_copy(dst, bank[:, :].rearrange(
                                 "p (f t) -> p f t", f=4)), inc="v")
                pt_free[grp % 2] = ev_e
                w2T_ready = ev_e

        ev_b2 = ph.dma("sync", b2f_s[:, :], b2f_d[:, :])
        ph.wait("vector", ev_rs2)
        ph.wait("vector", ev_b2)
        ph.wait("vector", w2_amax[DC - 1])
        ev_c2 = ph.op("vector", lambda e: e.tensor_scalar_mul(
            c2[:, :], sw2[:, :], s2bc[:, 0:1]), inc="v")
        ph.wait("vector", ev_c2)
        ev_c2 = ph.op("vector", lambda e: e.reciprocal(
            tqf[:, 0:DC], c2[:, :]), inc="v")
        ph.wait("vector", ev_c2)
        ev_c2 = ph.op("vector", lambda e: e.tensor_mul(
            tqf[:, 0:DC], b2f_s[:, :], tqf[:, 0:DC]), inc="v")
        ph.wait("scalar", ev_c2)
        ev_b2r = ph.op("scalar", lambda e: e.activation(
            tq32[:, 0:DC], tqf[:, 0:DC], AF.Identity), inc="s")
        ph.wait("scalar", ev_b2r)
        ev_b2r = ph.op("scalar", lambda e: e.activation(
            tqf[:, 0:DC], tq32[:, 0:DC], AF.Identity), inc="s")
        ph.wait("vector", ev_b2r)
        ev_b2q = ph.op("vector", lambda e: e.tensor_mul(
            b2qf[:, :], tqf[:, 0:DC], c2[:, :]), inc="v")
        ph.wait("gpsimd", ev_b2q)
        ev1 = ph.dma("gpsimd", c2r_d[:, :], c2[:, :])
        ev2 = ph.dma("gpsimd", b2qr_d[:, :], b2qf[:, :])
        ph.wait("gpsimd", ev2)
        src_c = bass.AP(c2r_d.tensor, 0, [[1, DC], [DC, 128]])
        ev3 = ph.op("gpsimd", lambda e: e.dma_start(
            out=c2row_d[:].rearrange("(d p) -> d p", d=DC), in_=src_c,
            allow_slow_non_contiguous=True), inc="st", incby=16)
        ph.wait("gpsimd", ev3)
        ev4 = ph.dma("gpsimd", c2t[:, :],
                     c2row_d[None, :].partition_broadcast(128))
        src_b = bass.AP(b2qr_d.tensor, 0, [[1, DC], [DC, 128]])
        ev5a = ph.op("gpsimd", lambda e: e.dma_start(
            out=b2row_d[:].rearrange("(d p) -> d p", d=DC),
            in_=src_b, allow_slow_non_contiguous=True), inc="st", incby=16)
        ph.wait("gpsimd", ev5a)
        ev5 = ph.dma("gpsimd", rows_f[0:1, 0:1024], b2row_d[None, :])
        ph.wait("vector", ev5)
        ev_b2row = ph.op("vector", lambda e: e.tensor_copy(
            b2q_row[0:1, :], rows_f[0:1, 0:1024]), inc="v")
        ph.wait("gpsimd", ev4)
        ph.run(nc)
        t_es.close()

        d_es = ExitStack()
        sbd = lambda n, s, d: d_es.enter_context(nc.sbuf_tensor(n, s, d))
        hld = [sbd("hld0", [128, 512], F32), sbd("hld1", [128, 512], F32)]
        h_u8 = sbd("h_u8", [128, 512], U8)
        hq = [sbd("hq0", [128, FC * 512], BF16), sbd("hq1", [128, FC * 512], BF16)]
        r1res = [sbd("r1res0", [128, 1024], F32), sbd("r1res1", [128, 1024], F32)]
        t2 = sbd("t2", [128, 1024], F32)
        rpre2 = [sbd("rpre20", [128, 1024], F32), sbd("rpre21", [128, 1024], F32)]
        yt = [sbd("yt0", [128, 1024], F32), sbd("yt1", [128, 1024], F32)]

        ph = Ph(k)
        for eng in ENGS:
            ph.wait(eng, ev_b2row)
            ph.wait(eng, w2T_ready)
            ph.wait(eng, ev_rs2)
        hq_cons = {}
        r1r_cons = {}
        yt_store = {}
        mm_t = {}
        mm_v = {}
        pend2 = None

        def flush2(ph):
            nonlocal pend2
            if pend2 is None:
                return
            ptt, ppar, pev_var = pend2
            ev_rstd = ln_b(ph, ppar, pev_var)
            pbb, ptl = ptt // NT, ptt % NT
            ph.wait("vector", yt_store.get(ptt - 2))
            ev_y = ln_c(ph, ppar, rpre2[ppar][:, :], yt[ptt % 2][:, :],
                        ev_rstd)
            ph.wait("gpsimd", ev_y)
            yt_store[ptt] = ph.dma(
                "gpsimd", y_d[pbb, ptl * 128:(ptl + 1) * 128, :],
                yt[ptt % 2][:, :])
            pend2 = None

        for it in range(8):
            hqb = hq[it % 2]
            flush2(ph)
            for fc in range(FC):
                gidx = it * FC + fc
                eng = "sync" if fc % 2 == 0 else "gpsimd"
                buf = hld[fc % 2]
                ph.wait(eng, hq_cons.get(gidx - 2))
                if it >= 2 and fc in (0, 1):
                    ph.wait(eng, mm_t.get((it - 2, 3)))
                ev_l = ph.dma(eng, buf[:, :],
                              h_d[fc, :, it * 512:(it + 1) * 512])
                ph.wait("scalar", ev_l)
                if it >= 2 and fc == 0:
                    ph.wait("scalar", mm_t.get((it - 2, 3)))
                def cq8(e, fc=fc, buf=buf, hqb=hqb):
                    e.activation(h_u8[:, :], buf[:, :], AF.Identity,
                                 scale=rs2[:, 0:1])
                    return e.activation(hqb[:, fc * 512:(fc + 1) * 512],
                                        h_u8[:, :], AF.Identity)
                hq_cons[gidx] = ph.op("scalar", cq8, inc="s")
            hq_ready = k.evt("s")
            for qp in range(4):
                tt = it * 4 + qp
                par = tt % 2
                pp = pq if qp % 2 == 0 else pv2
                if tt >= 2:
                    ph.wait("sync", r1r_cons[tt - 2])
                ev_r = ph.dma("sync", r1res[par][:, :],
                              r1_d[tt * 128:(tt + 1) * 128, :])
                ph.wait("tensor", hq_ready)
                ph.wait("tensor", mm_v.get((it, qp - 2)) if qp >= 2
                        else mm_v.get((it - 1, qp + 2)))
                def mm2(e, qp=qp, pp=pp, hqb=hqb):
                    for n2 in range(2):
                        pslc = pp[:, n2 * 512:(n2 + 1) * 512]
                        e.matmul(pslc, ones_bf[0:1, 0:128],
                                 b2q_row[0:1, n2 * 512:(n2 + 1) * 512],
                                 start=True, stop=False)
                        for fc in range(FC):
                            ins = e.matmul(
                                pslc,
                                hqb[:, fc * 512 + qp * 128:fc * 512 + (qp + 1) * 128],
                                w2T[:, fc * 1024 + n2 * 512:fc * 1024 + (n2 + 1) * 512],
                                start=False, stop=(fc == FC - 1))
                    return ins
                ev_m = ph.op("tensor", mm2, inc="t")
                mm_t[(it, qp)] = ev_m
                flush2(ph)
                ph.wait("vector", ev_m)
                ph.wait("vector", ev_r)
                ph.wait("vector", yt_store.get(tt - 2))
                def ev2f(e, pp=pp, par=par):
                    e.scalar_tensor_tensor(t2[:, :], pp[:, :], 1.0, c2t[:, :],
                                           ALU.mult, ALU.mult)
                    return e.tensor_add(rpre2[par][:, :], t2[:, :],
                                        r1res[par][:, :])
                ev_rp = ph.op("vector", ev2f, inc="v")
                mm_v[(it, qp)] = ev_rp
                r1r_cons[tt] = ev_rp
                ev_var = ln_a(ph, par, rpre2[par][:, :], 1024, ev_rp)
                pend2 = (tt, par, ev_var)
        flush2(ph)
        ph.wait("gpsimd", k.evt("st"))
        ph.run(nc)
        d_es.close()
        c_es.close()

    return nc


_NC_CACHE = None


def kernel(L0_attn_x, L0_attn_Wq, L0_attn_bq, L0_attn_Wk, L0_attn_bk,
           L0_attn_Wv, L0_attn_bv, L0_attn_Wo, L0_attn_bo,
           L0_ffn_w1, L0_ffn_b1, L0_ffn_w2, L0_ffn_b2):
    global _NC_CACHE
    x = np.ascontiguousarray(np.asarray(L0_attn_x, dtype=np.float32))
    wq = np.ascontiguousarray(np.asarray(L0_attn_Wq, dtype=np.float32))
    wk = np.ascontiguousarray(np.asarray(L0_attn_Wk, dtype=np.float32))
    wv = np.ascontiguousarray(np.asarray(L0_attn_Wv, dtype=np.float32))
    wo = np.ascontiguousarray(np.asarray(L0_attn_Wo, dtype=np.float32))
    rows = np.concatenate([
        np.asarray(L0_attn_bq, dtype=np.float32).reshape(-1),
        np.asarray(L0_attn_bo, dtype=np.float32).reshape(-1),
        np.asarray(L0_attn_bk, dtype=np.float32).reshape(-1),
        np.asarray(L0_attn_bv, dtype=np.float32).reshape(-1)]).reshape(1, 2560)
    rows = np.ascontiguousarray(rows)
    w1 = np.ascontiguousarray(np.asarray(L0_ffn_w1, dtype=np.float32))
    w2 = np.ascontiguousarray(np.asarray(L0_ffn_w2, dtype=np.float32))
    b1f = np.ascontiguousarray(
        np.asarray(L0_ffn_b1, dtype=np.float32).reshape(FC, 128).T)
    b2f = np.ascontiguousarray(
        np.asarray(L0_ffn_b2, dtype=np.float32).reshape(DC, 128).T)

    if _NC_CACHE is None:
        _NC_CACHE = build_nc()
    nc = _NC_CACHE

    in_maps = []
    for c in range(8):
        in_maps.append({
            "x": np.ascontiguousarray(x[2 * c:2 * c + 2]),
            "wq": wq, "wk": wk, "wv": wv, "wo": wo, "rows": rows,
            "w1": w1, "w2": w2, "b1f": b1f, "b2f": b2f,
        })
    res = run_bass_kernel_spmd(nc, in_maps, core_ids=list(range(8)))
    out = np.concatenate([r["y"] for r in res.results], axis=0)
    return np.ascontiguousarray(out.astype(np.float32))


# version 18
# speedup vs baseline: 1.0840x; 1.0840x over previous
import numpy as np
from contextlib import ExitStack

import concourse.bass as bass
import concourse.mybir as mybir
from concourse.bass_utils import run_bass_kernel_spmd

F32 = mybir.dt.float32
BF16 = mybir.dt.bfloat16
U8 = mybir.dt.uint8
I8 = mybir.dt.int8
I32 = mybir.dt.int32
AF = mybir.ActivationFunctionType
ALU = mybir.AluOpType
AX = mybir.AxisListType.X

B = 2
T = 2048
C = 1024
NH = 16
DC = C // 128
NT = T // 128
F = 4096
FC = F // 128
EPS_Q = 1e-8
EPS_LN = 1e-5
NEG = -1e30

ENGS = ("sync", "tensor", "vector", "scalar", "gpsimd")


class KCtx:
    def __init__(self, nc, es):
        self.nc = nc
        self.sems = {}
        self.cnt = {}
        for name in ("ld", "st", "t", "v", "s", "g", "cc"):
            self.sems[name] = es.enter_context(nc.semaphore(name))
            self.cnt[name] = 0

    def inc(self, name, by=1):
        self.cnt[name] += by
        return (name, self.cnt[name])

    def evt(self, name):
        return (name, self.cnt[name])


class Ph:

    def __init__(self, k):
        self.k = k
        self.q = {e: [] for e in ENGS}
        self.marks = {e: {} for e in ENGS}

    def wait(self, eng, evt):
        if evt is None:
            return
        name, val = evt
        if val <= 0 or self.marks[eng].get(name, 0) >= val:
            return
        self.marks[eng][name] = val
        sem = self.k.sems[name]
        self.q[eng].append(lambda e, s=sem, v=val: e.wait_ge(s, v))

    def op(self, eng, fn, inc=None, incby=1):
        if inc is not None:
            sem = self.k.sems[inc]
            evt = self.k.inc(inc, incby)

            def g(e, fn=fn, sem=sem, incby=incby):
                fn(e).then_inc(sem, incby)

            self.q[eng].append(g)
            return evt
        self.q[eng].append(lambda e, fn=fn: fn(e))
        return None

    def dma(self, eng, out, in_):
        name = "ld" if eng == "sync" else "st"
        return self.op(eng, lambda e, o=out, i=in_: e.dma_start(out=o, in_=i),
                       inc=name, incby=16)

    def run(self, nc):
        with nc.Block() as block:
            for eng in ENGS:
                if self.q[eng]:
                    def body(e, lst=tuple(self.q[eng])):
                        for f in lst:
                            f(e)
                    getattr(block, eng)(body)


def build_nc():
    nc = bass.Bass("TRN2", target_bir_lowering=False, debug=False,
                   detect_race_conditions=False)

    x_d = nc.dram_tensor("x", [B, T, C], F32, kind="ExternalInput").ap()
    wq_d = nc.dram_tensor("wq", [C, C], F32, kind="ExternalInput").ap()
    wk_d = nc.dram_tensor("wk", [C, 256], F32, kind="ExternalInput").ap()
    wv_d = nc.dram_tensor("wv", [C, 256], F32, kind="ExternalInput").ap()
    wo_d = nc.dram_tensor("wo", [C, C], F32, kind="ExternalInput").ap()
    rows_d = nc.dram_tensor("rows", [1, 2560], F32, kind="ExternalInput").ap()
    w1_d = nc.dram_tensor("w1", [F, C], F32, kind="ExternalInput").ap()
    w2_d = nc.dram_tensor("w2", [C, F], F32, kind="ExternalInput").ap()
    b1f_d = nc.dram_tensor("b1f", [128, FC], F32, kind="ExternalInput").ap()
    b2f_d = nc.dram_tensor("b2f", [128, DC], F32, kind="ExternalInput").ap()
    y_d = nc.dram_tensor("y", [B, T, C], F32, kind="ExternalOutput").ap()

    r1_d = nc.dram_tensor("r1sc", [B * T, C], F32).ap()
    h_d = nc.dram_tensor("hsc", [FC, 128, B * T], F32).ap()
    m1_d = nc.dram_tensor("m1sc", [128], F32).ap()
    m2_d = nc.dram_tensor("m2sc", [128], F32).ap()
    s1_d = nc.dram_tensor("s1sc", [1], F32).ap()
    s2_d = nc.dram_tensor("s2sc", [1], F32).ap()
    cc1i_d = nc.dram_tensor("cc1i", [1], F32).ap()
    cc1o_d = nc.dram_tensor("cc1o", [8], F32).ap()
    cc2i_d = nc.dram_tensor("cc2i", [1], F32).ap()
    cc2o_d = nc.dram_tensor("cc2o", [8], F32).ap()
    c2r_d = nc.dram_tensor("c2r", [128, DC], F32).ap()
    b2qr_d = nc.dram_tensor("b2qr", [128, DC], F32).ap()
    c2row_d = nc.dram_tensor("c2row", [1024], F32).ap()
    b2row_d = nc.dram_tensor("b2row", [1024], F32).ap()

    es = ExitStack()
    sb = lambda n, s, d: es.enter_context(nc.sbuf_tensor(n, s, d))
    ps = lambda n, s, d: es.enter_context(nc.psum_tensor(n, s, d))
    k = KCtx(nc, es)

    pq = ps("pq", [128, 1024], F32)
    pv2 = ps("pv2", [128, 1024], F32)
    psc = [ps("psc0", [128, 512], F32), ps("psc1", [128, 512], F32)]
    ppt = [ps("ppt0", [128, 512], BF16), ps("ppt1", [128, 512], BF16)]
    SCORE = [psc[0][:, 0:384], psc[1][:, 0:384],
             pv2[:, 0:384], pv2[:, 512:896]]
    AV = [psc[0][:, 384:449], psc[1][:, 384:449],
          pv2[:, 384:449], pv2[:, 896:961]]

    idb = sb("idb", [128, 128], BF16)
    mask2 = sb("mask2", [128, 384], BF16)
    maskT3 = sb("maskT3", [128, 384], BF16)
    mask1 = sb("mask1", [128, 256], BF16)
    mask0 = sb("mask0", [128, 128], BF16)
    ones_bf = sb("ones_bf", [1, 512], BF16)
    rows_f = sb("rows_f", [1, 2560], F32)
    rows_b = sb("rows_b", [1, 2560], BF16)
    b2q_row = sb("b2q_row", [1, 1024], BF16)
    sq_scr = sb("sq_scr", [128, 1024], F32)
    c2t = sb("c2t", [128, 1024], F32)

    m1acc = sb("m1acc", [128, 1], F32)
    m2acc = sb("m2acc", [128, 1], F32)
    trow = sb("trow", [1, 128], F32)
    lmax = sb("lmax", [1, 1], F32)
    g8 = sb("g8", [1, 8], F32)
    gg = sb("gg", [1, 1], F32)
    s1bc = sb("s1bc", [128, 1], F32)
    rs1 = sb("rs1", [128, 1], F32)
    s2bc = sb("s2bc", [128, 1], F32)
    rs2 = sb("rs2", [128, 1], F32)
    sw1 = sb("sw1", [128, FC], F32)
    rsw1 = sb("rsw1", [128, FC], F32)
    c1 = sb("c1", [128, FC], F32)
    b1q = sb("b1q", [128, FC], F32)
    b1f_s = sb("b1f_s", [128, FC], F32)
    sw2 = sb("sw2", [128, DC], F32)
    rsw2 = sb("rsw2", [128, DC], F32)
    c2 = sb("c2", [128, DC], F32)
    b2qf = sb("b2qf", [128, DC], F32)
    b2f_s = sb("b2f_s", [128, DC], F32)
    tq32 = sb("tq32", [128, FC], I32)
    tqf = sb("tqf", [128, FC], F32)
    ln_sm = sb("ln_sm", [128, 2], F32)
    ln_ssq = sb("ln_ssq", [128, 2], F32)
    ln_mean = sb("ln_mean", [128, 2], F32)
    ln_var = sb("ln_var", [128, 2], F32)
    ln_lnv = sb("ln_lnv", [128, 2], F32)
    ln_rstd = sb("ln_rstd", [128, 2], F32)
    ln_t1 = sb("ln_t1", [128, 2], F32)
    ln_nmr = sb("ln_nmr", [128, 2], F32)

    def ln_a(ph, par, rpre_ap, n_free, sv_evt):
        p = slice(par, par + 1)
        ph.wait("vector", sv_evt)
        ev = ph.op("vector", lambda e: e.reduce_sum(
            ln_sm[:, p], rpre_ap, axis=AX), inc="v")
        ph.wait("vector", ev)
        ev = ph.op("vector", lambda e: e.tensor_scalar_mul(
            ln_mean[:, p], ln_sm[:, p], 1.0 / n_free), inc="v")
        ph.wait("scalar", sv_evt)
        ev_sq = ph.op("scalar", lambda e: e.activation(
            sq_scr[:, 0:n_free], rpre_ap, AF.Square, accum_out=ln_ssq[:, p]),
            inc="s")
        ph.wait("vector", ev_sq)
        ph.wait("vector", ev)
        ev = ph.op("vector", lambda e: e.tensor_scalar_mul(
            ln_ssq[:, p], ln_ssq[:, p], 1.0 / n_free), inc="v")
        evm = ph.op("vector", lambda e: e.tensor_mul(
            ln_var[:, p], ln_mean[:, p], ln_mean[:, p]), inc="v")
        ph.wait("vector", ev)
        ph.wait("vector", evm)
        ev = ph.op("vector", lambda e: e.tensor_sub(
            ln_var[:, p], ln_ssq[:, p], ln_var[:, p]), inc="v")
        ph.wait("vector", ev)
        ev_var = ph.op("vector", lambda e: e.tensor_scalar_add(
            ln_var[:, p], ln_var[:, p], EPS_LN), inc="v")
        return ev_var

    def ln_b(ph, par, ev_var):
        p = slice(par, par + 1)
        ph.wait("scalar", ev_var)
        ev = ph.op("scalar", lambda e: e.activation(
            ln_lnv[:, p], ln_var[:, p], AF.Ln), inc="s")
        ph.wait("scalar", ev)
        return ph.op("scalar", lambda e: e.activation(
            ln_rstd[:, p], ln_lnv[:, p], AF.Exp, scale=-0.5), inc="s")

    def ln_c(ph, par, rpre_ap, out_ap, ev_rstd):
        p = slice(par, par + 1)
        ph.wait("vector", ev_rstd)
        ev = ph.op("vector", lambda e: e.tensor_mul(
            ln_t1[:, p], ln_rstd[:, p], ln_rstd[:, p]), inc="v")
        ph.wait("vector", ev)
        ev = ph.op("vector", lambda e: e.tensor_mul(
            ln_t1[:, p], ln_t1[:, p], ln_var[:, p]), inc="v")
        ph.wait("vector", ev)
        ev = ph.op("vector", lambda e: e.tensor_scalar(
            ln_t1[:, p], ln_t1[:, p], -0.5, 1.5, ALU.mult, ALU.add), inc="v")
        ph.wait("vector", ev)
        ev = ph.op("vector", lambda e: e.tensor_mul(
            ln_rstd[:, p], ln_rstd[:, p], ln_t1[:, p]), inc="v")
        ph.wait("vector", ev)
        ev = ph.op("vector", lambda e: e.scalar_tensor_tensor(
            ln_nmr[:, p], ln_mean[:, p], -1.0, ln_rstd[:, p],
            ALU.mult, ALU.mult), inc="v")
        ph.wait("vector", ev)
        return ph.op("vector", lambda e: e.tensor_scalar(
            out_ap, rpre_ap, ln_rstd[:, par:par + 1], ln_nmr[:, par:par + 1],
            ALU.mult, ALU.add), inc="v")

    def gather_scale(ph, macc, md, cci, cco, sd, sbc, rsb):
        ev1 = ph.dma("gpsimd", md[:], macc[:, 0:1])
        ph.wait("gpsimd", ev1)
        ev2 = ph.dma("gpsimd", trow[0:1, :], md[None, :])
        ph.wait("vector", ev2)
        ev_lm = ph.op("vector", lambda e: e.reduce_max(
            lmax[:, :], trow[:, :], axis=AX), inc="v")
        ph.wait("gpsimd", ev_lm)
        ev3 = ph.dma("gpsimd", cci[:], lmax[0:1, 0:1])
        ph.wait("gpsimd", ev3)
        ev_cc = ph.op("gpsimd", lambda e: e.collective_compute(
            "AllGather", ALU.bypass, replica_groups=[list(range(8))],
            ins=[cci[:].opt()], outs=[cco[:].opt()]), inc="cc")
        ph.wait("gpsimd", ev_cc)
        ev4 = ph.dma("gpsimd", g8[0:1, :], cco[None, :])
        ph.wait("vector", ev4)
        evr = ph.op("vector", lambda e: e.reduce_max(gg[:, :], g8[:, :], axis=AX),
                    inc="v")
        ph.wait("vector", evr)
        ev_s = ph.op("vector", lambda e: e.tensor_scalar(
            gg[:, :], gg[:, :], EPS_Q, 1.0 / 255.0, ALU.max, ALU.mult), inc="v")
        ph.wait("gpsimd", ev_s)
        ev5 = ph.dma("gpsimd", sd[:], gg[0:1, 0:1])
        ph.wait("gpsimd", ev5)
        ev6 = ph.dma("gpsimd", sbc[:, :], sd[None, :].partition_broadcast(128))
        ph.wait("vector", ev6)
        return ph.op("vector", lambda e: e.reciprocal(rsb[:, :], sbc[:, :]),
                     inc="v")

    with es:
        a_es = ExitStack()
        sba = lambda n, s, d: a_es.enter_context(nc.sbuf_tensor(n, s, d))
        x_fm = sba("x_fm", [128, DC * T], BF16)
        q_fm = sba("q_fm", [128, DC * T], BF16)
        k2 = sba("k2", [128, 4 * T], BF16)
        v_tm = sba("v_tm", [128, NT * 260], BF16)
        wqb = sba("wqb", [128, DC * 1024], BF16)
        wkb = sba("wkb", [128, DC * 256], BF16)
        wvb = sba("wvb", [128, DC * 256], BF16)
        wob = sba("wob", [128, DC * 1024], BF16)
        xld = [sba("xld0", [128, 1024], F32), sba("xld1", [128, 1024], F32)]
        xbf = sba("xbf", [128, 1024], BF16)
        PT = [sba("PT%d" % i, [128, 384], BF16) for i in range(4)]
        negmb = sba("negmb", [128, 4], BF16)
        nr = sba("nr", [1, 512], BF16)
        rl = sba("rl", [128, 4], F32)
        y_tm = sba("y_tm", [128, 1024], BF16)
        y_fm = sba("y_fm", [128, 1024], BF16)
        xres = [sba("xres0", [128, 1024], F32), sba("xres1", [128, 1024], F32)]
        rpre = [sba("rpre0", [128, 1024], F32), sba("rpre1", [128, 1024], F32)]
        r1t = [sba("r1t0", [128, 1024], F32), sba("r1t1", [128, 1024], F32)]
        tmx = sba("tmx", [128, 1], F32)
        wst = xld

        ph = Ph(k)

        def g_init(e):
            e.memset(idb[:, :], 1.0)
            e.affine_select(out=idb[:, :], in_=idb[:, :], pattern=[[-1, 128]],
                            compare_op=ALU.is_equal, fill=0.0, base=0,
                            channel_multiplier=1)
            e.memset(maskT3[:, :], 0.0)
            e.affine_select(out=maskT3[:, 0:128], in_=maskT3[:, 0:128],
                            pattern=[[-1, 128]], compare_op=ALU.is_ge,
                            fill=NEG, base=0, channel_multiplier=1)
            e.affine_select(out=maskT3[:, 128:256], in_=maskT3[:, 128:256],
                            pattern=[[-1, 128]], compare_op=ALU.is_ge,
                            fill=NEG, base=128, channel_multiplier=1)
            e.affine_select(out=maskT3[:, 128:256], in_=maskT3[:, 128:256],
                            pattern=[[1, 128]], compare_op=ALU.is_ge,
                            fill=NEG, base=128, channel_multiplier=-1)
            e.affine_select(out=maskT3[:, 256:384], in_=maskT3[:, 256:384],
                            pattern=[[1, 128]], compare_op=ALU.is_ge,
                            fill=NEG, base=0, channel_multiplier=-1)
            e.memset(mask2[:, :], 0.0)
            e.affine_select(out=mask2[:, :], in_=mask2[:, :], pattern=[[1, 384]],
                            compare_op=ALU.is_ge, fill=NEG, base=0,
                            channel_multiplier=-1)
            e.affine_select(out=mask2[:, :], in_=mask2[:, :], pattern=[[-1, 384]],
                            compare_op=ALU.is_ge, fill=NEG, base=256,
                            channel_multiplier=1)
            e.memset(mask1[:, :], 0.0)
            e.affine_select(out=mask1[:, :], in_=mask1[:, :], pattern=[[-1, 256]],
                            compare_op=ALU.is_ge, fill=NEG, base=128,
                            channel_multiplier=1)
            e.memset(mask0[:, :], 0.0)
            e.affine_select(out=mask0[:, :], in_=mask0[:, :], pattern=[[-1, 128]],
                            compare_op=ALU.is_ge, fill=NEG, base=0,
                            channel_multiplier=1)
            e.memset(ones_bf[:, :], 1.0)
            e.memset(m1acc[:, :], NEG)
            e.memset(m2acc[:, :], 0.0)
            vap = v_tm[:, :].rearrange("p (t g c) -> p t g c", t=NT, g=4)
            e.memset(vap[:, :, :, 64:65], 1.0)
            return e.engine_nop()
        ev_g = ph.op("gpsimd", g_init, inc="g")

        ev_rows = ph.dma("sync", rows_f[:, :], rows_d[:, :])
        ph.wait("vector", ev_rows)
        ev_rowsb = ph.op("vector", lambda e: e.tensor_copy(
            rows_b[:, :], rows_f[:, :]), inc="v")

        wsrc = [(wq_d, wqb, 1024), (wo_d, wob, 1024), (wk_d, wkb, 256),
                (wv_d, wvb, 256)]
        cast_evts = []
        i = 0
        for (src, dst, w) in wsrc:
            for dc in range(DC):
                buf = wst[i % 2]
                if i >= 2:
                    ph.wait("sync", cast_evts[i - 2])
                ev_l = ph.dma("sync", buf[:, 0:w], src[dc * 128:(dc + 1) * 128, :])
                ph.wait("vector", ev_l)
                ev_c = ph.op("vector", lambda e, d=dst, dc=dc, b=buf, w=w:
                             e.tensor_copy(d[:, dc * w:(dc + 1) * w], b[:, 0:w]),
                             inc="v")
                cast_evts.append(ev_c)
                i += 1
        ph.run(nc)

        ph = Ph(k)
        for eng in ENGS:
            ph.wait(eng, ev_g)
            ph.wait(eng, ev_rowsb)
            ph.wait(eng, cast_evts[-1])

        pt_free = {0: None, 1: None}
        r1_store = {}
        r1_mx = {}
        sl_exp = {i: None for i in range(4)}
        sl_nm = {i: None for i in range(4)}
        sl_ye = {i: None for i in range(4)}
        sl_pte = {i: None for i in range(4)}
        outproj_vread = None
        pend_stat = None
        pend_bc = None
        pending_tail = None
        ev_kd = None

        def flush_ln_c(ph):
            nonlocal pend_bc
            if pend_bc is None:
                return
            (gq, par, ev_var), ev_rstd = pend_bc
            pend_bc = None
            rb = gq % 2
            ph.wait("vector", r1_store.get(gq - 2))
            ph.wait("vector", r1_mx.get(gq - 2))
            ev_r1 = ln_c(ph, par, rpre[par][:, :], r1t[rb][:, :], ev_rstd)
            ph.wait("vector", ev_r1)
            ev_tm = ph.op("vector", lambda e, rb=rb: e.reduce_max(
                tmx[:, :], r1t[rb][:, :], axis=AX), inc="v")
            ph.wait("vector", ev_tm)
            r1_mx[gq] = ph.op("vector", lambda e: e.tensor_max(
                m1acc[:, :], m1acc[:, :], tmx[:, :]), inc="v")
            ph.wait("gpsimd", ev_r1)
            r1_store[gq] = ph.dma("gpsimd", r1_d[gq * 128:(gq + 1) * 128, :],
                                  r1t[rb][:, :])

        def flush_ln_b(ph):
            nonlocal pend_stat, pend_bc
            if pend_stat is None:
                return
            pend_bc = (pend_stat, ln_b(ph, pend_stat[1], pend_stat[2]))
            pend_stat = None

        for b in range(B):
            xfm_ready = None
            xbf_cons = None
            xld_cons = {}
            for tt in range(NT):
                buf = xld[tt % 2]
                if tt >= 2:
                    ph.wait("sync", xld_cons[tt - 2])
                ev_l = ph.dma("sync", buf[:, :],
                              x_d[b, tt * 128:(tt + 1) * 128, :])
                ph.wait("vector", ev_l)
                if xbf_cons is not None:
                    ph.wait("vector", xbf_cons)
                ev_cast = ph.op("vector", lambda e, buf=buf: e.tensor_copy(
                    xbf[:, :], buf[:, :]), inc="v")
                xld_cons[tt] = ev_cast
                for half in range(2):
                    bank = ppt[half]
                    ph.wait("tensor", ev_cast)
                    ph.wait("tensor", pt_free[half])
                    def tfn(e, half=half, bank=bank):
                        for j in range(4):
                            dc = half * 4 + j
                            ins = e.transpose(bank[:, j * 128:(j + 1) * 128],
                                              xbf[:, dc * 128:(dc + 1) * 128],
                                              idb[:, :])
                        return ins
                    ev_t = ph.op("tensor", tfn, inc="t")
                    ph.wait("vector", ev_t)
                    dst = x_fm[:, :].rearrange("p (d t) -> p d t", d=DC)[
                        :, half * 4:(half + 1) * 4, tt * 128:(tt + 1) * 128]
                    ev_e = ph.op("vector", lambda e, bank=bank, dst=dst:
                                 e.tensor_copy(dst, bank[:, :].rearrange(
                                     "p (d t) -> p d t", d=4)), inc="v")
                    pt_free[half] = ev_e
                    xfm_ready = ev_e
                xbf_cons = ev_t
            kv_ready = None
            q_ready = None
            for tw in range(4):
                for mc in range(2):
                    pslc = pq[:, mc * 512:(mc + 1) * 512]
                    ph.wait("tensor", xfm_ready)
                    ph.wait("tensor", outproj_vread)
                    if kv_ready is not None:
                        ph.wait("tensor", kv_ready)
                    def kfn(e, tw=tw, mc=mc, pslc=pslc):
                        e.matmul(pslc,
                                 rows_b[0:1, 2048 + mc * 128:2048 + (mc + 1) * 128],
                                 ones_bf[0:1, 0:512], start=True, stop=False)
                        for dc in range(DC):
                            ins = e.matmul(
                                pslc,
                                wkb[:, dc * 256 + mc * 128:dc * 256 + (mc + 1) * 128],
                                x_fm[:, dc * T + tw * 512:dc * T + (tw + 1) * 512],
                                start=False, stop=(dc == DC - 1))
                        return ins
                    ev_k = ph.op("tensor", kfn, inc="t")
                    ph.wait("vector", ev_k)
                    ph.wait("vector", ev_kd)
                    g0, g1 = 2 * mc, 2 * mc + 1
                    def kev(e, pslc=pslc, g0=g0, g1=g1, tw=tw):
                        e.tensor_copy(
                            k2[0:64, g0 * T + tw * 512:g0 * T + (tw + 1) * 512],
                            pslc[0:64, :])
                        return e.tensor_copy(
                            k2[64:128, g1 * T + tw * 512:g1 * T + (tw + 1) * 512],
                            pslc[64:128, :])
                    kv_ready = ph.op("vector", kev, inc="v")
                for mc in range(DC):
                    pslc = pq[:, (mc % 2) * 512:((mc % 2) + 1) * 512]
                    ph.wait("tensor", kv_ready if mc < 2 else q_ready)
                    def qfn(e, tw=tw, mc=mc, pslc=pslc):
                        e.matmul(pslc, rows_b[0:1, mc * 128:(mc + 1) * 128],
                                 ones_bf[0:1, 0:512], start=True, stop=False)
                        for dc in range(DC):
                            ins = e.matmul(
                                pslc,
                                wqb[:, dc * 1024 + mc * 128:dc * 1024 + (mc + 1) * 128],
                                x_fm[:, dc * T + tw * 512:dc * T + (tw + 1) * 512],
                                start=False, stop=(dc == DC - 1))
                        return ins
                    ev_q = ph.op("tensor", qfn, inc="t")
                    ph.wait("scalar", ev_q)
                    q_ready = ph.op("scalar", lambda e, tw=tw, mc=mc, pslc=pslc:
                                    e.activation(
                                        q_fm[:, mc * T + tw * 512:mc * T + (tw + 1) * 512],
                                        pslc, AF.Identity, scale=0.125), inc="s")
            ph.wait("gpsimd", kv_ready)
            ph.dma("gpsimd", k2[64:128, 0:T], k2[0:64, 0:T])
            ph.dma("gpsimd", k2[64:128, 2 * T:3 * T], k2[0:64, 2 * T:3 * T])
            ph.dma("gpsimd", k2[0:64, T:2 * T], k2[64:128, T:2 * T])
            ev_kd = ph.dma("gpsimd", k2[0:64, 3 * T:4 * T],
                           k2[64:128, 3 * T:4 * T])
            v_ready = None
            for tt in range(NT):
                pslc = pv2[:, 0:256]
                ph.wait("tensor", xfm_ready)
                ph.wait("tensor", sl_exp[2])
                ph.wait("tensor", sl_exp[3])
                ph.wait("tensor", sl_ye[2])
                ph.wait("tensor", sl_ye[3])
                if v_ready is not None:
                    ph.wait("tensor", v_ready)
                def vfn(e, tt=tt, pslc=pslc):
                    e.matmul(pslc, ones_bf[0:1, 0:128],
                             rows_b[0:1, 2304:2560], start=True, stop=False)
                    for dc in range(DC):
                        ins = e.matmul(
                            pslc, x_fm[:, dc * T + tt * 128:dc * T + (tt + 1) * 128],
                            wvb[:, dc * 256:(dc + 1) * 256],
                            start=False, stop=(dc == DC - 1))
                    return ins
                ev_v = ph.op("tensor", vfn, inc="t")
                ph.wait("vector", ev_v)
                def vev(e, tt=tt, pslc=pslc):
                    for g in range(4):
                        ins = e.tensor_copy(
                            v_tm[:, tt * 260 + g * 65:tt * 260 + g * 65 + 64],
                            pslc[:, g * 64:(g + 1) * 64])
                    return ins
                v_ready = ph.op("vector", vev, inc="v")

            xres_cons = {}
            for qt in range(NT):
                kts = min(qt + 1, 3)
                klen = kts * 128
                k0 = (qt - kts + 1) * 128
                mask = {1: mask0, 2: mask1, 3: mask2}[kts]
                gq = b * NT + qt
                par = gq % 2
                if qt >= 2:
                    ph.wait("sync", xres_cons.get(qt - 2))
                ev_xr = ph.dma("sync", xres[qt % 2][:, :],
                               x_d[b, qt * 128:(qt + 1) * 128, :])

                nr_rdy = {}
                st_done = {}
                for step in range(NH + 2):
                    if step < NH:
                        h = step
                        sl = h % 4
                        g = h // 4
                        qoff = (h % 2) * 64
                        ph.wait("tensor", q_ready)
                        ph.wait("tensor", kv_ready)
                        ph.wait("tensor", ev_kd)
                        ph.wait("tensor", v_ready)
                        ph.wait("tensor", sl_exp[sl])
                        def dfn(e, sl=sl, g=g, qoff=qoff, qt=qt, h=h):
                            return e.matmul(
                                SCORE[sl][:, 0:128],
                                q_fm[qoff:qoff + 64,
                                     (h // 2) * T + qt * 128:(h // 2) * T + (qt + 1) * 128],
                                k2[qoff:qoff + 64,
                                   g * T + qt * 128:g * T + (qt + 1) * 128],
                                start=True, stop=True)
                        ev_d = ph.op("tensor", dfn, inc="t")
                        ph.wait("vector", ev_d)
                        ev_nm = ph.op("vector", lambda e, sl=sl: e.reduce_max(
                            negmb[:, sl:sl + 1], SCORE[sl][:, 0:128],
                            axis=AX, negate=True), inc="v")
                        sl_nm[sl] = ev_nm
                        ph.wait("tensor", ev_nm)
                        ph.wait("tensor", pt_free[sl % 2])
                        ev_ntr = ph.op("tensor", lambda e, sl=sl: e.transpose(
                            ppt[sl % 2][0:1, 0:128], negmb[:, sl:sl + 1],
                            idb[:, :]), inc="t")
                        ph.wait("vector", ev_ntr)
                        ev_nre = ph.op("vector", lambda e, sl=sl: e.tensor_copy(
                            nr[0:1, sl * 128:(sl + 1) * 128],
                            ppt[sl % 2][0:1, 0:128]), inc="v")
                        pt_free[sl % 2] = ev_nre
                        nr_rdy[h] = ev_nre
                        if step == 3:
                            flush_ln_b(ph)
                        if step == 8 and pending_tail is not None:
                            pending_tail(ph)
                            pending_tail = None
                    if 1 <= step <= NH:
                        h1 = step - 1
                        sl1 = h1 % 4
                        g1 = h1 // 4
                        qoff1 = (h1 % 2) * 64
                        qsl = q_fm[qoff1:qoff1 + 64,
                                   (h1 // 2) * T + qt * 128:(h1 // 2) * T + (qt + 1) * 128]
                        ph.wait("tensor", nr_rdy[h1])
                        def stfn(e, sl=sl1, g=g1, qoff=qoff1, qt=qt, k0=k0,
                                 kts=kts, qsl=qsl):
                            for jb in range(kts):
                                e.matmul(
                                    SCORE[sl][:, jb * 128:(jb + 1) * 128],
                                    k2[qoff:qoff + 64,
                                       g * T + k0 + jb * 128:g * T + k0 + (jb + 1) * 128],
                                    qsl, start=True, stop=False)
                                e.matmul(SCORE[sl][:, jb * 128:(jb + 1) * 128],
                                         idb[:, :],
                                         maskT3[:, (3 - kts + jb) * 128:
                                                (4 - kts + jb) * 128],
                                         start=False, stop=False)
                                ins = e.matmul(
                                    SCORE[sl][:, jb * 128:(jb + 1) * 128],
                                    ones_bf[0:1, 0:128],
                                    nr[0:1, sl * 128:(sl + 1) * 128],
                                    start=False, stop=True)
                            return ins
                        ev_st = ph.op("tensor", stfn, inc="t")
                        st_done[h1] = ev_st
                        ph.wait("scalar", ev_st)
                        ev_exp = ph.op("scalar", lambda e, sl=sl1, klen=klen:
                                       e.activation(PT[sl][:, 0:klen],
                                                    SCORE[sl][:, 0:klen],
                                                    AF.Exp), inc="s")
                        sl_exp[sl1] = ev_exp
                    if step >= 2:
                        h2 = step - 2
                        sl2 = h2 % 4
                        g2 = h2 // 4
                        ph.wait("tensor", sl_exp[sl2])
                        ph.wait("tensor", sl_ye[sl2])
                        def avfn(e, sl2=sl2, kts=kts, k0=k0, g2=g2):
                            for kt in range(kts):
                                ktok = k0 // 128 + kt
                                ins = e.matmul(
                                    AV[sl2],
                                    PT[sl2][:, kt * 128:(kt + 1) * 128],
                                    v_tm[:, ktok * 260 + g2 * 65:ktok * 260 + (g2 + 1) * 65],
                                    start=(kt == 0), stop=(kt == kts - 1))
                            return ins
                        ev_av = ph.op("tensor", avfn, inc="t")
                        ph.wait("vector", ev_av)
                        ev_rl = ph.op("vector", lambda e, sl2=sl2: e.reciprocal(
                            rl[:, sl2:sl2 + 1], AV[sl2][:, 64:65]), inc="v")
                        ph.wait("vector", ev_rl)
                        ev_ye = ph.op("vector", lambda e, sl2=sl2, h2=h2:
                                      e.tensor_scalar_mul(
                                          y_tm[:, h2 * 64:(h2 + 1) * 64],
                                          AV[sl2][:, 0:64], rl[:, sl2:sl2 + 1]),
                                      inc="v")
                        sl_ye[sl2] = ev_ye
                        yev_last = ev_ye
                flush_ln_c(ph)

                for half in range(2):
                    bank = ppt[half]
                    ph.wait("tensor", yev_last)
                    ph.wait("tensor", pt_free[half])
                    def ytr(e, half=half, bank=bank):
                        for j in range(4):
                            c = half * 4 + j
                            ins = e.transpose(bank[:, j * 128:(j + 1) * 128],
                                              y_tm[:, c * 128:(c + 1) * 128],
                                              idb[:, :])
                        return ins
                    ev_yt = ph.op("tensor", ytr, inc="t")
                    ph.wait("vector", ev_yt)
                    ev_yfe = ph.op("vector", lambda e, half=half, bank=bank:
                                   e.tensor_copy(
                                       y_fm[:, half * 512:(half + 1) * 512],
                                       bank[:, :]), inc="v")
                    pt_free[half] = ev_yfe
                    yfm_ready = ev_yfe
                ph.wait("tensor", yfm_ready)
                ph.wait("tensor", outproj_vread)
                def ofn(e):
                    for n2 in range(2):
                        pslc = pq[:, n2 * 512:(n2 + 1) * 512]
                        e.matmul(pslc, ones_bf[0:1, 0:128],
                                 rows_b[0:1, 1024 + n2 * 512:1024 + (n2 + 1) * 512],
                                 start=True, stop=False)
                        for dc in range(DC):
                            ins = e.matmul(
                                pslc, y_fm[:, dc * 128:(dc + 1) * 128],
                                wob[:, dc * 1024 + n2 * 512:dc * 1024 + (n2 + 1) * 512],
                                start=False, stop=(dc == DC - 1))
                    return ins
                ev_o = ph.op("tensor", ofn, inc="t")
                ph.wait("vector", ev_o)
                ph.wait("vector", ev_xr)
                ph.wait("vector", r1_store.get(gq - 2))
                ev_rp = ph.op("vector", lambda e, par=par, qt=qt: e.tensor_add(
                    rpre[par][:, :], pq[:, :], xres[qt % 2][:, :]), inc="v")
                xres_cons[qt] = ev_rp
                outproj_vread = ev_rp
                def tail(ph, gq=gq, par=par, ev_rp=ev_rp):
                    nonlocal pend_stat
                    pend_stat = (gq, par,
                                 ln_a(ph, par, rpre[par][:, :], 1024, ev_rp))
                pending_tail = tail
            flush_ln_b(ph)
            flush_ln_c(ph)
            if pending_tail is not None:
                pending_tail(ph)
                pending_tail = None
            flush_ln_b(ph)
            flush_ln_c(ph)

        ph.wait("gpsimd", k.evt("v"))
        ph.wait("gpsimd", r1_store[B * NT - 1])
        ev_rs1 = gather_scale(ph, m1acc, m1_d, cc1i_d, cc1o_d, s1_d, s1bc, rs1)
        ph.run(nc)
        a_es.close()

        b_es = ExitStack()
        sbb = lambda n, s, d: b_es.enter_context(nc.sbuf_tensor(n, s, d))
        w1T = sbb("w1T", [128, DC * F], BF16)
        w1st = [sbb("w1st0", [128, 1024], F32), sbb("w1st1", [128, 1024], F32)]
        w1i8 = sbb("w1i8", [128, 1024], I8)
        w1cb = sbb("w1cb", [128, 1024], BF16)
        r1ld = [sbb("r1ld0", [128, 4096], F32), sbb("r1ld1", [128, 4096], F32)]
        xq_u8 = sbb("xq_u8", [128, 4096], U8)
        xq_bf = sbb("xq_bf", [128, 4096], BF16)
        xq_fm = sbb("xq_fm", [128, DC * 512], BF16)
        h_t = [sbb("h_t0", [128, 512], F32), sbb("h_t1", [128, 512], F32)]
        hmx = sbb("hmx", [128, 1], F32)
        amax_t = sbb("amax_t", [128, 1], F32)

        ph = Ph(k)
        pt_free = {0: None, 1: None}
        w1_cons = {}
        w1_amax = {}
        w1_trd = {}
        w1T_ready = None
        for fc in range(FC):
            buf = w1st[fc % 2]
            if fc >= 2:
                ph.wait("sync", w1_cons[fc - 2])
                ph.wait("sync", w1_amax[fc - 2])
            ev_l = ph.dma("sync", buf[:, :], w1_d[fc * 128:(fc + 1) * 128, :])
            ph.wait("vector", ev_l)
            ev_a = ph.op("vector", lambda e, buf=buf: e.reduce_max(
                amax_t[:, :], buf[:, :], axis=AX,
                apply_absolute_value=True), inc="v")
            ph.wait("vector", ev_a)
            ev_a = ph.op("vector", lambda e, fc=fc: e.tensor_scalar(
                sw1[:, fc:fc + 1], amax_t[:, :], EPS_Q,
                1.0 / 127.0, ALU.max, ALU.mult), inc="v")
            ph.wait("vector", ev_a)
            ev_a = ph.op("vector", lambda e, fc=fc: e.reciprocal(
                rsw1[:, fc:fc + 1], sw1[:, fc:fc + 1]), inc="v")
            w1_amax[fc] = ev_a
            ph.wait("scalar", ev_a)
            ph.wait("scalar", w1_trd.get(fc - 1))
            def w1qz(e, fc=fc, buf=buf):
                e.activation(w1i8[:, :], buf[:, :], AF.Identity,
                             scale=rsw1[:, fc:fc + 1])
                return e.activation(w1cb[:, :], w1i8[:, :], AF.Identity)
            ev_qz = ph.op("scalar", w1qz, inc="s")
            w1_cons[fc] = ev_qz
            for half in range(2):
                bank = ppt[half]
                ph.wait("tensor", ev_qz)
                ph.wait("tensor", pt_free[half])
                def w1tr(e, half=half, bank=bank):
                    for j in range(4):
                        dc = half * 4 + j
                        ins = e.transpose(bank[:, j * 128:(j + 1) * 128],
                                          w1cb[:, dc * 128:(dc + 1) * 128],
                                          idb[:, :])
                    return ins
                ev_t = ph.op("tensor", w1tr, inc="t")
                w1_trd[fc] = ev_t
                ph.wait("vector", ev_t)
                dst = w1T[:, :].rearrange("p (d f) -> p d f", d=DC)[
                    :, half * 4:(half + 1) * 4, fc * 128:(fc + 1) * 128]
                ev_e = ph.op("vector", lambda e, bank=bank, dst=dst:
                             e.tensor_copy(dst, bank[:, :].rearrange(
                                 "p (d t) -> p d t", d=4)), inc="v")
                pt_free[half] = ev_e
                w1T_ready = ev_e

        ev_b1 = ph.dma("sync", b1f_s[:, :], b1f_d[:, :])
        ph.wait("vector", ev_rs1)
        ph.wait("vector", ev_b1)
        ph.wait("vector", w1_amax[FC - 1])
        ev_c1 = ph.op("vector", lambda e: e.tensor_scalar_mul(
            c1[:, :], sw1[:, :], s1bc[:, 0:1]), inc="v")
        ph.wait("vector", ev_c1)
        ev_c1 = ph.op("vector", lambda e: e.reciprocal(
            tqf[:, :], c1[:, :]), inc="v")
        ph.wait("vector", ev_c1)
        ev_c1 = ph.op("vector", lambda e: e.tensor_mul(
            tqf[:, :], b1f_s[:, :], tqf[:, :]), inc="v")
        ph.wait("scalar", ev_c1)
        ev_b1r = ph.op("scalar", lambda e: e.activation(
            tq32[:, :], tqf[:, :], AF.Identity), inc="s")
        ph.wait("scalar", ev_b1r)
        ev_b1r = ph.op("scalar", lambda e: e.activation(
            tqf[:, :], tq32[:, :], AF.Identity), inc="s")
        ph.wait("vector", ev_b1r)
        ev_b1q = ph.op("vector", lambda e: e.tensor_mul(
            b1q[:, :], tqf[:, :], c1[:, :]), inc="v")

        ht_hm = {}
        ht_ev = {}
        ht_st = {}
        r1ld_cons = {}
        xq_trdone = None
        for it in range(8):
            buf = r1ld[it % 2]
            if it >= 2:
                ph.wait("sync", r1ld_cons[it - 2])
            src = r1_d[it * 512:(it + 1) * 512, :].rearrange(
                "(q p) c -> p q c", p=128)
            ev_l = ph.dma("sync", buf[:, :].rearrange("p (q c) -> p q c", q=4),
                          src)
            ph.wait("scalar", ev_l)
            ph.wait("scalar", ev_rs1)
            ph.wait("scalar", xq_trdone)
            def bq8(e, buf=buf):
                e.activation(xq_u8[:, :], buf[:, :], AF.Identity,
                             scale=rs1[:, 0:1])
                return e.activation(xq_bf[:, :], xq_u8[:, :], AF.Identity)
            ev_q8 = ph.op("scalar", bq8, inc="s")
            r1ld_cons[it] = ev_q8
            xq_ready = None
            for dc in range(DC):
                bank = ppt[dc % 2]
                ph.wait("tensor", ev_q8)
                ph.wait("tensor", pt_free[dc % 2])
                def xtr(e, dc=dc, bank=bank):
                    for j in range(4):
                        ins = e.transpose(
                            bank[:, j * 128:(j + 1) * 128],
                            xq_bf[:, j * 1024 + dc * 128:j * 1024 + (dc + 1) * 128],
                            idb[:, :])
                    return ins
                ev_t = ph.op("tensor", xtr, inc="t")
                ph.wait("vector", ev_t)
                ev_e = ph.op("vector", lambda e, dc=dc, bank=bank: e.tensor_copy(
                    xq_fm[:, dc * 512:(dc + 1) * 512], bank[:, :]), inc="v")
                pt_free[dc % 2] = ev_e
                xq_ready = ev_e
            xq_trdone = k.evt("t")
            for fc in range(FC):
                gidx = it * FC + fc
                pslc = pq[:, (fc % 2) * 512:((fc % 2) + 1) * 512]
                ph.wait("tensor", xq_ready)
                ph.wait("tensor", w1T_ready)
                ph.wait("tensor", ht_ev.get(gidx - 2))
                def mm1(e, fc=fc, pslc=pslc):
                    for dc in range(DC):
                        ins = e.matmul(
                            pslc, w1T[:, dc * F + fc * 128:dc * F + (fc + 1) * 128],
                            xq_fm[:, dc * 512:(dc + 1) * 512],
                            start=(dc == 0), stop=(dc == DC - 1))
                    return ins
                ev_m = ph.op("tensor", mm1, inc="t")
                hb = gidx % 2
                ph.wait("scalar", ev_m)
                ph.wait("scalar", ev_b1q)
                ph.wait("scalar", ht_hm.get(gidx - 2))
                ph.wait("scalar", ht_st.get(gidx - 2))
                ev_h = ph.op("scalar", lambda e, fc=fc, hb=hb, pslc=pslc:
                             e.activation(h_t[hb][:, :], pslc, AF.Relu,
                                          scale=c1[:, fc:fc + 1],
                                          bias=b1q[:, fc:fc + 1]), inc="s")
                ht_ev[gidx] = ev_h
                ph.wait("vector", ev_h)
                ev_hx = ph.op("vector", lambda e, hb=hb: e.reduce_max(
                    hmx[:, :], h_t[hb][:, :], axis=AX), inc="v")
                ph.wait("vector", ev_hx)
                ht_hm[gidx] = ph.op("vector", lambda e: e.tensor_max(
                    m2acc[:, :], m2acc[:, :], hmx[:, :]), inc="v")
                ph.wait("gpsimd", ev_h)
                ht_st[gidx] = ph.dma("gpsimd",
                                     h_d[fc, :, it * 512:(it + 1) * 512],
                                     h_t[hb][:, :])

        ph.wait("gpsimd", k.evt("v"))
        ph.wait("gpsimd", ht_st[8 * FC - 1])
        ev_rs2 = gather_scale(ph, m2acc, m2_d, cc2i_d, cc2o_d, s2_d, s2bc, rs2)
        ph.run(nc)
        b_es.close()

        c_es = ExitStack()
        sbc = lambda n, s, d: c_es.enter_context(nc.sbuf_tensor(n, s, d))
        w2T = sbc("w2T", [128, FC * 1024], BF16)
        t_es = ExitStack()
        sbt = lambda n, s, d: t_es.enter_context(nc.sbuf_tensor(n, s, d))
        w2st = [sbt("w2st0", [128, 4096], F32), sbt("w2st1", [128, 4096], F32)]
        w2i8 = sbt("w2i8", [128, 4096], I8)
        w2cb = sbt("w2cb", [128, 4096], BF16)
        amax2 = sbt("amax2", [128, 1], F32)

        ph = Ph(k)
        pt_free = {0: None, 1: None}
        w2_cons = {}
        w2_amax = {}
        w2_trd = {}
        w2T_ready = None
        for dc in range(DC):
            buf = w2st[dc % 2]
            if dc >= 2:
                ph.wait("sync", w2_cons[dc - 2])
                ph.wait("sync", w2_amax[dc - 2])
            ev_l = ph.dma("sync", buf[:, :], w2_d[dc * 128:(dc + 1) * 128, :])
            ph.wait("vector", ev_l)
            ev_a = ph.op("vector", lambda e, buf=buf: e.reduce_max(
                amax2[:, :], buf[:, :], axis=AX,
                apply_absolute_value=True), inc="v")
            ph.wait("vector", ev_a)
            ev_a = ph.op("vector", lambda e, dc=dc: e.tensor_scalar(
                sw2[:, dc:dc + 1], amax2[:, :], EPS_Q,
                1.0 / 127.0, ALU.max, ALU.mult), inc="v")
            ph.wait("vector", ev_a)
            ev_a = ph.op("vector", lambda e, dc=dc: e.reciprocal(
                rsw2[:, dc:dc + 1], sw2[:, dc:dc + 1]), inc="v")
            w2_amax[dc] = ev_a
            ph.wait("scalar", ev_a)
            ph.wait("scalar", w2_trd.get(dc - 1))
            def w2qz(e, dc=dc, buf=buf):
                e.activation(w2i8[:, :], buf[:, :], AF.Identity,
                             scale=rsw2[:, dc:dc + 1])
                return e.activation(w2cb[:, :], w2i8[:, :], AF.Identity)
            ev_qz = ph.op("scalar", w2qz, inc="s")
            w2_cons[dc] = ev_qz
            for grp in range(DC):
                bank = ppt[grp % 2]
                ph.wait("tensor", ev_qz)
                ph.wait("tensor", pt_free[grp % 2])
                def w2tr(e, grp=grp, bank=bank):
                    for j in range(4):
                        fc = grp * 4 + j
                        ins = e.transpose(bank[:, j * 128:(j + 1) * 128],
                                          w2cb[:, fc * 128:(fc + 1) * 128],
                                          idb[:, :])
                    return ins
                ev_t = ph.op("tensor", w2tr, inc="t")
                w2_trd[dc] = ev_t
                ph.wait("vector", ev_t)
                dst = w2T[:, :].rearrange("p (f d) -> p f d", f=FC)[
                    :, grp * 4:(grp + 1) * 4, dc * 128:(dc + 1) * 128]
                ev_e = ph.op("vector", lambda e, bank=bank, dst=dst:
                             e.tensor_copy(dst, bank[:, :].rearrange(
                                 "p (f t) -> p f t", f=4)), inc="v")
                pt_free[grp % 2] = ev_e
                w2T_ready = ev_e

        ev_b2 = ph.dma("sync", b2f_s[:, :], b2f_d[:, :])
        ph.wait("vector", ev_rs2)
        ph.wait("vector", ev_b2)
        ph.wait("vector", w2_amax[DC - 1])
        ev_c2 = ph.op("vector", lambda e: e.tensor_scalar_mul(
            c2[:, :], sw2[:, :], s2bc[:, 0:1]), inc="v")
        ph.wait("vector", ev_c2)
        ev_c2 = ph.op("vector", lambda e: e.reciprocal(
            tqf[:, 0:DC], c2[:, :]), inc="v")
        ph.wait("vector", ev_c2)
        ev_c2 = ph.op("vector", lambda e: e.tensor_mul(
            tqf[:, 0:DC], b2f_s[:, :], tqf[:, 0:DC]), inc="v")
        ph.wait("scalar", ev_c2)
        ev_b2r = ph.op("scalar", lambda e: e.activation(
            tq32[:, 0:DC], tqf[:, 0:DC], AF.Identity), inc="s")
        ph.wait("scalar", ev_b2r)
        ev_b2r = ph.op("scalar", lambda e: e.activation(
            tqf[:, 0:DC], tq32[:, 0:DC], AF.Identity), inc="s")
        ph.wait("vector", ev_b2r)
        ev_b2q = ph.op("vector", lambda e: e.tensor_mul(
            b2qf[:, :], tqf[:, 0:DC], c2[:, :]), inc="v")
        ph.wait("gpsimd", ev_b2q)
        ev1 = ph.dma("gpsimd", c2r_d[:, :], c2[:, :])
        ev2 = ph.dma("gpsimd", b2qr_d[:, :], b2qf[:, :])
        ph.wait("gpsimd", ev2)
        src_c = bass.AP(c2r_d.tensor, 0, [[1, DC], [DC, 128]])
        ev3 = ph.op("gpsimd", lambda e: e.dma_start(
            out=c2row_d[:].rearrange("(d p) -> d p", d=DC), in_=src_c,
            allow_slow_non_contiguous=True), inc="st", incby=16)
        ph.wait("gpsimd", ev3)
        ev4 = ph.dma("gpsimd", c2t[:, :],
                     c2row_d[None, :].partition_broadcast(128))
        src_b = bass.AP(b2qr_d.tensor, 0, [[1, DC], [DC, 128]])
        ev5a = ph.op("gpsimd", lambda e: e.dma_start(
            out=b2row_d[:].rearrange("(d p) -> d p", d=DC),
            in_=src_b, allow_slow_non_contiguous=True), inc="st", incby=16)
        ph.wait("gpsimd", ev5a)
        ev5 = ph.dma("gpsimd", rows_f[0:1, 0:1024], b2row_d[None, :])
        ph.wait("vector", ev5)
        ev_b2row = ph.op("vector", lambda e: e.tensor_copy(
            b2q_row[0:1, :], rows_f[0:1, 0:1024]), inc="v")
        ph.wait("gpsimd", ev4)
        ph.run(nc)
        t_es.close()

        d_es = ExitStack()
        sbd = lambda n, s, d: d_es.enter_context(nc.sbuf_tensor(n, s, d))
        hld = [sbd("hld0", [128, 512], F32), sbd("hld1", [128, 512], F32)]
        h_u8 = sbd("h_u8", [128, 512], U8)
        hq = [sbd("hq0", [128, FC * 512], BF16), sbd("hq1", [128, FC * 512], BF16)]
        r1res = [sbd("r1res0", [128, 1024], F32), sbd("r1res1", [128, 1024], F32)]
        t2 = sbd("t2", [128, 1024], F32)
        rpre2 = [sbd("rpre20", [128, 1024], F32), sbd("rpre21", [128, 1024], F32)]
        yt = [sbd("yt0", [128, 1024], F32), sbd("yt1", [128, 1024], F32)]

        ph = Ph(k)
        for eng in ENGS:
            ph.wait(eng, ev_b2row)
            ph.wait(eng, w2T_ready)
            ph.wait(eng, ev_rs2)
        hq_cons = {}
        r1r_cons = {}
        yt_store = {}
        mm_t = {}
        mm_v = {}
        pend2 = None

        def flush2(ph):
            nonlocal pend2
            if pend2 is None:
                return
            ptt, ppar, pev_var = pend2
            ev_rstd = ln_b(ph, ppar, pev_var)
            pbb, ptl = ptt // NT, ptt % NT
            ph.wait("vector", yt_store.get(ptt - 2))
            ev_y = ln_c(ph, ppar, rpre2[ppar][:, :], yt[ptt % 2][:, :],
                        ev_rstd)
            ph.wait("gpsimd", ev_y)
            yt_store[ptt] = ph.dma(
                "gpsimd", y_d[pbb, ptl * 128:(ptl + 1) * 128, :],
                yt[ptt % 2][:, :])
            pend2 = None

        for it in range(8):
            hqb = hq[it % 2]
            flush2(ph)
            for fc in range(FC):
                gidx = it * FC + fc
                eng = "sync" if fc % 2 == 0 else "gpsimd"
                buf = hld[fc % 2]
                ph.wait(eng, hq_cons.get(gidx - 2))
                if it >= 2 and fc in (0, 1):
                    ph.wait(eng, mm_t.get((it - 2, 3)))
                ev_l = ph.dma(eng, buf[:, :],
                              h_d[fc, :, it * 512:(it + 1) * 512])
                ph.wait("scalar", ev_l)
                if it >= 2 and fc == 0:
                    ph.wait("scalar", mm_t.get((it - 2, 3)))
                def cq8(e, fc=fc, buf=buf, hqb=hqb):
                    e.activation(h_u8[:, :], buf[:, :], AF.Identity,
                                 scale=rs2[:, 0:1])
                    return e.activation(hqb[:, fc * 512:(fc + 1) * 512],
                                        h_u8[:, :], AF.Identity)
                hq_cons[gidx] = ph.op("scalar", cq8, inc="s")
            hq_ready = k.evt("s")
            for qp in range(4):
                tt = it * 4 + qp
                par = tt % 2
                pp = pq if qp % 2 == 0 else pv2
                if tt >= 2:
                    ph.wait("sync", r1r_cons[tt - 2])
                ev_r = ph.dma("sync", r1res[par][:, :],
                              r1_d[tt * 128:(tt + 1) * 128, :])
                ph.wait("tensor", hq_ready)
                ph.wait("tensor", mm_v.get((it, qp - 2)) if qp >= 2
                        else mm_v.get((it - 1, qp + 2)))
                def mm2(e, qp=qp, pp=pp, hqb=hqb):
                    for n2 in range(2):
                        pslc = pp[:, n2 * 512:(n2 + 1) * 512]
                        e.matmul(pslc, ones_bf[0:1, 0:128],
                                 b2q_row[0:1, n2 * 512:(n2 + 1) * 512],
                                 start=True, stop=False)
                        for fc in range(FC):
                            ins = e.matmul(
                                pslc,
                                hqb[:, fc * 512 + qp * 128:fc * 512 + (qp + 1) * 128],
                                w2T[:, fc * 1024 + n2 * 512:fc * 1024 + (n2 + 1) * 512],
                                start=False, stop=(fc == FC - 1))
                    return ins
                ev_m = ph.op("tensor", mm2, inc="t")
                mm_t[(it, qp)] = ev_m
                flush2(ph)
                ph.wait("vector", ev_m)
                ph.wait("vector", ev_r)
                ph.wait("vector", yt_store.get(tt - 2))
                def ev2f(e, pp=pp, par=par):
                    e.scalar_tensor_tensor(t2[:, :], pp[:, :], 1.0, c2t[:, :],
                                           ALU.mult, ALU.mult)
                    return e.tensor_add(rpre2[par][:, :], t2[:, :],
                                        r1res[par][:, :])
                ev_rp = ph.op("vector", ev2f, inc="v")
                mm_v[(it, qp)] = ev_rp
                r1r_cons[tt] = ev_rp
                ev_var = ln_a(ph, par, rpre2[par][:, :], 1024, ev_rp)
                pend2 = (tt, par, ev_var)
        flush2(ph)
        ph.wait("gpsimd", k.evt("st"))
        ph.run(nc)
        d_es.close()
        c_es.close()

    return nc


_NC_CACHE = None


def kernel(L0_attn_x, L0_attn_Wq, L0_attn_bq, L0_attn_Wk, L0_attn_bk,
           L0_attn_Wv, L0_attn_bv, L0_attn_Wo, L0_attn_bo,
           L0_ffn_w1, L0_ffn_b1, L0_ffn_w2, L0_ffn_b2):
    global _NC_CACHE
    x = np.ascontiguousarray(np.asarray(L0_attn_x, dtype=np.float32))
    wq = np.ascontiguousarray(np.asarray(L0_attn_Wq, dtype=np.float32))
    wk = np.ascontiguousarray(np.asarray(L0_attn_Wk, dtype=np.float32))
    wv = np.ascontiguousarray(np.asarray(L0_attn_Wv, dtype=np.float32))
    wo = np.ascontiguousarray(np.asarray(L0_attn_Wo, dtype=np.float32))
    rows = np.concatenate([
        np.asarray(L0_attn_bq, dtype=np.float32).reshape(-1),
        np.asarray(L0_attn_bo, dtype=np.float32).reshape(-1),
        np.asarray(L0_attn_bk, dtype=np.float32).reshape(-1),
        np.asarray(L0_attn_bv, dtype=np.float32).reshape(-1)]).reshape(1, 2560)
    rows = np.ascontiguousarray(rows)
    w1 = np.ascontiguousarray(np.asarray(L0_ffn_w1, dtype=np.float32))
    w2 = np.ascontiguousarray(np.asarray(L0_ffn_w2, dtype=np.float32))
    b1f = np.ascontiguousarray(
        np.asarray(L0_ffn_b1, dtype=np.float32).reshape(FC, 128).T)
    b2f = np.ascontiguousarray(
        np.asarray(L0_ffn_b2, dtype=np.float32).reshape(DC, 128).T)

    if _NC_CACHE is None:
        _NC_CACHE = build_nc()
    nc = _NC_CACHE

    in_maps = []
    for c in range(8):
        in_maps.append({
            "x": np.ascontiguousarray(x[2 * c:2 * c + 2]),
            "wq": wq, "wk": wk, "wv": wv, "wo": wo, "rows": rows,
            "w1": w1, "w2": w2, "b1f": b1f, "b2f": b2f,
        })
    res = run_bass_kernel_spmd(nc, in_maps, core_ids=list(range(8)))
    out = np.concatenate([r["y"] for r in res.results], axis=0)
    return np.ascontiguousarray(out.astype(np.float32))


# version 19
# speedup vs baseline: 1.2257x; 1.1307x over previous
import numpy as np
from contextlib import ExitStack

import concourse.bass as bass
import concourse.mybir as mybir
from concourse.bass_utils import run_bass_kernel_spmd

F32 = mybir.dt.float32
BF16 = mybir.dt.bfloat16
U8 = mybir.dt.uint8
I8 = mybir.dt.int8
I32 = mybir.dt.int32
AF = mybir.ActivationFunctionType
ALU = mybir.AluOpType
AX = mybir.AxisListType.X

B = 2
T = 2048
C = 1024
NH = 16
DC = C // 128
NT = T // 128
F = 4096
FC = F // 128
EPS_Q = 1e-8
EPS_LN = 1e-5
NEG = -1e30

ENGS = ("sync", "tensor", "vector", "scalar", "gpsimd")


class KCtx:
    def __init__(self, nc, es):
        self.nc = nc
        self.sems = {}
        self.cnt = {}
        for name in ("ld", "st", "t", "v", "s", "g", "cc"):
            self.sems[name] = es.enter_context(nc.semaphore(name))
            self.cnt[name] = 0

    def inc(self, name, by=1):
        self.cnt[name] += by
        return (name, self.cnt[name])

    def evt(self, name):
        return (name, self.cnt[name])


class Ph:

    def __init__(self, k):
        self.k = k
        self.q = {e: [] for e in ENGS}
        self.marks = {e: {} for e in ENGS}

    def wait(self, eng, evt):
        if evt is None:
            return
        name, val = evt
        if val <= 0 or self.marks[eng].get(name, 0) >= val:
            return
        self.marks[eng][name] = val
        sem = self.k.sems[name]
        self.q[eng].append(lambda e, s=sem, v=val: e.wait_ge(s, v))

    def op(self, eng, fn, inc=None, incby=1):
        if inc is not None:
            sem = self.k.sems[inc]
            evt = self.k.inc(inc, incby)

            def g(e, fn=fn, sem=sem, incby=incby):
                fn(e).then_inc(sem, incby)

            self.q[eng].append(g)
            return evt
        self.q[eng].append(lambda e, fn=fn: fn(e))
        return None

    def dma(self, eng, out, in_):
        name = "ld" if eng == "sync" else "st"
        return self.op(eng, lambda e, o=out, i=in_: e.dma_start(out=o, in_=i),
                       inc=name, incby=16)

    def run(self, nc):
        with nc.Block() as block:
            for eng in ENGS:
                if self.q[eng]:
                    def body(e, lst=tuple(self.q[eng])):
                        for f in lst:
                            f(e)
                    getattr(block, eng)(body)


def build_nc():
    nc = bass.Bass("TRN2", target_bir_lowering=False, debug=False,
                   detect_race_conditions=False)

    x_d = nc.dram_tensor("x", [B, T, C], F32, kind="ExternalInput").ap()
    wq_d = nc.dram_tensor("wq", [C, C], F32, kind="ExternalInput").ap()
    wk_d = nc.dram_tensor("wk", [C, 256], F32, kind="ExternalInput").ap()
    wv_d = nc.dram_tensor("wv", [C, 256], F32, kind="ExternalInput").ap()
    wo_d = nc.dram_tensor("wo", [C, C], F32, kind="ExternalInput").ap()
    rows_d = nc.dram_tensor("rows", [1, 2560], F32, kind="ExternalInput").ap()
    w1_d = nc.dram_tensor("w1", [F, C], F32, kind="ExternalInput").ap()
    w2_d = nc.dram_tensor("w2", [C, F], F32, kind="ExternalInput").ap()
    b1f_d = nc.dram_tensor("b1f", [128, FC], F32, kind="ExternalInput").ap()
    b2f_d = nc.dram_tensor("b2f", [128, DC], F32, kind="ExternalInput").ap()
    y_d = nc.dram_tensor("y", [B, T, C], F32, kind="ExternalOutput").ap()

    r1_d = nc.dram_tensor("r1sc", [B * T, C], F32).ap()
    h_d = nc.dram_tensor("hsc", [FC, 128, B * T], F32).ap()
    m1_d = nc.dram_tensor("m1sc", [128], F32).ap()
    m2_d = nc.dram_tensor("m2sc", [128], F32).ap()
    s1_d = nc.dram_tensor("s1sc", [1], F32).ap()
    s2_d = nc.dram_tensor("s2sc", [1], F32).ap()
    cc1i_d = nc.dram_tensor("cc1i", [1], F32).ap()
    cc1o_d = nc.dram_tensor("cc1o", [8], F32).ap()
    cc2i_d = nc.dram_tensor("cc2i", [1], F32).ap()
    cc2o_d = nc.dram_tensor("cc2o", [8], F32).ap()
    c2r_d = nc.dram_tensor("c2r", [128, DC], F32).ap()
    b2qr_d = nc.dram_tensor("b2qr", [128, DC], F32).ap()
    c2row_d = nc.dram_tensor("c2row", [1024], F32).ap()
    b2row_d = nc.dram_tensor("b2row", [1024], F32).ap()

    es = ExitStack()
    sb = lambda n, s, d: es.enter_context(nc.sbuf_tensor(n, s, d))
    ps = lambda n, s, d: es.enter_context(nc.psum_tensor(n, s, d))
    k = KCtx(nc, es)

    pq = ps("pq", [128, 1024], F32)
    pv2 = ps("pv2", [128, 1024], F32)
    psc = [ps("psc0", [128, 512], F32), ps("psc1", [128, 512], F32)]
    ppt = [ps("ppt0", [128, 512], BF16), ps("ppt1", [128, 512], BF16)]
    SCORE = [psc[0][:, 0:384], psc[1][:, 0:384],
             pv2[:, 0:384], pv2[:, 512:896]]
    AV = [psc[0][:, 384:449], psc[1][:, 384:449],
          pv2[:, 384:449], pv2[:, 896:961]]

    idb = sb("idb", [128, 128], BF16)
    mask2 = sb("mask2", [128, 384], BF16)
    mask1 = sb("mask1", [128, 256], BF16)
    mask0 = sb("mask0", [128, 128], BF16)
    ones_bf = sb("ones_bf", [1, 512], BF16)
    rows_f = sb("rows_f", [1, 2560], F32)
    rows_b = sb("rows_b", [1, 2560], BF16)
    b2q_row = sb("b2q_row", [1, 1024], BF16)
    sq_scr = sb("sq_scr", [128, 1024], F32)
    c2t = sb("c2t", [128, 1024], F32)

    m1acc = sb("m1acc", [128, 1], F32)
    m2acc = sb("m2acc", [128, 1], F32)
    trow = sb("trow", [1, 128], F32)
    lmax = sb("lmax", [1, 1], F32)
    g8 = sb("g8", [1, 8], F32)
    gg = sb("gg", [1, 1], F32)
    s1bc = sb("s1bc", [128, 1], F32)
    rs1 = sb("rs1", [128, 1], F32)
    s2bc = sb("s2bc", [128, 1], F32)
    rs2 = sb("rs2", [128, 1], F32)
    sw1 = sb("sw1", [128, FC], F32)
    rsw1 = sb("rsw1", [128, FC], F32)
    c1 = sb("c1", [128, FC], F32)
    b1q = sb("b1q", [128, FC], F32)
    b1f_s = sb("b1f_s", [128, FC], F32)
    sw2 = sb("sw2", [128, DC], F32)
    rsw2 = sb("rsw2", [128, DC], F32)
    c2 = sb("c2", [128, DC], F32)
    b2qf = sb("b2qf", [128, DC], F32)
    b2f_s = sb("b2f_s", [128, DC], F32)
    tq32 = sb("tq32", [128, FC], I32)
    tqf = sb("tqf", [128, FC], F32)
    ln_sm = sb("ln_sm", [128, 2], F32)
    ln_ssq = sb("ln_ssq", [128, 2], F32)
    ln_mean = sb("ln_mean", [128, 2], F32)
    ln_var = sb("ln_var", [128, 2], F32)
    ln_lnv = sb("ln_lnv", [128, 2], F32)
    ln_rstd = sb("ln_rstd", [128, 2], F32)
    ln_t1 = sb("ln_t1", [128, 2], F32)
    ln_nmr = sb("ln_nmr", [128, 2], F32)

    def ln_a(ph, par, rpre_ap, n_free, sv_evt):
        p = slice(par, par + 1)
        ph.wait("vector", sv_evt)
        ev = ph.op("vector", lambda e: e.reduce_sum(
            ln_sm[:, p], rpre_ap, axis=AX), inc="v")
        ph.wait("vector", ev)
        ev = ph.op("vector", lambda e: e.tensor_scalar_mul(
            ln_mean[:, p], ln_sm[:, p], 1.0 / n_free), inc="v")
        ph.wait("scalar", sv_evt)
        ev_sq = ph.op("scalar", lambda e: e.activation(
            sq_scr[:, 0:n_free], rpre_ap, AF.Square, accum_out=ln_ssq[:, p]),
            inc="s")
        ph.wait("vector", ev_sq)
        ph.wait("vector", ev)
        ev = ph.op("vector", lambda e: e.tensor_scalar_mul(
            ln_ssq[:, p], ln_ssq[:, p], 1.0 / n_free), inc="v")
        evm = ph.op("vector", lambda e: e.tensor_mul(
            ln_var[:, p], ln_mean[:, p], ln_mean[:, p]), inc="v")
        ph.wait("vector", ev)
        ph.wait("vector", evm)
        ev = ph.op("vector", lambda e: e.tensor_sub(
            ln_var[:, p], ln_ssq[:, p], ln_var[:, p]), inc="v")
        ph.wait("vector", ev)
        ev_var = ph.op("vector", lambda e: e.tensor_scalar_add(
            ln_var[:, p], ln_var[:, p], EPS_LN), inc="v")
        return ev_var

    def ln_b(ph, par, ev_var):
        p = slice(par, par + 1)
        ph.wait("scalar", ev_var)
        ev = ph.op("scalar", lambda e: e.activation(
            ln_lnv[:, p], ln_var[:, p], AF.Ln), inc="s")
        ph.wait("scalar", ev)
        return ph.op("scalar", lambda e: e.activation(
            ln_rstd[:, p], ln_lnv[:, p], AF.Exp, scale=-0.5), inc="s")

    def ln_c(ph, par, rpre_ap, out_ap, ev_rstd):
        p = slice(par, par + 1)
        ph.wait("vector", ev_rstd)
        ev = ph.op("vector", lambda e: e.tensor_mul(
            ln_t1[:, p], ln_rstd[:, p], ln_rstd[:, p]), inc="v")
        ph.wait("vector", ev)
        ev = ph.op("vector", lambda e: e.tensor_mul(
            ln_t1[:, p], ln_t1[:, p], ln_var[:, p]), inc="v")
        ph.wait("vector", ev)
        ev = ph.op("vector", lambda e: e.tensor_scalar(
            ln_t1[:, p], ln_t1[:, p], -0.5, 1.5, ALU.mult, ALU.add), inc="v")
        ph.wait("vector", ev)
        ev = ph.op("vector", lambda e: e.tensor_mul(
            ln_rstd[:, p], ln_rstd[:, p], ln_t1[:, p]), inc="v")
        ph.wait("vector", ev)
        ev = ph.op("vector", lambda e: e.scalar_tensor_tensor(
            ln_nmr[:, p], ln_mean[:, p], -1.0, ln_rstd[:, p],
            ALU.mult, ALU.mult), inc="v")
        ph.wait("vector", ev)
        return ph.op("vector", lambda e: e.tensor_scalar(
            out_ap, rpre_ap, ln_rstd[:, par:par + 1], ln_nmr[:, par:par + 1],
            ALU.mult, ALU.add), inc="v")

    def gather_scale(ph, macc, md, cci, cco, sd, sbc, rsb):
        ev1 = ph.dma("gpsimd", md[:], macc[:, 0:1])
        ph.wait("gpsimd", ev1)
        ev2 = ph.dma("gpsimd", trow[0:1, :], md[None, :])
        ph.wait("vector", ev2)
        ev_lm = ph.op("vector", lambda e: e.reduce_max(
            lmax[:, :], trow[:, :], axis=AX), inc="v")
        ph.wait("gpsimd", ev_lm)
        ev3 = ph.dma("gpsimd", cci[:], lmax[0:1, 0:1])
        ph.wait("gpsimd", ev3)
        ev_cc = ph.op("gpsimd", lambda e: e.collective_compute(
            "AllGather", ALU.bypass, replica_groups=[list(range(8))],
            ins=[cci[:].opt()], outs=[cco[:].opt()]), inc="cc")
        ph.wait("gpsimd", ev_cc)
        ev4 = ph.dma("gpsimd", g8[0:1, :], cco[None, :])
        ph.wait("vector", ev4)
        evr = ph.op("vector", lambda e: e.reduce_max(gg[:, :], g8[:, :], axis=AX),
                    inc="v")
        ph.wait("vector", evr)
        ev_s = ph.op("vector", lambda e: e.tensor_scalar(
            gg[:, :], gg[:, :], EPS_Q, 1.0 / 255.0, ALU.max, ALU.mult), inc="v")
        ph.wait("gpsimd", ev_s)
        ev5 = ph.dma("gpsimd", sd[:], gg[0:1, 0:1])
        ph.wait("gpsimd", ev5)
        ev6 = ph.dma("gpsimd", sbc[:, :], sd[None, :].partition_broadcast(128))
        ph.wait("vector", ev6)
        return ph.op("vector", lambda e: e.reciprocal(rsb[:, :], sbc[:, :]),
                     inc="v")

    with es:
        a_es = ExitStack()
        sba = lambda n, s, d: a_es.enter_context(nc.sbuf_tensor(n, s, d))
        x_fm = sba("x_fm", [128, DC * T], BF16)
        q_fm = sba("q_fm", [128, DC * T], BF16)
        k2 = sba("k2", [128, 4 * T], BF16)
        v_tm = sba("v_tm", [128, NT * 260], BF16)
        wqb = sba("wqb", [128, DC * 1024], BF16)
        wkb = sba("wkb", [128, DC * 256], BF16)
        wvb = sba("wvb", [128, DC * 256], BF16)
        wob = sba("wob", [128, DC * 1024], BF16)
        xld = [sba("xld0", [128, 1024], F32), sba("xld1", [128, 1024], F32)]
        xbf = sba("xbf", [128, 1024], BF16)
        P = [sba("P%d" % i, [128, 384], BF16) for i in range(4)]
        PT = [sba("PT%d" % i, [128, 384], BF16) for i in range(4)]
        negm = sba("negm", [128, 4], F32)
        rl = sba("rl", [128, 4], F32)
        y_tm = sba("y_tm", [128, 1024], BF16)
        y_fm = sba("y_fm", [128, 1024], BF16)
        xres = [sba("xres0", [128, 1024], F32), sba("xres1", [128, 1024], F32)]
        rpre = [sba("rpre0", [128, 1024], F32), sba("rpre1", [128, 1024], F32)]
        r1t = [sba("r1t0", [128, 1024], F32), sba("r1t1", [128, 1024], F32)]
        tmx = sba("tmx", [128, 1], F32)
        wst = xld

        ph = Ph(k)

        def g_init(e):
            e.memset(idb[:, :], 1.0)
            e.affine_select(out=idb[:, :], in_=idb[:, :], pattern=[[-1, 128]],
                            compare_op=ALU.is_equal, fill=0.0, base=0,
                            channel_multiplier=1)
            e.memset(mask2[:, :], 0.0)
            e.affine_select(out=mask2[:, :], in_=mask2[:, :], pattern=[[1, 384]],
                            compare_op=ALU.is_ge, fill=NEG, base=0,
                            channel_multiplier=-1)
            e.affine_select(out=mask2[:, :], in_=mask2[:, :], pattern=[[-1, 384]],
                            compare_op=ALU.is_ge, fill=NEG, base=256,
                            channel_multiplier=1)
            e.memset(mask1[:, :], 0.0)
            e.affine_select(out=mask1[:, :], in_=mask1[:, :], pattern=[[-1, 256]],
                            compare_op=ALU.is_ge, fill=NEG, base=128,
                            channel_multiplier=1)
            e.memset(mask0[:, :], 0.0)
            e.affine_select(out=mask0[:, :], in_=mask0[:, :], pattern=[[-1, 128]],
                            compare_op=ALU.is_ge, fill=NEG, base=0,
                            channel_multiplier=1)
            e.memset(ones_bf[:, :], 1.0)
            e.memset(m1acc[:, :], NEG)
            e.memset(m2acc[:, :], 0.0)
            vap = v_tm[:, :].rearrange("p (t g c) -> p t g c", t=NT, g=4)
            e.memset(vap[:, :, :, 64:65], 1.0)
            return e.engine_nop()
        ev_g = ph.op("gpsimd", g_init, inc="g")

        ev_rows = ph.dma("sync", rows_f[:, :], rows_d[:, :])
        ph.wait("vector", ev_rows)
        ev_rowsb = ph.op("vector", lambda e: e.tensor_copy(
            rows_b[:, :], rows_f[:, :]), inc="v")

        wsrc = [(wq_d, wqb, 1024), (wo_d, wob, 1024), (wk_d, wkb, 256),
                (wv_d, wvb, 256)]
        cast_evts = []
        i = 0
        for (src, dst, w) in wsrc:
            for dc in range(DC):
                buf = wst[i % 2]
                if i >= 2:
                    ph.wait("sync", cast_evts[i - 2])
                ev_l = ph.dma("sync", buf[:, 0:w], src[dc * 128:(dc + 1) * 128, :])
                ph.wait("vector", ev_l)
                ev_c = ph.op("vector", lambda e, d=dst, dc=dc, b=buf, w=w:
                             e.tensor_copy(d[:, dc * w:(dc + 1) * w], b[:, 0:w]),
                             inc="v")
                cast_evts.append(ev_c)
                i += 1
        ph.run(nc)

        ph = Ph(k)
        for eng in ENGS:
            ph.wait(eng, ev_g)
            ph.wait(eng, ev_rowsb)
            ph.wait(eng, cast_evts[-1])

        pt_free = {0: None, 1: None}
        r1_store = {}
        r1_mx = {}
        sl_exp = {i: None for i in range(4)}
        sl_nm = {i: None for i in range(4)}
        sl_ye = {i: None for i in range(4)}
        sl_pte = {i: None for i in range(4)}
        outproj_vread = None
        pend_stat = None
        pend_bc = None
        pending_tail = None
        ev_kd = None

        def flush_ln_c(ph):
            nonlocal pend_bc
            if pend_bc is None:
                return
            (gq, par, ev_var), ev_rstd = pend_bc
            pend_bc = None
            rb = gq % 2
            ph.wait("vector", r1_store.get(gq - 2))
            ph.wait("vector", r1_mx.get(gq - 2))
            ev_r1 = ln_c(ph, par, rpre[par][:, :], r1t[rb][:, :], ev_rstd)
            ph.wait("vector", ev_r1)
            ev_tm = ph.op("vector", lambda e, rb=rb: e.reduce_max(
                tmx[:, :], r1t[rb][:, :], axis=AX), inc="v")
            ph.wait("vector", ev_tm)
            r1_mx[gq] = ph.op("vector", lambda e: e.tensor_max(
                m1acc[:, :], m1acc[:, :], tmx[:, :]), inc="v")
            ph.wait("gpsimd", ev_r1)
            r1_store[gq] = ph.dma("gpsimd", r1_d[gq * 128:(gq + 1) * 128, :],
                                  r1t[rb][:, :])

        def flush_ln_b(ph):
            nonlocal pend_stat, pend_bc
            if pend_stat is None:
                return
            pend_bc = (pend_stat, ln_b(ph, pend_stat[1], pend_stat[2]))
            pend_stat = None

        for b in range(B):
            xfm_ready = None
            xbf_cons = None
            xld_cons = {}
            for tt in range(NT):
                buf = xld[tt % 2]
                if tt >= 2:
                    ph.wait("sync", xld_cons[tt - 2])
                ev_l = ph.dma("sync", buf[:, :],
                              x_d[b, tt * 128:(tt + 1) * 128, :])
                ph.wait("vector", ev_l)
                if xbf_cons is not None:
                    ph.wait("vector", xbf_cons)
                ev_cast = ph.op("vector", lambda e, buf=buf: e.tensor_copy(
                    xbf[:, :], buf[:, :]), inc="v")
                xld_cons[tt] = ev_cast
                for half in range(2):
                    bank = ppt[half]
                    ph.wait("tensor", ev_cast)
                    ph.wait("tensor", pt_free[half])
                    def tfn(e, half=half, bank=bank):
                        for j in range(4):
                            dc = half * 4 + j
                            ins = e.transpose(bank[:, j * 128:(j + 1) * 128],
                                              xbf[:, dc * 128:(dc + 1) * 128],
                                              idb[:, :])
                        return ins
                    ev_t = ph.op("tensor", tfn, inc="t")
                    ph.wait("vector", ev_t)
                    dst = x_fm[:, :].rearrange("p (d t) -> p d t", d=DC)[
                        :, half * 4:(half + 1) * 4, tt * 128:(tt + 1) * 128]
                    ev_e = ph.op("vector", lambda e, bank=bank, dst=dst:
                                 e.tensor_copy(dst, bank[:, :].rearrange(
                                     "p (d t) -> p d t", d=4)), inc="v")
                    pt_free[half] = ev_e
                    xfm_ready = ev_e
                xbf_cons = ev_t
            kv_ready = None
            q_ready = None
            for tw in range(4):
                for mc in range(2):
                    pslc = pq[:, mc * 512:(mc + 1) * 512]
                    ph.wait("tensor", xfm_ready)
                    ph.wait("tensor", outproj_vread)
                    if kv_ready is not None:
                        ph.wait("tensor", kv_ready)
                    def kfn(e, tw=tw, mc=mc, pslc=pslc):
                        e.matmul(pslc,
                                 rows_b[0:1, 2048 + mc * 128:2048 + (mc + 1) * 128],
                                 ones_bf[0:1, 0:512], start=True, stop=False)
                        for dc in range(DC):
                            ins = e.matmul(
                                pslc,
                                wkb[:, dc * 256 + mc * 128:dc * 256 + (mc + 1) * 128],
                                x_fm[:, dc * T + tw * 512:dc * T + (tw + 1) * 512],
                                start=False, stop=(dc == DC - 1))
                        return ins
                    ev_k = ph.op("tensor", kfn, inc="t")
                    ph.wait("vector", ev_k)
                    ph.wait("vector", ev_kd)
                    g0, g1 = 2 * mc, 2 * mc + 1
                    def kev(e, pslc=pslc, g0=g0, g1=g1, tw=tw):
                        e.tensor_copy(
                            k2[0:64, g0 * T + tw * 512:g0 * T + (tw + 1) * 512],
                            pslc[0:64, :])
                        return e.tensor_copy(
                            k2[64:128, g1 * T + tw * 512:g1 * T + (tw + 1) * 512],
                            pslc[64:128, :])
                    kv_ready = ph.op("vector", kev, inc="v")
                for mc in range(DC):
                    pslc = pq[:, (mc % 2) * 512:((mc % 2) + 1) * 512]
                    ph.wait("tensor", kv_ready if mc < 2 else q_ready)
                    def qfn(e, tw=tw, mc=mc, pslc=pslc):
                        e.matmul(pslc, rows_b[0:1, mc * 128:(mc + 1) * 128],
                                 ones_bf[0:1, 0:512], start=True, stop=False)
                        for dc in range(DC):
                            ins = e.matmul(
                                pslc,
                                wqb[:, dc * 1024 + mc * 128:dc * 1024 + (mc + 1) * 128],
                                x_fm[:, dc * T + tw * 512:dc * T + (tw + 1) * 512],
                                start=False, stop=(dc == DC - 1))
                        return ins
                    ev_q = ph.op("tensor", qfn, inc="t")
                    ph.wait("scalar", ev_q)
                    q_ready = ph.op("scalar", lambda e, tw=tw, mc=mc, pslc=pslc:
                                    e.activation(
                                        q_fm[:, mc * T + tw * 512:mc * T + (tw + 1) * 512],
                                        pslc, AF.Identity, scale=0.125), inc="s")
            ph.wait("gpsimd", kv_ready)
            ph.dma("gpsimd", k2[64:128, 0:T], k2[0:64, 0:T])
            ph.dma("gpsimd", k2[64:128, 2 * T:3 * T], k2[0:64, 2 * T:3 * T])
            ph.dma("gpsimd", k2[0:64, T:2 * T], k2[64:128, T:2 * T])
            ev_kd = ph.dma("gpsimd", k2[0:64, 3 * T:4 * T],
                           k2[64:128, 3 * T:4 * T])
            v_ready = None
            for tt in range(NT):
                pslc = pv2[:, 0:256]
                ph.wait("tensor", xfm_ready)
                ph.wait("tensor", sl_exp[2])
                ph.wait("tensor", sl_exp[3])
                ph.wait("tensor", sl_ye[2])
                ph.wait("tensor", sl_ye[3])
                if v_ready is not None:
                    ph.wait("tensor", v_ready)
                def vfn(e, tt=tt, pslc=pslc):
                    e.matmul(pslc, ones_bf[0:1, 0:128],
                             rows_b[0:1, 2304:2560], start=True, stop=False)
                    for dc in range(DC):
                        ins = e.matmul(
                            pslc, x_fm[:, dc * T + tt * 128:dc * T + (tt + 1) * 128],
                            wvb[:, dc * 256:(dc + 1) * 256],
                            start=False, stop=(dc == DC - 1))
                    return ins
                ev_v = ph.op("tensor", vfn, inc="t")
                ph.wait("vector", ev_v)
                def vev(e, tt=tt, pslc=pslc):
                    for g in range(4):
                        ins = e.tensor_copy(
                            v_tm[:, tt * 260 + g * 65:tt * 260 + g * 65 + 64],
                            pslc[:, g * 64:(g + 1) * 64])
                    return ins
                v_ready = ph.op("vector", vev, inc="v")

            xres_cons = {}
            for qt in range(NT):
                kts = min(qt + 1, 3)
                klen = kts * 128
                k0 = (qt - kts + 1) * 128
                mask = {1: mask0, 2: mask1, 3: mask2}[kts]
                gq = b * NT + qt
                par = gq % 2
                if qt >= 2:
                    ph.wait("sync", xres_cons.get(qt - 2))
                ev_xr = ph.dma("sync", xres[qt % 2][:, :],
                               x_d[b, qt * 128:(qt + 1) * 128, :])

                for step in range(NH + 1):
                    if step < NH:
                        h = step
                        sl = h % 4
                        g = h // 4
                        qoff = (h % 2) * 64
                        ph.wait("tensor", q_ready)
                        ph.wait("tensor", kv_ready)
                        ph.wait("tensor", ev_kd)
                        ph.wait("tensor", v_ready)
                        ph.wait("tensor", sl_exp[sl])
                        ph.wait("tensor", sl_nm[sl])
                        def qkfn(e, sl=sl, g=g, qoff=qoff, h=h, qt=qt, k0=k0,
                                 klen=klen, mask=mask):
                            e.matmul(SCORE[sl][:, 0:klen],
                                     q_fm[qoff:qoff + 64,
                                          (h // 2) * T + qt * 128:(h // 2) * T + (qt + 1) * 128],
                                     k2[qoff:qoff + 64, g * T + k0:g * T + k0 + klen],
                                     start=True, stop=False)
                            return e.matmul(SCORE[sl][:, 0:klen], idb[:, :],
                                            mask[:, 0:klen], start=False,
                                            stop=True)
                        ev_qk = ph.op("tensor", qkfn, inc="t")
                        ph.wait("vector", ev_qk)
                        ev_nm = ph.op("vector", lambda e, sl=sl, klen=klen:
                                      e.reduce_max(negm[:, sl:sl + 1],
                                                   SCORE[sl][:, 0:klen],
                                                   axis=AX, negate=True),
                                      inc="v")
                        sl_nm[sl] = ev_nm
                        ph.wait("scalar", ev_nm)
                        ev_exp = ph.op("scalar", lambda e, sl=sl, klen=klen:
                                       e.activation(P[sl][:, 0:klen],
                                                    SCORE[sl][:, 0:klen],
                                                    AF.Exp,
                                                    bias=negm[:, sl:sl + 1]),
                                       inc="s")
                        sl_exp[sl] = ev_exp
                        if step == 3:
                            flush_ln_b(ph)
                        if step == 8 and pending_tail is not None:
                            pending_tail(ph)
                            pending_tail = None
                    if step >= 1:
                        h2 = step - 1
                        sl2 = h2 % 4
                        g2 = h2 // 4
                        ph.wait("tensor", sl_exp[sl2])
                        ph.wait("tensor", pt_free[sl2 % 2])
                        def trfn(e, sl2=sl2, kts=kts):
                            for kt in range(kts):
                                ins = e.transpose(
                                    ppt[sl2 % 2][:, kt * 128:(kt + 1) * 128],
                                    P[sl2][:, kt * 128:(kt + 1) * 128],
                                    idb[:, :])
                            return ins
                        ev_tr = ph.op("tensor", trfn, inc="t")
                        ph.wait("vector", ev_tr)
                        ev_pte = ph.op("vector", lambda e, sl2=sl2, klen=klen:
                                       e.tensor_copy(PT[sl2][:, 0:klen],
                                                     ppt[sl2 % 2][:, 0:klen]),
                                       inc="v")
                        pt_free[sl2 % 2] = ev_pte
                        ph.wait("tensor", ev_pte)
                        ph.wait("tensor", sl_ye[sl2])
                        def avfn(e, sl2=sl2, kts=kts, k0=k0, g2=g2):
                            for kt in range(kts):
                                ktok = k0 // 128 + kt
                                ins = e.matmul(
                                    AV[sl2],
                                    PT[sl2][:, kt * 128:(kt + 1) * 128],
                                    v_tm[:, ktok * 260 + g2 * 65:ktok * 260 + (g2 + 1) * 65],
                                    start=(kt == 0), stop=(kt == kts - 1))
                            return ins
                        ev_av = ph.op("tensor", avfn, inc="t")
                        ph.wait("vector", ev_av)
                        ev_rl = ph.op("vector", lambda e, sl2=sl2: e.reciprocal(
                            rl[:, sl2:sl2 + 1], AV[sl2][:, 64:65]), inc="v")
                        ph.wait("vector", ev_rl)
                        ev_ye = ph.op("vector", lambda e, sl2=sl2, h2=h2:
                                      e.tensor_scalar_mul(
                                          y_tm[:, h2 * 64:(h2 + 1) * 64],
                                          AV[sl2][:, 0:64], rl[:, sl2:sl2 + 1]),
                                      inc="v")
                        sl_ye[sl2] = ev_ye
                        yev_last = ev_ye
                flush_ln_c(ph)

                for half in range(2):
                    bank = ppt[half]
                    ph.wait("tensor", yev_last)
                    ph.wait("tensor", pt_free[half])
                    def ytr(e, half=half, bank=bank):
                        for j in range(4):
                            c = half * 4 + j
                            ins = e.transpose(bank[:, j * 128:(j + 1) * 128],
                                              y_tm[:, c * 128:(c + 1) * 128],
                                              idb[:, :])
                        return ins
                    ev_yt = ph.op("tensor", ytr, inc="t")
                    ph.wait("vector", ev_yt)
                    ev_yfe = ph.op("vector", lambda e, half=half, bank=bank:
                                   e.tensor_copy(
                                       y_fm[:, half * 512:(half + 1) * 512],
                                       bank[:, :]), inc="v")
                    pt_free[half] = ev_yfe
                    yfm_ready = ev_yfe
                ph.wait("tensor", yfm_ready)
                ph.wait("tensor", outproj_vread)
                def ofn(e):
                    for n2 in range(2):
                        pslc = pq[:, n2 * 512:(n2 + 1) * 512]
                        e.matmul(pslc, ones_bf[0:1, 0:128],
                                 rows_b[0:1, 1024 + n2 * 512:1024 + (n2 + 1) * 512],
                                 start=True, stop=False)
                        for dc in range(DC):
                            ins = e.matmul(
                                pslc, y_fm[:, dc * 128:(dc + 1) * 128],
                                wob[:, dc * 1024 + n2 * 512:dc * 1024 + (n2 + 1) * 512],
                                start=False, stop=(dc == DC - 1))
                    return ins
                ev_o = ph.op("tensor", ofn, inc="t")
                def tail(ph, gq=gq, par=par, qt=qt, ev_o=ev_o, ev_xr=ev_xr):
                    nonlocal outproj_vread, pend_stat
                    ph.wait("vector", ev_o)
                    ph.wait("vector", ev_xr)
                    ph.wait("vector", r1_store.get(gq - 2))
                    ev_rp = ph.op("vector", lambda e: e.tensor_add(
                        rpre[par][:, :], pq[:, :], xres[qt % 2][:, :]),
                        inc="v")
                    xres_cons[qt] = ev_rp
                    outproj_vread = ev_rp
                    pend_stat = (gq, par,
                                 ln_a(ph, par, rpre[par][:, :], 1024, ev_rp))
                pending_tail = tail
            flush_ln_b(ph)
            flush_ln_c(ph)
            if pending_tail is not None:
                pending_tail(ph)
                pending_tail = None
            flush_ln_b(ph)
            flush_ln_c(ph)

        ph.wait("gpsimd", k.evt("v"))
        ph.wait("gpsimd", r1_store[B * NT - 1])
        ev_rs1 = gather_scale(ph, m1acc, m1_d, cc1i_d, cc1o_d, s1_d, s1bc, rs1)
        ph.run(nc)
        a_es.close()

        b_es = ExitStack()
        sbb = lambda n, s, d: b_es.enter_context(nc.sbuf_tensor(n, s, d))
        w1T = sbb("w1T", [128, DC * F], BF16)
        w1st = [sbb("w1st0", [128, 1024], F32), sbb("w1st1", [128, 1024], F32)]
        w1i8 = sbb("w1i8", [128, 1024], I8)
        w1cb = sbb("w1cb", [128, 1024], BF16)
        r1ld = [sbb("r1ld0", [128, 4096], F32), sbb("r1ld1", [128, 4096], F32)]
        xq_u8 = sbb("xq_u8", [128, 4096], U8)
        xq_bf = sbb("xq_bf", [128, 4096], BF16)
        xq_fm = sbb("xq_fm", [128, DC * 512], BF16)
        h_t = [sbb("h_t0", [128, 512], F32), sbb("h_t1", [128, 512], F32)]
        hmx = sbb("hmx", [128, 1], F32)
        amax_t = sbb("amax_t", [128, 1], F32)

        ph = Ph(k)
        pt_free = {0: None, 1: None}
        w1_cons = {}
        w1_amax = {}
        w1_trd = {}
        w1T_ready = None
        for fc in range(FC):
            buf = w1st[fc % 2]
            if fc >= 2:
                ph.wait("sync", w1_cons[fc - 2])
                ph.wait("sync", w1_amax[fc - 2])
            ev_l = ph.dma("sync", buf[:, :], w1_d[fc * 128:(fc + 1) * 128, :])
            ph.wait("vector", ev_l)
            ev_a = ph.op("vector", lambda e, buf=buf: e.reduce_max(
                amax_t[:, :], buf[:, :], axis=AX,
                apply_absolute_value=True), inc="v")
            ph.wait("vector", ev_a)
            ev_a = ph.op("vector", lambda e, fc=fc: e.tensor_scalar(
                sw1[:, fc:fc + 1], amax_t[:, :], EPS_Q,
                1.0 / 127.0, ALU.max, ALU.mult), inc="v")
            ph.wait("vector", ev_a)
            ev_a = ph.op("vector", lambda e, fc=fc: e.reciprocal(
                rsw1[:, fc:fc + 1], sw1[:, fc:fc + 1]), inc="v")
            w1_amax[fc] = ev_a
            ph.wait("scalar", ev_a)
            ph.wait("scalar", w1_trd.get(fc - 1))
            def w1qz(e, fc=fc, buf=buf):
                e.activation(w1i8[:, :], buf[:, :], AF.Identity,
                             scale=rsw1[:, fc:fc + 1])
                return e.activation(w1cb[:, :], w1i8[:, :], AF.Identity)
            ev_qz = ph.op("scalar", w1qz, inc="s")
            w1_cons[fc] = ev_qz
            for half in range(2):
                bank = ppt[half]
                ph.wait("tensor", ev_qz)
                ph.wait("tensor", pt_free[half])
                def w1tr(e, half=half, bank=bank):
                    for j in range(4):
                        dc = half * 4 + j
                        ins = e.transpose(bank[:, j * 128:(j + 1) * 128],
                                          w1cb[:, dc * 128:(dc + 1) * 128],
                                          idb[:, :])
                    return ins
                ev_t = ph.op("tensor", w1tr, inc="t")
                w1_trd[fc] = ev_t
                ph.wait("vector", ev_t)
                dst = w1T[:, :].rearrange("p (d f) -> p d f", d=DC)[
                    :, half * 4:(half + 1) * 4, fc * 128:(fc + 1) * 128]
                ev_e = ph.op("vector", lambda e, bank=bank, dst=dst:
                             e.tensor_copy(dst, bank[:, :].rearrange(
                                 "p (d t) -> p d t", d=4)), inc="v")
                pt_free[half] = ev_e
                w1T_ready = ev_e

        ev_b1 = ph.dma("sync", b1f_s[:, :], b1f_d[:, :])
        ph.wait("vector", ev_rs1)
        ph.wait("vector", ev_b1)
        ph.wait("vector", w1_amax[FC - 1])
        ev_c1 = ph.op("vector", lambda e: e.tensor_scalar_mul(
            c1[:, :], sw1[:, :], s1bc[:, 0:1]), inc="v")
        ph.wait("vector", ev_c1)
        ev_c1 = ph.op("vector", lambda e: e.reciprocal(
            tqf[:, :], c1[:, :]), inc="v")
        ph.wait("vector", ev_c1)
        ev_c1 = ph.op("vector", lambda e: e.tensor_mul(
            tqf[:, :], b1f_s[:, :], tqf[:, :]), inc="v")
        ph.wait("scalar", ev_c1)
        ev_b1r = ph.op("scalar", lambda e: e.activation(
            tq32[:, :], tqf[:, :], AF.Identity), inc="s")
        ph.wait("scalar", ev_b1r)
        ev_b1r = ph.op("scalar", lambda e: e.activation(
            tqf[:, :], tq32[:, :], AF.Identity), inc="s")
        ph.wait("vector", ev_b1r)
        ev_b1q = ph.op("vector", lambda e: e.tensor_mul(
            b1q[:, :], tqf[:, :], c1[:, :]), inc="v")

        ht_hm = {}
        ht_ev = {}
        ht_st = {}
        r1ld_cons = {}
        xq_trdone = None
        for it in range(8):
            buf = r1ld[it % 2]
            if it >= 2:
                ph.wait("sync", r1ld_cons[it - 2])
            src = r1_d[it * 512:(it + 1) * 512, :].rearrange(
                "(q p) c -> p q c", p=128)
            ev_l = ph.dma("sync", buf[:, :].rearrange("p (q c) -> p q c", q=4),
                          src)
            ph.wait("scalar", ev_l)
            ph.wait("scalar", ev_rs1)
            ph.wait("scalar", xq_trdone)
            def bq8(e, buf=buf):
                e.activation(xq_u8[:, :], buf[:, :], AF.Identity,
                             scale=rs1[:, 0:1])
                return e.activation(xq_bf[:, :], xq_u8[:, :], AF.Identity)
            ev_q8 = ph.op("scalar", bq8, inc="s")
            r1ld_cons[it] = ev_q8
            xq_ready = None
            for dc in range(DC):
                bank = ppt[dc % 2]
                ph.wait("tensor", ev_q8)
                ph.wait("tensor", pt_free[dc % 2])
                def xtr(e, dc=dc, bank=bank):
                    for j in range(4):
                        ins = e.transpose(
                            bank[:, j * 128:(j + 1) * 128],
                            xq_bf[:, j * 1024 + dc * 128:j * 1024 + (dc + 1) * 128],
                            idb[:, :])
                    return ins
                ev_t = ph.op("tensor", xtr, inc="t")
                ph.wait("vector", ev_t)
                ev_e = ph.op("vector", lambda e, dc=dc, bank=bank: e.tensor_copy(
                    xq_fm[:, dc * 512:(dc + 1) * 512], bank[:, :]), inc="v")
                pt_free[dc % 2] = ev_e
                xq_ready = ev_e
            xq_trdone = k.evt("t")
            for fc in range(FC):
                gidx = it * FC + fc
                pslc = pq[:, (fc % 2) * 512:((fc % 2) + 1) * 512]
                ph.wait("tensor", xq_ready)
                ph.wait("tensor", w1T_ready)
                ph.wait("tensor", ht_ev.get(gidx - 2))
                def mm1(e, fc=fc, pslc=pslc):
                    for dc in range(DC):
                        ins = e.matmul(
                            pslc, w1T[:, dc * F + fc * 128:dc * F + (fc + 1) * 128],
                            xq_fm[:, dc * 512:(dc + 1) * 512],
                            start=(dc == 0), stop=(dc == DC - 1))
                    return ins
                ev_m = ph.op("tensor", mm1, inc="t")
                hb = gidx % 2
                ph.wait("scalar", ev_m)
                ph.wait("scalar", ev_b1q)
                ph.wait("scalar", ht_hm.get(gidx - 2))
                ph.wait("scalar", ht_st.get(gidx - 2))
                ev_h = ph.op("scalar", lambda e, fc=fc, hb=hb, pslc=pslc:
                             e.activation(h_t[hb][:, :], pslc, AF.Relu,
                                          scale=c1[:, fc:fc + 1],
                                          bias=b1q[:, fc:fc + 1]), inc="s")
                ht_ev[gidx] = ev_h
                ph.wait("vector", ev_h)
                ev_hx = ph.op("vector", lambda e, hb=hb: e.reduce_max(
                    hmx[:, :], h_t[hb][:, :], axis=AX), inc="v")
                ph.wait("vector", ev_hx)
                ht_hm[gidx] = ph.op("vector", lambda e: e.tensor_max(
                    m2acc[:, :], m2acc[:, :], hmx[:, :]), inc="v")
                ph.wait("gpsimd", ev_h)
                ht_st[gidx] = ph.dma("gpsimd",
                                     h_d[fc, :, it * 512:(it + 1) * 512],
                                     h_t[hb][:, :])

        ph.wait("gpsimd", k.evt("v"))
        ph.wait("gpsimd", ht_st[8 * FC - 1])
        ev_rs2 = gather_scale(ph, m2acc, m2_d, cc2i_d, cc2o_d, s2_d, s2bc, rs2)
        ph.run(nc)
        b_es.close()

        c_es = ExitStack()
        sbc = lambda n, s, d: c_es.enter_context(nc.sbuf_tensor(n, s, d))
        w2T = sbc("w2T", [128, FC * 1024], BF16)
        t_es = ExitStack()
        sbt = lambda n, s, d: t_es.enter_context(nc.sbuf_tensor(n, s, d))
        w2st = [sbt("w2st0", [128, 4096], F32), sbt("w2st1", [128, 4096], F32)]
        w2i8 = sbt("w2i8", [128, 4096], I8)
        w2cb = sbt("w2cb", [128, 4096], BF16)
        amax2 = sbt("amax2", [128, 1], F32)

        ph = Ph(k)
        pt_free = {0: None, 1: None}
        w2_cons = {}
        w2_amax = {}
        w2_trd = {}
        w2T_ready = None
        for dc in range(DC):
            buf = w2st[dc % 2]
            if dc >= 2:
                ph.wait("sync", w2_cons[dc - 2])
                ph.wait("sync", w2_amax[dc - 2])
            ev_l = ph.dma("sync", buf[:, :], w2_d[dc * 128:(dc + 1) * 128, :])
            ph.wait("vector", ev_l)
            ev_a = ph.op("vector", lambda e, buf=buf: e.reduce_max(
                amax2[:, :], buf[:, :], axis=AX,
                apply_absolute_value=True), inc="v")
            ph.wait("vector", ev_a)
            ev_a = ph.op("vector", lambda e, dc=dc: e.tensor_scalar(
                sw2[:, dc:dc + 1], amax2[:, :], EPS_Q,
                1.0 / 127.0, ALU.max, ALU.mult), inc="v")
            ph.wait("vector", ev_a)
            ev_a = ph.op("vector", lambda e, dc=dc: e.reciprocal(
                rsw2[:, dc:dc + 1], sw2[:, dc:dc + 1]), inc="v")
            w2_amax[dc] = ev_a
            ph.wait("scalar", ev_a)
            ph.wait("scalar", w2_trd.get(dc - 1))
            def w2qz(e, dc=dc, buf=buf):
                e.activation(w2i8[:, :], buf[:, :], AF.Identity,
                             scale=rsw2[:, dc:dc + 1])
                return e.activation(w2cb[:, :], w2i8[:, :], AF.Identity)
            ev_qz = ph.op("scalar", w2qz, inc="s")
            w2_cons[dc] = ev_qz
            for grp in range(DC):
                bank = ppt[grp % 2]
                ph.wait("tensor", ev_qz)
                ph.wait("tensor", pt_free[grp % 2])
                def w2tr(e, grp=grp, bank=bank):
                    for j in range(4):
                        fc = grp * 4 + j
                        ins = e.transpose(bank[:, j * 128:(j + 1) * 128],
                                          w2cb[:, fc * 128:(fc + 1) * 128],
                                          idb[:, :])
                    return ins
                ev_t = ph.op("tensor", w2tr, inc="t")
                w2_trd[dc] = ev_t
                ph.wait("vector", ev_t)
                dst = w2T[:, :].rearrange("p (f d) -> p f d", f=FC)[
                    :, grp * 4:(grp + 1) * 4, dc * 128:(dc + 1) * 128]
                ev_e = ph.op("vector", lambda e, bank=bank, dst=dst:
                             e.tensor_copy(dst, bank[:, :].rearrange(
                                 "p (f t) -> p f t", f=4)), inc="v")
                pt_free[grp % 2] = ev_e
                w2T_ready = ev_e

        ev_b2 = ph.dma("sync", b2f_s[:, :], b2f_d[:, :])
        ph.wait("vector", ev_rs2)
        ph.wait("vector", ev_b2)
        ph.wait("vector", w2_amax[DC - 1])
        ev_c2 = ph.op("vector", lambda e: e.tensor_scalar_mul(
            c2[:, :], sw2[:, :], s2bc[:, 0:1]), inc="v")
        ph.wait("vector", ev_c2)
        ev_c2 = ph.op("vector", lambda e: e.reciprocal(
            tqf[:, 0:DC], c2[:, :]), inc="v")
        ph.wait("vector", ev_c2)
        ev_c2 = ph.op("vector", lambda e: e.tensor_mul(
            tqf[:, 0:DC], b2f_s[:, :], tqf[:, 0:DC]), inc="v")
        ph.wait("scalar", ev_c2)
        ev_b2r = ph.op("scalar", lambda e: e.activation(
            tq32[:, 0:DC], tqf[:, 0:DC], AF.Identity), inc="s")
        ph.wait("scalar", ev_b2r)
        ev_b2r = ph.op("scalar", lambda e: e.activation(
            tqf[:, 0:DC], tq32[:, 0:DC], AF.Identity), inc="s")
        ph.wait("vector", ev_b2r)
        ev_b2q = ph.op("vector", lambda e: e.tensor_mul(
            b2qf[:, :], tqf[:, 0:DC], c2[:, :]), inc="v")
        ph.wait("gpsimd", ev_b2q)
        ev1 = ph.dma("gpsimd", c2r_d[:, :], c2[:, :])
        ev2 = ph.dma("gpsimd", b2qr_d[:, :], b2qf[:, :])
        ph.wait("gpsimd", ev2)
        src_c = bass.AP(c2r_d.tensor, 0, [[1, DC], [DC, 128]])
        ev3 = ph.op("gpsimd", lambda e: e.dma_start(
            out=c2row_d[:].rearrange("(d p) -> d p", d=DC), in_=src_c,
            allow_slow_non_contiguous=True), inc="st", incby=16)
        ph.wait("gpsimd", ev3)
        ev4 = ph.dma("gpsimd", c2t[:, :],
                     c2row_d[None, :].partition_broadcast(128))
        src_b = bass.AP(b2qr_d.tensor, 0, [[1, DC], [DC, 128]])
        ev5a = ph.op("gpsimd", lambda e: e.dma_start(
            out=b2row_d[:].rearrange("(d p) -> d p", d=DC),
            in_=src_b, allow_slow_non_contiguous=True), inc="st", incby=16)
        ph.wait("gpsimd", ev5a)
        ev5 = ph.dma("gpsimd", rows_f[0:1, 0:1024], b2row_d[None, :])
        ph.wait("vector", ev5)
        ev_b2row = ph.op("vector", lambda e: e.tensor_copy(
            b2q_row[0:1, :], rows_f[0:1, 0:1024]), inc="v")
        ph.wait("gpsimd", ev4)
        ph.run(nc)
        t_es.close()

        d_es = ExitStack()
        sbd = lambda n, s, d: d_es.enter_context(nc.sbuf_tensor(n, s, d))
        hld = [sbd("hld0", [128, 512], F32), sbd("hld1", [128, 512], F32)]
        h_u8 = sbd("h_u8", [128, 512], U8)
        hq = [sbd("hq0", [128, FC * 512], BF16), sbd("hq1", [128, FC * 512], BF16)]
        r1res = [sbd("r1res0", [128, 1024], F32), sbd("r1res1", [128, 1024], F32)]
        t2 = sbd("t2", [128, 1024], F32)
        rpre2 = [sbd("rpre20", [128, 1024], F32), sbd("rpre21", [128, 1024], F32)]
        yt = [sbd("yt0", [128, 1024], F32), sbd("yt1", [128, 1024], F32)]

        ph = Ph(k)
        for eng in ENGS:
            ph.wait(eng, ev_b2row)
            ph.wait(eng, w2T_ready)
            ph.wait(eng, ev_rs2)
        hq_cons = {}
        r1r_cons = {}
        yt_store = {}
        mm_t = {}
        mm_v = {}
        pend2 = None

        def flush2(ph):
            nonlocal pend2
            if pend2 is None:
                return
            ptt, ppar, pev_var = pend2
            ev_rstd = ln_b(ph, ppar, pev_var)
            pbb, ptl = ptt // NT, ptt % NT
            ph.wait("vector", yt_store.get(ptt - 2))
            ev_y = ln_c(ph, ppar, rpre2[ppar][:, :], yt[ptt % 2][:, :],
                        ev_rstd)
            ph.wait("gpsimd", ev_y)
            yt_store[ptt] = ph.dma(
                "gpsimd", y_d[pbb, ptl * 128:(ptl + 1) * 128, :],
                yt[ptt % 2][:, :])
            pend2 = None

        for it in range(8):
            hqb = hq[it % 2]
            for fc in range(FC):
                gidx = it * FC + fc
                eng = "sync" if fc % 2 == 0 else "gpsimd"
                buf = hld[fc % 2]
                ph.wait(eng, hq_cons.get(gidx - 2))
                if it >= 2 and fc in (0, 1):
                    ph.wait(eng, mm_t.get((it - 2, 3)))
                ev_l = ph.dma(eng, buf[:, :],
                              h_d[fc, :, it * 512:(it + 1) * 512])
                ph.wait("scalar", ev_l)
                if it >= 2 and fc == 0:
                    ph.wait("scalar", mm_t.get((it - 2, 3)))
                def cq8(e, fc=fc, buf=buf, hqb=hqb):
                    e.activation(h_u8[:, :], buf[:, :], AF.Identity,
                                 scale=rs2[:, 0:1])
                    return e.activation(hqb[:, fc * 512:(fc + 1) * 512],
                                        h_u8[:, :], AF.Identity)
                hq_cons[gidx] = ph.op("scalar", cq8, inc="s")
            hq_ready = k.evt("s")
            flush2(ph)
            for qp in range(4):
                tt = it * 4 + qp
                par = tt % 2
                pp = pq if qp % 2 == 0 else pv2
                if tt >= 2:
                    ph.wait("sync", r1r_cons[tt - 2])
                ev_r = ph.dma("sync", r1res[par][:, :],
                              r1_d[tt * 128:(tt + 1) * 128, :])
                ph.wait("tensor", hq_ready)
                ph.wait("tensor", mm_v.get((it, qp - 2)) if qp >= 2
                        else mm_v.get((it - 1, qp + 2)))
                def mm2(e, qp=qp, pp=pp, hqb=hqb):
                    for n2 in range(2):
                        pslc = pp[:, n2 * 512:(n2 + 1) * 512]
                        e.matmul(pslc, ones_bf[0:1, 0:128],
                                 b2q_row[0:1, n2 * 512:(n2 + 1) * 512],
                                 start=True, stop=False)
                        for fc in range(FC):
                            ins = e.matmul(
                                pslc,
                                hqb[:, fc * 512 + qp * 128:fc * 512 + (qp + 1) * 128],
                                w2T[:, fc * 1024 + n2 * 512:fc * 1024 + (n2 + 1) * 512],
                                start=False, stop=(fc == FC - 1))
                    return ins
                ev_m = ph.op("tensor", mm2, inc="t")
                mm_t[(it, qp)] = ev_m
                flush2(ph)
                ph.wait("vector", ev_m)
                ph.wait("vector", ev_r)
                ph.wait("vector", yt_store.get(tt - 2))
                def ev2f(e, pp=pp, par=par):
                    e.scalar_tensor_tensor(t2[:, :], pp[:, :], 1.0, c2t[:, :],
                                           ALU.mult, ALU.mult)
                    return e.tensor_add(rpre2[par][:, :], t2[:, :],
                                        r1res[par][:, :])
                ev_rp = ph.op("vector", ev2f, inc="v")
                mm_v[(it, qp)] = ev_rp
                r1r_cons[tt] = ev_rp
                ev_var = ln_a(ph, par, rpre2[par][:, :], 1024, ev_rp)
                pend2 = (tt, par, ev_var)
        flush2(ph)
        ph.wait("gpsimd", k.evt("st"))
        ph.run(nc)
        d_es.close()
        c_es.close()

    return nc


_NC_CACHE = None


def kernel(L0_attn_x, L0_attn_Wq, L0_attn_bq, L0_attn_Wk, L0_attn_bk,
           L0_attn_Wv, L0_attn_bv, L0_attn_Wo, L0_attn_bo,
           L0_ffn_w1, L0_ffn_b1, L0_ffn_w2, L0_ffn_b2):
    global _NC_CACHE
    x = np.ascontiguousarray(np.asarray(L0_attn_x, dtype=np.float32))
    wq = np.ascontiguousarray(np.asarray(L0_attn_Wq, dtype=np.float32))
    wk = np.ascontiguousarray(np.asarray(L0_attn_Wk, dtype=np.float32))
    wv = np.ascontiguousarray(np.asarray(L0_attn_Wv, dtype=np.float32))
    wo = np.ascontiguousarray(np.asarray(L0_attn_Wo, dtype=np.float32))
    rows = np.concatenate([
        np.asarray(L0_attn_bq, dtype=np.float32).reshape(-1),
        np.asarray(L0_attn_bo, dtype=np.float32).reshape(-1),
        np.asarray(L0_attn_bk, dtype=np.float32).reshape(-1),
        np.asarray(L0_attn_bv, dtype=np.float32).reshape(-1)]).reshape(1, 2560)
    rows = np.ascontiguousarray(rows)
    w1 = np.ascontiguousarray(np.asarray(L0_ffn_w1, dtype=np.float32))
    w2 = np.ascontiguousarray(np.asarray(L0_ffn_w2, dtype=np.float32))
    b1f = np.ascontiguousarray(
        np.asarray(L0_ffn_b1, dtype=np.float32).reshape(FC, 128).T)
    b2f = np.ascontiguousarray(
        np.asarray(L0_ffn_b2, dtype=np.float32).reshape(DC, 128).T)

    if _NC_CACHE is None:
        _NC_CACHE = build_nc()
    nc = _NC_CACHE

    in_maps = []
    for c in range(8):
        in_maps.append({
            "x": np.ascontiguousarray(x[2 * c:2 * c + 2]),
            "wq": wq, "wk": wk, "wv": wv, "wo": wo, "rows": rows,
            "w1": w1, "w2": w2, "b1f": b1f, "b2f": b2f,
        })
    res = run_bass_kernel_spmd(nc, in_maps, core_ids=list(range(8)))
    out = np.concatenate([r["y"] for r in res.results], axis=0)
    return np.ascontiguousarray(out.astype(np.float32))


# version 22
# speedup vs baseline: 1.3292x; 1.0844x over previous
import numpy as np
from contextlib import ExitStack

import concourse.bass as bass
import concourse.mybir as mybir
from concourse.bass_utils import run_bass_kernel_spmd

F32 = mybir.dt.float32
BF16 = mybir.dt.bfloat16
U8 = mybir.dt.uint8
I8 = mybir.dt.int8
I32 = mybir.dt.int32
AF = mybir.ActivationFunctionType
ALU = mybir.AluOpType
AX = mybir.AxisListType.X

B = 2
T = 2048
C = 1024
NH = 16
DC = C // 128
NT = T // 128
F = 4096
FC = F // 128
EPS_Q = 1e-8
EPS_LN = 1e-5
NEG = -1e30

ENGS = ("sync", "tensor", "vector", "scalar", "gpsimd")


class KCtx:
    def __init__(self, nc, es):
        self.nc = nc
        self.sems = {}
        self.cnt = {}
        for name in ("ld", "st", "t", "v", "s", "g", "cc"):
            self.sems[name] = es.enter_context(nc.semaphore(name))
            self.cnt[name] = 0

    def inc(self, name, by=1):
        self.cnt[name] += by
        return (name, self.cnt[name])

    def evt(self, name):
        return (name, self.cnt[name])


class Ph:

    def __init__(self, k):
        self.k = k
        self.q = {e: [] for e in ENGS}
        self.marks = {e: {} for e in ENGS}

    def wait(self, eng, evt):
        if evt is None:
            return
        name, val = evt
        if val <= 0 or self.marks[eng].get(name, 0) >= val:
            return
        self.marks[eng][name] = val
        sem = self.k.sems[name]
        self.q[eng].append(lambda e, s=sem, v=val: e.wait_ge(s, v))

    def op(self, eng, fn, inc=None, incby=1):
        if inc is not None:
            sem = self.k.sems[inc]
            evt = self.k.inc(inc, incby)

            def g(e, fn=fn, sem=sem, incby=incby):
                fn(e).then_inc(sem, incby)

            self.q[eng].append(g)
            return evt
        self.q[eng].append(lambda e, fn=fn: fn(e))
        return None

    def dma(self, eng, out, in_):
        name = "ld" if eng == "sync" else "st"
        return self.op(eng, lambda e, o=out, i=in_: e.dma_start(out=o, in_=i),
                       inc=name, incby=16)

    def run(self, nc):
        with nc.Block() as block:
            for eng in ENGS:
                if self.q[eng]:
                    def body(e, lst=tuple(self.q[eng])):
                        for f in lst:
                            f(e)
                    getattr(block, eng)(body)


def build_nc():
    nc = bass.Bass("TRN2", target_bir_lowering=False, debug=False,
                   detect_race_conditions=False)

    x_d = nc.dram_tensor("x", [B, T, C], F32, kind="ExternalInput").ap()
    wq_d = nc.dram_tensor("wq", [C, C], F32, kind="ExternalInput").ap()
    wk_d = nc.dram_tensor("wk", [C, 256], F32, kind="ExternalInput").ap()
    wv_d = nc.dram_tensor("wv", [C, 256], F32, kind="ExternalInput").ap()
    wo_d = nc.dram_tensor("wo", [C, C], F32, kind="ExternalInput").ap()
    rows_d = nc.dram_tensor("rows", [1, 2560], F32, kind="ExternalInput").ap()
    w1_d = nc.dram_tensor("w1", [F, C], F32, kind="ExternalInput").ap()
    w2_d = nc.dram_tensor("w2", [C, F], F32, kind="ExternalInput").ap()
    b1f_d = nc.dram_tensor("b1f", [128, FC], F32, kind="ExternalInput").ap()
    b2f_d = nc.dram_tensor("b2f", [128, DC], F32, kind="ExternalInput").ap()
    y_d = nc.dram_tensor("y", [B, T, C], F32, kind="ExternalOutput").ap()

    r1_d = nc.dram_tensor("r1sc", [B * T, C], F32).ap()
    h_d = nc.dram_tensor("hsc", [FC, 128, B * T], F32).ap()
    m1_d = nc.dram_tensor("m1sc", [128], F32).ap()
    m2_d = nc.dram_tensor("m2sc", [128], F32).ap()
    s1_d = nc.dram_tensor("s1sc", [1], F32).ap()
    s2_d = nc.dram_tensor("s2sc", [1], F32).ap()
    cc1i_d = nc.dram_tensor("cc1i", [1], F32).ap()
    cc1o_d = nc.dram_tensor("cc1o", [8], F32).ap()
    cc2i_d = nc.dram_tensor("cc2i", [1], F32).ap()
    cc2o_d = nc.dram_tensor("cc2o", [8], F32).ap()
    c2r_d = nc.dram_tensor("c2r", [128, DC], F32).ap()
    b2qr_d = nc.dram_tensor("b2qr", [128, DC], F32).ap()
    c2row_d = nc.dram_tensor("c2row", [1024], F32).ap()
    b2row_d = nc.dram_tensor("b2row", [1024], F32).ap()

    es = ExitStack()
    sb = lambda n, s, d: es.enter_context(nc.sbuf_tensor(n, s, d))
    ps = lambda n, s, d: es.enter_context(nc.psum_tensor(n, s, d))
    k = KCtx(nc, es)

    pq = ps("pq", [128, 1024], F32)
    pv2 = ps("pv2", [128, 1024], F32)
    psc = [ps("psc0", [128, 512], F32), ps("psc1", [128, 512], F32)]
    ppt = [ps("ppt0", [128, 512], BF16), ps("ppt1", [128, 512], BF16)]
    SCORE = [psc[0][:, 0:384], psc[1][:, 0:384],
             pv2[:, 0:384], pv2[:, 512:896]]
    AV = [psc[0][:, 384:449], psc[1][:, 384:449],
          pv2[:, 384:449], pv2[:, 896:961]]

    idb = sb("idb", [128, 128], BF16)
    mask2 = sb("mask2", [128, 384], BF16)
    mask1 = sb("mask1", [128, 256], BF16)
    mask0 = sb("mask0", [128, 128], BF16)
    ones_bf = sb("ones_bf", [1, 512], BF16)
    rows_f = sb("rows_f", [1, 2560], F32)
    rows_b = sb("rows_b", [1, 2560], BF16)
    b2q_row = sb("b2q_row", [1, 1024], BF16)
    sq_scr = sb("sq_scr", [128, 1024], F32)
    c2t = sb("c2t", [128, 1024], F32)

    m1acc = sb("m1acc", [128, 1], F32)
    m2acc = sb("m2acc", [128, 1], F32)
    trow = sb("trow", [1, 128], F32)
    lmax = sb("lmax", [1, 1], F32)
    g8 = sb("g8", [1, 8], F32)
    gg = sb("gg", [1, 1], F32)
    s1bc = sb("s1bc", [128, 1], F32)
    rs1 = sb("rs1", [128, 1], F32)
    s2bc = sb("s2bc", [128, 1], F32)
    rs2 = sb("rs2", [128, 1], F32)
    sw1 = sb("sw1", [128, FC], F32)
    rsw1 = sb("rsw1", [128, FC], F32)
    c1 = sb("c1", [128, FC], F32)
    b1q = sb("b1q", [128, FC], F32)
    b1f_s = sb("b1f_s", [128, FC], F32)
    sw2 = sb("sw2", [128, DC], F32)
    rsw2 = sb("rsw2", [128, DC], F32)
    c2 = sb("c2", [128, DC], F32)
    b2qf = sb("b2qf", [128, DC], F32)
    b2f_s = sb("b2f_s", [128, DC], F32)
    tq32 = sb("tq32", [128, FC], I32)
    tqf = sb("tqf", [128, FC], F32)
    ln_sm = sb("ln_sm", [128, 2], F32)
    ln_ssq = sb("ln_ssq", [128, 2], F32)
    ln_mean = sb("ln_mean", [128, 2], F32)
    ln_var = sb("ln_var", [128, 2], F32)
    ln_lnv = sb("ln_lnv", [128, 2], F32)
    ln_rstd = sb("ln_rstd", [128, 2], F32)
    ln_t1 = sb("ln_t1", [128, 2], F32)
    ln_nmr = sb("ln_nmr", [128, 2], F32)

    def ln_a(ph, par, rpre_ap, n_free, sv_evt):
        p = slice(par, par + 1)
        ph.wait("vector", sv_evt)
        ev = ph.op("vector", lambda e: e.reduce_sum(
            ln_sm[:, p], rpre_ap, axis=AX), inc="v")
        ph.wait("vector", ev)
        ev = ph.op("vector", lambda e: e.tensor_scalar_mul(
            ln_mean[:, p], ln_sm[:, p], 1.0 / n_free), inc="v")
        ph.wait("scalar", sv_evt)
        ev_sq = ph.op("scalar", lambda e: e.activation(
            sq_scr[:, 0:n_free], rpre_ap, AF.Square, accum_out=ln_ssq[:, p]),
            inc="s")
        ph.wait("vector", ev_sq)
        ph.wait("vector", ev)
        ev = ph.op("vector", lambda e: e.tensor_scalar_mul(
            ln_ssq[:, p], ln_ssq[:, p], 1.0 / n_free), inc="v")
        evm = ph.op("vector", lambda e: e.tensor_mul(
            ln_var[:, p], ln_mean[:, p], ln_mean[:, p]), inc="v")
        ph.wait("vector", ev)
        ph.wait("vector", evm)
        ev = ph.op("vector", lambda e: e.tensor_sub(
            ln_var[:, p], ln_ssq[:, p], ln_var[:, p]), inc="v")
        ph.wait("vector", ev)
        ev_var = ph.op("vector", lambda e: e.tensor_scalar_add(
            ln_var[:, p], ln_var[:, p], EPS_LN), inc="v")
        return ev_var

    def ln_b(ph, par, ev_var):
        p = slice(par, par + 1)
        ph.wait("scalar", ev_var)
        ev = ph.op("scalar", lambda e: e.activation(
            ln_lnv[:, p], ln_var[:, p], AF.Ln), inc="s")
        ph.wait("scalar", ev)
        return ph.op("scalar", lambda e: e.activation(
            ln_rstd[:, p], ln_lnv[:, p], AF.Exp, scale=-0.5), inc="s")

    def ln_c(ph, par, rpre_ap, out_ap, ev_rstd):
        p = slice(par, par + 1)
        ph.wait("vector", ev_rstd)
        ev = ph.op("vector", lambda e: e.tensor_mul(
            ln_t1[:, p], ln_rstd[:, p], ln_rstd[:, p]), inc="v")
        ph.wait("vector", ev)
        ev = ph.op("vector", lambda e: e.tensor_mul(
            ln_t1[:, p], ln_t1[:, p], ln_var[:, p]), inc="v")
        ph.wait("vector", ev)
        ev = ph.op("vector", lambda e: e.tensor_scalar(
            ln_t1[:, p], ln_t1[:, p], -0.5, 1.5, ALU.mult, ALU.add), inc="v")
        ph.wait("vector", ev)
        ev = ph.op("vector", lambda e: e.tensor_mul(
            ln_rstd[:, p], ln_rstd[:, p], ln_t1[:, p]), inc="v")
        ph.wait("vector", ev)
        ev = ph.op("vector", lambda e: e.scalar_tensor_tensor(
            ln_nmr[:, p], ln_mean[:, p], -1.0, ln_rstd[:, p],
            ALU.mult, ALU.mult), inc="v")
        ph.wait("vector", ev)
        return ph.op("vector", lambda e: e.tensor_scalar(
            out_ap, rpre_ap, ln_rstd[:, par:par + 1], ln_nmr[:, par:par + 1],
            ALU.mult, ALU.add), inc="v")

    def gather_scale(ph, macc, md, cci, cco, sd, sbc, rsb):
        ev1 = ph.dma("gpsimd", md[:], macc[:, 0:1])
        ph.wait("gpsimd", ev1)
        ev2 = ph.dma("gpsimd", trow[0:1, :], md[None, :])
        ph.wait("vector", ev2)
        ev_lm = ph.op("vector", lambda e: e.reduce_max(
            lmax[:, :], trow[:, :], axis=AX), inc="v")
        ph.wait("gpsimd", ev_lm)
        ev3 = ph.dma("gpsimd", cci[:], lmax[0:1, 0:1])
        ph.wait("gpsimd", ev3)
        ev_cc = ph.op("gpsimd", lambda e: e.collective_compute(
            "AllGather", ALU.bypass, replica_groups=[list(range(8))],
            ins=[cci[:].opt()], outs=[cco[:].opt()]), inc="cc")
        ph.wait("gpsimd", ev_cc)
        ev4 = ph.dma("gpsimd", g8[0:1, :], cco[None, :])
        ph.wait("vector", ev4)
        evr = ph.op("vector", lambda e: e.reduce_max(gg[:, :], g8[:, :], axis=AX),
                    inc="v")
        ph.wait("vector", evr)
        ev_s = ph.op("vector", lambda e: e.tensor_scalar(
            gg[:, :], gg[:, :], EPS_Q, 1.0 / 255.0, ALU.max, ALU.mult), inc="v")
        ph.wait("gpsimd", ev_s)
        ev5 = ph.dma("gpsimd", sd[:], gg[0:1, 0:1])
        ph.wait("gpsimd", ev5)
        ev6 = ph.dma("gpsimd", sbc[:, :], sd[None, :].partition_broadcast(128))
        ph.wait("vector", ev6)
        return ph.op("vector", lambda e: e.reciprocal(rsb[:, :], sbc[:, :]),
                     inc="v")

    with es:
        a_es = ExitStack()
        sba = lambda n, s, d: a_es.enter_context(nc.sbuf_tensor(n, s, d))
        x_fm = sba("x_fm", [128, DC * T], BF16)
        q_fm = sba("q_fm", [128, DC * T], BF16)
        k2 = sba("k2", [128, 4 * T], BF16)
        v_tm = sba("v_tm", [128, NT * 260], BF16)
        wqb = sba("wqb", [128, DC * 1024], BF16)
        wkb = sba("wkb", [128, DC * 256], BF16)
        wvb = sba("wvb", [128, DC * 256], BF16)
        wob = sba("wob", [128, DC * 1024], BF16)
        xld = [sba("xld0", [128, 1024], F32), sba("xld1", [128, 1024], F32)]
        xbf = sba("xbf", [128, 1024], BF16)
        P = [sba("P%d" % i, [128, 384], BF16) for i in range(4)]
        PT = [sba("PT%d" % i, [128, 384], BF16) for i in range(4)]
        negm = sba("negm", [128, 4], F32)
        rl = sba("rl", [128, 4], F32)
        y_tm = sba("y_tm", [128, 1024], BF16)
        y_fm = sba("y_fm", [128, 1024], BF16)
        xres = [sba("xres0", [128, 1024], F32), sba("xres1", [128, 1024], F32)]
        rpre = [sba("rpre0", [128, 1024], F32), sba("rpre1", [128, 1024], F32)]
        r1t = [sba("r1t0", [128, 1024], F32), sba("r1t1", [128, 1024], F32)]
        tmx = sba("tmx", [128, 1], F32)
        wst = xld

        ph = Ph(k)

        def g_init(e):
            e.memset(idb[:, :], 1.0)
            e.affine_select(out=idb[:, :], in_=idb[:, :], pattern=[[-1, 128]],
                            compare_op=ALU.is_equal, fill=0.0, base=0,
                            channel_multiplier=1)
            e.memset(mask2[:, :], 0.0)
            e.affine_select(out=mask2[:, :], in_=mask2[:, :], pattern=[[1, 384]],
                            compare_op=ALU.is_ge, fill=NEG, base=0,
                            channel_multiplier=-1)
            e.affine_select(out=mask2[:, :], in_=mask2[:, :], pattern=[[-1, 384]],
                            compare_op=ALU.is_ge, fill=NEG, base=256,
                            channel_multiplier=1)
            e.memset(mask1[:, :], 0.0)
            e.affine_select(out=mask1[:, :], in_=mask1[:, :], pattern=[[-1, 256]],
                            compare_op=ALU.is_ge, fill=NEG, base=128,
                            channel_multiplier=1)
            e.memset(mask0[:, :], 0.0)
            e.affine_select(out=mask0[:, :], in_=mask0[:, :], pattern=[[-1, 128]],
                            compare_op=ALU.is_ge, fill=NEG, base=0,
                            channel_multiplier=1)
            e.memset(ones_bf[:, :], 1.0)
            e.memset(m1acc[:, :], NEG)
            e.memset(m2acc[:, :], 0.0)
            vap = v_tm[:, :].rearrange("p (t g c) -> p t g c", t=NT, g=4)
            e.memset(vap[:, :, :, 64:65], 1.0)
            return e.engine_nop()
        ev_g = ph.op("gpsimd", g_init, inc="g")

        ev_rows = ph.dma("sync", rows_f[:, :], rows_d[:, :])
        ph.wait("vector", ev_rows)
        ev_rowsb = ph.op("vector", lambda e: e.tensor_copy(
            rows_b[:, :], rows_f[:, :]), inc="v")

        wsrc = [(wq_d, wqb, 1024), (wo_d, wob, 1024), (wk_d, wkb, 256),
                (wv_d, wvb, 256)]
        cast_evts = []
        i = 0
        for (src, dst, w) in wsrc:
            for dc in range(DC):
                buf = wst[i % 2]
                if i >= 2:
                    ph.wait("sync", cast_evts[i - 2])
                ev_l = ph.dma("sync", buf[:, 0:w], src[dc * 128:(dc + 1) * 128, :])
                ph.wait("vector", ev_l)
                ev_c = ph.op("vector", lambda e, d=dst, dc=dc, b=buf, w=w:
                             e.tensor_copy(d[:, dc * w:(dc + 1) * w], b[:, 0:w]),
                             inc="v")
                cast_evts.append(ev_c)
                i += 1
        ph.run(nc)

        ph = Ph(k)
        for eng in ENGS:
            ph.wait(eng, ev_g)
            ph.wait(eng, ev_rowsb)
            ph.wait(eng, cast_evts[-1])

        pt_free = {0: None, 1: None}
        r1_store = {}
        r1_mx = {}
        sl_exp = {i: None for i in range(4)}
        sl_nm = {i: None for i in range(4)}
        sl_ye = {i: None for i in range(4)}
        sl_pte = {i: None for i in range(4)}
        outproj_vread = None
        pend_stat = None
        pend_bc = None
        pending_tail = None
        ev_kd = None

        def flush_ln_c(ph):
            nonlocal pend_bc
            if pend_bc is None:
                return
            (gq, par, ev_var), ev_rstd = pend_bc
            pend_bc = None
            rb = gq % 2
            ph.wait("vector", r1_store.get(gq - 2))
            ph.wait("vector", r1_mx.get(gq - 2))
            ev_r1 = ln_c(ph, par, rpre[par][:, :], r1t[rb][:, :], ev_rstd)
            ph.wait("vector", ev_r1)
            ev_tm = ph.op("vector", lambda e, rb=rb: e.reduce_max(
                tmx[:, :], r1t[rb][:, :], axis=AX), inc="v")
            ph.wait("vector", ev_tm)
            r1_mx[gq] = ph.op("vector", lambda e: e.tensor_max(
                m1acc[:, :], m1acc[:, :], tmx[:, :]), inc="v")
            ph.wait("gpsimd", ev_r1)
            r1_store[gq] = ph.dma("gpsimd", r1_d[gq * 128:(gq + 1) * 128, :],
                                  r1t[rb][:, :])

        def flush_ln_b(ph):
            nonlocal pend_stat, pend_bc
            if pend_stat is None:
                return
            pend_bc = (pend_stat, ln_b(ph, pend_stat[1], pend_stat[2]))
            pend_stat = None

        for b in range(B):
            xfm_ready = None
            xbf_cons = None
            xld_cons = {}
            for tt in range(NT):
                buf = xld[tt % 2]
                if tt >= 2:
                    ph.wait("sync", xld_cons[tt - 2])
                ev_l = ph.dma("sync", buf[:, :],
                              x_d[b, tt * 128:(tt + 1) * 128, :])
                ph.wait("vector", ev_l)
                if xbf_cons is not None:
                    ph.wait("vector", xbf_cons)
                ev_cast = ph.op("vector", lambda e, buf=buf: e.tensor_copy(
                    xbf[:, :], buf[:, :]), inc="v")
                xld_cons[tt] = ev_cast
                for half in range(2):
                    bank = ppt[half]
                    ph.wait("tensor", ev_cast)
                    ph.wait("tensor", pt_free[half])
                    def tfn(e, half=half, bank=bank):
                        for j in range(4):
                            dc = half * 4 + j
                            ins = e.transpose(bank[:, j * 128:(j + 1) * 128],
                                              xbf[:, dc * 128:(dc + 1) * 128],
                                              idb[:, :])
                        return ins
                    ev_t = ph.op("tensor", tfn, inc="t")
                    ph.wait("vector", ev_t)
                    dst = x_fm[:, :].rearrange("p (d t) -> p d t", d=DC)[
                        :, half * 4:(half + 1) * 4, tt * 128:(tt + 1) * 128]
                    ev_e = ph.op("vector", lambda e, bank=bank, dst=dst:
                                 e.tensor_copy(dst, bank[:, :].rearrange(
                                     "p (d t) -> p d t", d=4)), inc="v")
                    pt_free[half] = ev_e
                    xfm_ready = ev_e
                xbf_cons = ev_t
            kv_ready = None
            q_ready = None
            for tw in range(4):
                for mc in range(2):
                    pslc = pq[:, mc * 512:(mc + 1) * 512]
                    ph.wait("tensor", xfm_ready)
                    ph.wait("tensor", outproj_vread)
                    if kv_ready is not None:
                        ph.wait("tensor", kv_ready)
                    def kfn(e, tw=tw, mc=mc, pslc=pslc):
                        e.matmul(pslc,
                                 rows_b[0:1, 2048 + mc * 128:2048 + (mc + 1) * 128],
                                 ones_bf[0:1, 0:512], start=True, stop=False)
                        for dc in range(DC):
                            ins = e.matmul(
                                pslc,
                                wkb[:, dc * 256 + mc * 128:dc * 256 + (mc + 1) * 128],
                                x_fm[:, dc * T + tw * 512:dc * T + (tw + 1) * 512],
                                start=False, stop=(dc == DC - 1))
                        return ins
                    ev_k = ph.op("tensor", kfn, inc="t")
                    ph.wait("vector", ev_k)
                    ph.wait("vector", ev_kd)
                    g0, g1 = 2 * mc, 2 * mc + 1
                    def kev(e, pslc=pslc, g0=g0, g1=g1, tw=tw):
                        e.tensor_copy(
                            k2[0:64, g0 * T + tw * 512:g0 * T + (tw + 1) * 512],
                            pslc[0:64, :])
                        return e.tensor_copy(
                            k2[64:128, g1 * T + tw * 512:g1 * T + (tw + 1) * 512],
                            pslc[64:128, :])
                    kv_ready = ph.op("vector", kev, inc="v")
                for mc in range(DC):
                    pslc = pq[:, (mc % 2) * 512:((mc % 2) + 1) * 512]
                    ph.wait("tensor", kv_ready if mc < 2 else q_ready)
                    def qfn(e, tw=tw, mc=mc, pslc=pslc):
                        e.matmul(pslc, rows_b[0:1, mc * 128:(mc + 1) * 128],
                                 ones_bf[0:1, 0:512], start=True, stop=False)
                        for dc in range(DC):
                            ins = e.matmul(
                                pslc,
                                wqb[:, dc * 1024 + mc * 128:dc * 1024 + (mc + 1) * 128],
                                x_fm[:, dc * T + tw * 512:dc * T + (tw + 1) * 512],
                                start=False, stop=(dc == DC - 1))
                        return ins
                    ev_q = ph.op("tensor", qfn, inc="t")
                    ph.wait("scalar", ev_q)
                    q_ready = ph.op("scalar", lambda e, tw=tw, mc=mc, pslc=pslc:
                                    e.activation(
                                        q_fm[:, mc * T + tw * 512:mc * T + (tw + 1) * 512],
                                        pslc, AF.Identity, scale=0.125), inc="s")
            ph.wait("gpsimd", kv_ready)
            ph.dma("gpsimd", k2[64:128, 0:T], k2[0:64, 0:T])
            ph.dma("gpsimd", k2[64:128, 2 * T:3 * T], k2[0:64, 2 * T:3 * T])
            ph.dma("gpsimd", k2[0:64, T:2 * T], k2[64:128, T:2 * T])
            ev_kd = ph.dma("gpsimd", k2[0:64, 3 * T:4 * T],
                           k2[64:128, 3 * T:4 * T])
            v_ready = None
            for tt in range(NT):
                pslc = pv2[:, 0:256]
                ph.wait("tensor", xfm_ready)
                ph.wait("tensor", sl_exp[2])
                ph.wait("tensor", sl_exp[3])
                ph.wait("tensor", sl_ye[2])
                ph.wait("tensor", sl_ye[3])
                if v_ready is not None:
                    ph.wait("tensor", v_ready)
                def vfn(e, tt=tt, pslc=pslc):
                    e.matmul(pslc, ones_bf[0:1, 0:128],
                             rows_b[0:1, 2304:2560], start=True, stop=False)
                    for dc in range(DC):
                        ins = e.matmul(
                            pslc, x_fm[:, dc * T + tt * 128:dc * T + (tt + 1) * 128],
                            wvb[:, dc * 256:(dc + 1) * 256],
                            start=False, stop=(dc == DC - 1))
                    return ins
                ev_v = ph.op("tensor", vfn, inc="t")
                ph.wait("vector", ev_v)
                def vev(e, tt=tt, pslc=pslc):
                    for g in range(4):
                        ins = e.tensor_copy(
                            v_tm[:, tt * 260 + g * 65:tt * 260 + g * 65 + 64],
                            pslc[:, g * 64:(g + 1) * 64])
                    return ins
                v_ready = ph.op("vector", vev, inc="v")

            xres_cons = {}
            for qt in range(NT):
                kts = min(qt + 1, 3)
                klen = kts * 128
                k0 = (qt - kts + 1) * 128
                mask = {1: mask0, 2: mask1, 3: mask2}[kts]
                gq = b * NT + qt
                par = gq % 2
                if qt >= 2:
                    ph.wait("sync", xres_cons.get(qt - 2))
                ev_xr = ph.dma("sync", xres[qt % 2][:, :],
                               x_d[b, qt * 128:(qt + 1) * 128, :])

                for step in range(NH + 1):
                    if step < NH:
                        h = step
                        sl = h % 4
                        g = h // 4
                        qoff = (h % 2) * 64
                        ph.wait("tensor", q_ready)
                        ph.wait("tensor", kv_ready)
                        ph.wait("tensor", ev_kd)
                        ph.wait("tensor", v_ready)
                        ph.wait("tensor", sl_exp[sl])
                        ph.wait("tensor", sl_nm[sl])
                        def qkfn(e, sl=sl, g=g, qoff=qoff, h=h, qt=qt, k0=k0,
                                 klen=klen, mask=mask):
                            e.matmul(SCORE[sl][:, 0:klen],
                                     q_fm[qoff:qoff + 64,
                                          (h // 2) * T + qt * 128:(h // 2) * T + (qt + 1) * 128],
                                     k2[qoff:qoff + 64, g * T + k0:g * T + k0 + klen],
                                     start=True, stop=False)
                            return e.matmul(SCORE[sl][:, 0:klen], idb[:, :],
                                            mask[:, 0:klen], start=False,
                                            stop=True)
                        ev_qk = ph.op("tensor", qkfn, inc="t")
                        ph.wait("vector", ev_qk)
                        ev_nm = ph.op("vector", lambda e, sl=sl, klen=klen:
                                      e.reduce_max(negm[:, sl:sl + 1],
                                                   SCORE[sl][:, 0:klen],
                                                   axis=AX, negate=True),
                                      inc="v")
                        sl_nm[sl] = ev_nm
                        ph.wait("scalar", ev_nm)
                        ev_exp = ph.op("scalar", lambda e, sl=sl, klen=klen:
                                       e.activation(P[sl][:, 0:klen],
                                                    SCORE[sl][:, 0:klen],
                                                    AF.Exp,
                                                    bias=negm[:, sl:sl + 1]),
                                       inc="s")
                        sl_exp[sl] = ev_exp
                        if step == 3:
                            flush_ln_b(ph)
                        if step == 8 and pending_tail is not None:
                            pending_tail(ph)
                            pending_tail = None
                    if step >= 1:
                        h2 = step - 1
                        sl2 = h2 % 4
                        g2 = h2 // 4
                        ph.wait("tensor", sl_exp[sl2])
                        ph.wait("tensor", pt_free[sl2 % 2])
                        def trfn(e, sl2=sl2, kts=kts):
                            for kt in range(kts):
                                ins = e.transpose(
                                    ppt[sl2 % 2][:, kt * 128:(kt + 1) * 128],
                                    P[sl2][:, kt * 128:(kt + 1) * 128],
                                    idb[:, :])
                            return ins
                        ev_tr = ph.op("tensor", trfn, inc="t")
                        ph.wait("vector", ev_tr)
                        ev_pte = ph.op("vector", lambda e, sl2=sl2, klen=klen:
                                       e.tensor_copy(PT[sl2][:, 0:klen],
                                                     ppt[sl2 % 2][:, 0:klen]),
                                       inc="v")
                        pt_free[sl2 % 2] = ev_pte
                        ph.wait("tensor", ev_pte)
                        ph.wait("tensor", sl_ye[sl2])
                        def avfn(e, sl2=sl2, kts=kts, k0=k0, g2=g2):
                            for kt in range(kts):
                                ktok = k0 // 128 + kt
                                ins = e.matmul(
                                    AV[sl2],
                                    PT[sl2][:, kt * 128:(kt + 1) * 128],
                                    v_tm[:, ktok * 260 + g2 * 65:ktok * 260 + (g2 + 1) * 65],
                                    start=(kt == 0), stop=(kt == kts - 1))
                            return ins
                        ev_av = ph.op("tensor", avfn, inc="t")
                        ph.wait("vector", ev_av)
                        ev_rl = ph.op("vector", lambda e, sl2=sl2: e.reciprocal(
                            rl[:, sl2:sl2 + 1], AV[sl2][:, 64:65]), inc="v")
                        ph.wait("vector", ev_rl)
                        ev_ye = ph.op("vector", lambda e, sl2=sl2, h2=h2:
                                      e.tensor_scalar_mul(
                                          y_tm[:, h2 * 64:(h2 + 1) * 64],
                                          AV[sl2][:, 0:64], rl[:, sl2:sl2 + 1]),
                                      inc="v")
                        sl_ye[sl2] = ev_ye
                        yev_last = ev_ye
                flush_ln_c(ph)

                for half in range(2):
                    bank = ppt[half]
                    ph.wait("tensor", yev_last)
                    ph.wait("tensor", pt_free[half])
                    def ytr(e, half=half, bank=bank):
                        for j in range(4):
                            c = half * 4 + j
                            ins = e.transpose(bank[:, j * 128:(j + 1) * 128],
                                              y_tm[:, c * 128:(c + 1) * 128],
                                              idb[:, :])
                        return ins
                    ev_yt = ph.op("tensor", ytr, inc="t")
                    ph.wait("vector", ev_yt)
                    ev_yfe = ph.op("vector", lambda e, half=half, bank=bank:
                                   e.tensor_copy(
                                       y_fm[:, half * 512:(half + 1) * 512],
                                       bank[:, :]), inc="v")
                    pt_free[half] = ev_yfe
                    yfm_ready = ev_yfe
                ph.wait("tensor", yfm_ready)
                ph.wait("tensor", outproj_vread)
                def ofn(e):
                    for n2 in range(2):
                        pslc = pq[:, n2 * 512:(n2 + 1) * 512]
                        e.matmul(pslc, ones_bf[0:1, 0:128],
                                 rows_b[0:1, 1024 + n2 * 512:1024 + (n2 + 1) * 512],
                                 start=True, stop=False)
                        for dc in range(DC):
                            ins = e.matmul(
                                pslc, y_fm[:, dc * 128:(dc + 1) * 128],
                                wob[:, dc * 1024 + n2 * 512:dc * 1024 + (n2 + 1) * 512],
                                start=False, stop=(dc == DC - 1))
                    return ins
                ev_o = ph.op("tensor", ofn, inc="t")
                def tail(ph, gq=gq, par=par, qt=qt, ev_o=ev_o, ev_xr=ev_xr):
                    nonlocal outproj_vread, pend_stat
                    ph.wait("vector", ev_o)
                    ph.wait("vector", ev_xr)
                    ph.wait("vector", r1_store.get(gq - 2))
                    ev_rp = ph.op("vector", lambda e: e.tensor_add(
                        rpre[par][:, :], pq[:, :], xres[qt % 2][:, :]),
                        inc="v")
                    xres_cons[qt] = ev_rp
                    outproj_vread = ev_rp
                    pend_stat = (gq, par,
                                 ln_a(ph, par, rpre[par][:, :], 1024, ev_rp))
                pending_tail = tail
            flush_ln_b(ph)
            flush_ln_c(ph)
            if pending_tail is not None:
                pending_tail(ph)
                pending_tail = None
            flush_ln_b(ph)
            flush_ln_c(ph)

        ph.wait("gpsimd", k.evt("v"))
        ph.wait("gpsimd", r1_store[B * NT - 1])
        ev_rs1 = gather_scale(ph, m1acc, m1_d, cc1i_d, cc1o_d, s1_d, s1bc, rs1)
        ph.run(nc)
        a_es.close()

        b_es = ExitStack()
        sbb = lambda n, s, d: b_es.enter_context(nc.sbuf_tensor(n, s, d))
        w1T = sbb("w1T", [128, DC * F], BF16)
        w1st = [sbb("w1st0", [128, 1024], F32), sbb("w1st1", [128, 1024], F32)]
        w1i8 = sbb("w1i8", [128, 1024], I8)
        w1cb = sbb("w1cb", [128, 1024], BF16)
        r1ld = [sbb("r1ld0", [128, 4096], F32), sbb("r1ld1", [128, 4096], F32)]
        xq_u8 = sbb("xq_u8", [128, 4096], U8)
        xq_bf = sbb("xq_bf", [128, 4096], BF16)
        xq_fm = sbb("xq_fm", [128, DC * 512], BF16)
        h_t = [sbb("h_t0", [128, 512], F32), sbb("h_t1", [128, 512], F32)]
        hmx = sbb("hmx", [128, 1], F32)
        amax_t = sbb("amax_t", [128, 1], F32)

        ph = Ph(k)
        src0 = r1_d[0:512, :].rearrange("(q p) c -> p q c", p=128)
        ev_l0 = ph.dma("sync", r1ld[0][:, :].rearrange("p (q c) -> p q c", q=4),
                       src0)
        pt_free = {0: None, 1: None}
        w1_cons = {}
        w1_amax = {}
        w1_trd = {}
        w1_evt = {}
        w1T_ready = None
        for fc in range(FC):
            eng = "sync" if fc % 2 == 0 else "gpsimd"
            buf = w1st[fc % 2]
            if fc >= 2:
                ph.wait(eng, w1_cons[fc - 2])
                ph.wait(eng, w1_amax[fc - 2])
            ev_l = ph.dma(eng, buf[:, :], w1_d[fc * 128:(fc + 1) * 128, :])
            ph.wait("vector", ev_l)
            ev_a = ph.op("vector", lambda e, buf=buf: e.reduce_max(
                amax_t[:, :], buf[:, :], axis=AX,
                apply_absolute_value=True), inc="v")
            ph.wait("vector", ev_a)
            ev_a = ph.op("vector", lambda e, fc=fc: e.tensor_scalar(
                sw1[:, fc:fc + 1], amax_t[:, :], EPS_Q,
                1.0 / 127.0, ALU.max, ALU.mult), inc="v")
            ph.wait("vector", ev_a)
            ev_a = ph.op("vector", lambda e, fc=fc: e.reciprocal(
                rsw1[:, fc:fc + 1], sw1[:, fc:fc + 1]), inc="v")
            w1_amax[fc] = ev_a
            ph.wait("scalar", ev_a)
            ph.wait("scalar", w1_trd.get(fc - 1))
            def w1qz(e, fc=fc, buf=buf):
                e.activation(w1i8[:, :], buf[:, :], AF.Identity,
                             scale=rsw1[:, fc:fc + 1])
                return e.activation(w1cb[:, :], w1i8[:, :], AF.Identity)
            ev_qz = ph.op("scalar", w1qz, inc="s")
            w1_cons[fc] = ev_qz
            for half in range(2):
                bank = ppt[half]
                ph.wait("tensor", ev_qz)
                ph.wait("tensor", pt_free[half])
                def w1tr(e, half=half, bank=bank):
                    for j in range(4):
                        dc = half * 4 + j
                        ins = e.transpose(bank[:, j * 128:(j + 1) * 128],
                                          w1cb[:, dc * 128:(dc + 1) * 128],
                                          idb[:, :])
                    return ins
                ev_t = ph.op("tensor", w1tr, inc="t")
                w1_trd[fc] = ev_t
                ph.wait("vector", ev_t)
                dst = w1T[:, :].rearrange("p (d f) -> p d f", d=DC)[
                    :, half * 4:(half + 1) * 4, fc * 128:(fc + 1) * 128]
                ev_e = ph.op("vector", lambda e, bank=bank, dst=dst:
                             e.tensor_copy(dst, bank[:, :].rearrange(
                                 "p (d t) -> p d t", d=4)), inc="v")
                pt_free[half] = ev_e
                w1T_ready = ev_e
            w1_evt[fc] = ev_e

        ev_b1 = ph.dma("sync", b1f_s[:, :], b1f_d[:, :])
        ph.wait("vector", ev_rs1)
        ph.wait("vector", ev_b1)
        ph.wait("vector", w1_amax[FC - 1])
        ev_c1 = ph.op("vector", lambda e: e.tensor_scalar_mul(
            c1[:, :], sw1[:, :], s1bc[:, 0:1]), inc="v")
        ph.wait("vector", ev_c1)
        ev_c1 = ph.op("vector", lambda e: e.reciprocal(
            tqf[:, :], c1[:, :]), inc="v")
        ph.wait("vector", ev_c1)
        ev_c1 = ph.op("vector", lambda e: e.tensor_mul(
            tqf[:, :], b1f_s[:, :], tqf[:, :]), inc="v")
        ph.wait("scalar", ev_c1)
        ev_b1r = ph.op("scalar", lambda e: e.activation(
            tq32[:, :], tqf[:, :], AF.Identity), inc="s")
        ph.wait("scalar", ev_b1r)
        ev_b1r = ph.op("scalar", lambda e: e.activation(
            tqf[:, :], tq32[:, :], AF.Identity), inc="s")
        ph.wait("vector", ev_b1r)
        ev_b1q = ph.op("vector", lambda e: e.tensor_mul(
            b1q[:, :], tqf[:, :], c1[:, :]), inc="v")

        ht_hm = {}
        ht_ev = {}
        ht_st = {}
        r1ld_cons = {}
        xq_trdone = None
        for it in range(8):
            buf = r1ld[it % 2]
            if it == 0:
                ev_l = ev_l0
            else:
                if it >= 2:
                    ph.wait("sync", r1ld_cons[it - 2])
                src = r1_d[it * 512:(it + 1) * 512, :].rearrange(
                    "(q p) c -> p q c", p=128)
                ev_l = ph.dma("sync",
                              buf[:, :].rearrange("p (q c) -> p q c", q=4), src)
            ph.wait("scalar", ev_l)
            ph.wait("scalar", ev_rs1)
            ph.wait("scalar", xq_trdone)
            def bq8(e, buf=buf):
                e.activation(xq_u8[:, :], buf[:, :], AF.Identity,
                             scale=rs1[:, 0:1])
                return e.activation(xq_bf[:, :], xq_u8[:, :], AF.Identity)
            ev_q8 = ph.op("scalar", bq8, inc="s")
            r1ld_cons[it] = ev_q8
            xq_ready = None
            for dc in range(DC):
                bank = ppt[dc % 2]
                ph.wait("tensor", ev_q8)
                ph.wait("tensor", pt_free[dc % 2])
                def xtr(e, dc=dc, bank=bank):
                    for j in range(4):
                        ins = e.transpose(
                            bank[:, j * 128:(j + 1) * 128],
                            xq_bf[:, j * 1024 + dc * 128:j * 1024 + (dc + 1) * 128],
                            idb[:, :])
                    return ins
                ev_t = ph.op("tensor", xtr, inc="t")
                ph.wait("vector", ev_t)
                ev_e = ph.op("vector", lambda e, dc=dc, bank=bank: e.tensor_copy(
                    xq_fm[:, dc * 512:(dc + 1) * 512], bank[:, :]), inc="v")
                pt_free[dc % 2] = ev_e
                xq_ready = ev_e
            xq_trdone = k.evt("t")
            for fc in range(FC):
                gidx = it * FC + fc
                pslc = pq[:, (fc % 2) * 512:((fc % 2) + 1) * 512]
                ph.wait("tensor", xq_ready)
                ph.wait("tensor", w1_evt[fc] if it == 0 else w1T_ready)
                ph.wait("tensor", ht_ev.get(gidx - 2))
                def mm1(e, fc=fc, pslc=pslc):
                    for dc in range(DC):
                        ins = e.matmul(
                            pslc, w1T[:, dc * F + fc * 128:dc * F + (fc + 1) * 128],
                            xq_fm[:, dc * 512:(dc + 1) * 512],
                            start=(dc == 0), stop=(dc == DC - 1))
                    return ins
                ev_m = ph.op("tensor", mm1, inc="t")
                hb = gidx % 2
                ph.wait("scalar", ev_m)
                ph.wait("scalar", ev_b1q)
                ph.wait("scalar", ht_hm.get(gidx - 2))
                ph.wait("scalar", ht_st.get(gidx - 2))
                ev_h = ph.op("scalar", lambda e, fc=fc, hb=hb, pslc=pslc:
                             e.activation(h_t[hb][:, :], pslc, AF.Relu,
                                          scale=c1[:, fc:fc + 1],
                                          bias=b1q[:, fc:fc + 1]), inc="s")
                ht_ev[gidx] = ev_h
                ph.wait("vector", ev_h)
                ev_hx = ph.op("vector", lambda e, hb=hb: e.reduce_max(
                    hmx[:, :], h_t[hb][:, :], axis=AX), inc="v")
                ph.wait("vector", ev_hx)
                ht_hm[gidx] = ph.op("vector", lambda e: e.tensor_max(
                    m2acc[:, :], m2acc[:, :], hmx[:, :]), inc="v")
                ph.wait("gpsimd", ev_h)
                ht_st[gidx] = ph.dma("gpsimd",
                                     h_d[fc, :, it * 512:(it + 1) * 512],
                                     h_t[hb][:, :])

        ph.wait("gpsimd", k.evt("v"))
        ph.wait("gpsimd", ht_st[8 * FC - 1])
        ev_rs2 = gather_scale(ph, m2acc, m2_d, cc2i_d, cc2o_d, s2_d, s2bc, rs2)
        ph.run(nc)
        b_es.close()

        c_es = ExitStack()
        sbc = lambda n, s, d: c_es.enter_context(nc.sbuf_tensor(n, s, d))
        w2T = sbc("w2T", [128, FC * 1024], BF16)
        hld = [sbc("hld%d" % i, [128, 512], F32) for i in range(6)]
        h_u8 = sbc("h_u8", [128, 512], U8)
        hq = [sbc("hq0", [128, FC * 512], BF16), sbc("hq1", [128, FC * 512], BF16)]
        t_es = ExitStack()
        sbt = lambda n, s, d: t_es.enter_context(nc.sbuf_tensor(n, s, d))
        w2st = [sbt("w2st0", [128, 2048], F32), sbt("w2st1", [128, 2048], F32)]
        w2i8 = sbt("w2i8", [128, 2048], I8)
        w2cb = sbt("w2cb", [128, 2048], BF16)
        amax2 = sbt("amax2", [128, 2], F32)

        ph = Ph(k)
        hq_cons = {}
        for fc in range(FC):
            eng = "sync" if fc % 2 == 0 else "gpsimd"
            buf = hld[fc % 6]
            ph.wait(eng, hq_cons.get(fc - 6))
            ev_l = ph.dma(eng, buf[:, :], h_d[fc, :, 0:512])
            ph.wait("scalar", ev_l)
            ph.wait("scalar", ev_rs2)
            def cq8p(e, fc=fc, buf=buf):
                e.activation(h_u8[:, :], buf[:, :], AF.Identity,
                             scale=rs2[:, 0:1])
                return e.activation(hq[0][:, fc * 512:(fc + 1) * 512],
                                    h_u8[:, :], AF.Identity)
            hq_cons[fc] = ph.op("scalar", cq8p, inc="s")
        pt_free = {0: None, 1: None}
        w2_cons = {}
        w2_amax = {}
        w2_trd = {}
        w2T_ready = None
        for dc in range(DC):
            ev_l = {}
            for hf in range(2):
                if dc >= 1:
                    ph.wait("sync", w2_cons[(dc - 1, hf)])
                    ph.wait("sync", w2_amax[dc - 1])
                ev_l[hf] = ph.dma("sync", w2st[hf][:, :],
                                  w2_d[dc * 128:(dc + 1) * 128,
                                       hf * 2048:(hf + 1) * 2048])
                ph.wait("vector", ev_l[hf])
                ev_a = ph.op("vector", lambda e, hf=hf: e.reduce_max(
                    amax2[:, hf:hf + 1], w2st[hf][:, :], axis=AX,
                    apply_absolute_value=True), inc="v")
            ph.wait("vector", ev_a)
            ev_a = ph.op("vector", lambda e: e.tensor_max(
                amax2[:, 0:1], amax2[:, 0:1], amax2[:, 1:2]), inc="v")
            ph.wait("vector", ev_a)
            ev_a = ph.op("vector", lambda e, dc=dc: e.tensor_scalar(
                sw2[:, dc:dc + 1], amax2[:, 0:1], EPS_Q,
                1.0 / 127.0, ALU.max, ALU.mult), inc="v")
            ph.wait("vector", ev_a)
            ev_a = ph.op("vector", lambda e, dc=dc: e.reciprocal(
                rsw2[:, dc:dc + 1], sw2[:, dc:dc + 1]), inc="v")
            w2_amax[dc] = ev_a
            for hf in range(2):
                ph.wait("scalar", ev_a)
                ph.wait("scalar", w2_trd.get((dc, hf - 1)))
                ph.wait("scalar", w2_trd.get((dc - 1, 1)))
                def w2qz(e, dc=dc, hf=hf):
                    e.activation(w2i8[:, :], w2st[hf][:, :], AF.Identity,
                                 scale=rsw2[:, dc:dc + 1])
                    return e.activation(w2cb[:, :], w2i8[:, :], AF.Identity)
                ev_qz = ph.op("scalar", w2qz, inc="s")
                w2_cons[(dc, hf)] = ev_qz
                for grp in range(4):
                    bank = ppt[grp % 2]
                    ph.wait("tensor", ev_qz)
                    ph.wait("tensor", pt_free[grp % 2])
                    def w2tr(e, grp=grp, bank=bank):
                        for j in range(4):
                            fc = grp * 4 + j
                            ins = e.transpose(bank[:, j * 128:(j + 1) * 128],
                                              w2cb[:, fc * 128:(fc + 1) * 128],
                                              idb[:, :])
                        return ins
                    ev_t = ph.op("tensor", w2tr, inc="t")
                    w2_trd[(dc, hf)] = ev_t
                    ph.wait("vector", ev_t)
                    fbase = hf * 16 + grp * 4
                    dst = w2T[:, :].rearrange("p (f d) -> p f d", f=FC)[
                        :, fbase:fbase + 4, dc * 128:(dc + 1) * 128]
                    ev_e = ph.op("vector", lambda e, bank=bank, dst=dst:
                                 e.tensor_copy(dst, bank[:, :].rearrange(
                                     "p (f t) -> p f t", f=4)), inc="v")
                    pt_free[grp % 2] = ev_e
                    w2T_ready = ev_e

        ev_b2 = ph.dma("sync", b2f_s[:, :], b2f_d[:, :])
        ph.wait("vector", ev_rs2)
        ph.wait("vector", ev_b2)
        ph.wait("vector", w2_amax[DC - 1])
        ev_c2 = ph.op("vector", lambda e: e.tensor_scalar_mul(
            c2[:, :], sw2[:, :], s2bc[:, 0:1]), inc="v")
        ph.wait("vector", ev_c2)
        ev_c2 = ph.op("vector", lambda e: e.reciprocal(
            tqf[:, 0:DC], c2[:, :]), inc="v")
        ph.wait("vector", ev_c2)
        ev_c2 = ph.op("vector", lambda e: e.tensor_mul(
            tqf[:, 0:DC], b2f_s[:, :], tqf[:, 0:DC]), inc="v")
        ph.wait("scalar", ev_c2)
        ev_b2r = ph.op("scalar", lambda e: e.activation(
            tq32[:, 0:DC], tqf[:, 0:DC], AF.Identity), inc="s")
        ph.wait("scalar", ev_b2r)
        ev_b2r = ph.op("scalar", lambda e: e.activation(
            tqf[:, 0:DC], tq32[:, 0:DC], AF.Identity), inc="s")
        ph.wait("vector", ev_b2r)
        ev_b2q = ph.op("vector", lambda e: e.tensor_mul(
            b2qf[:, :], tqf[:, 0:DC], c2[:, :]), inc="v")
        ph.wait("gpsimd", ev_b2q)
        ev1 = ph.dma("gpsimd", c2r_d[:, :], c2[:, :])
        ev2 = ph.dma("gpsimd", b2qr_d[:, :], b2qf[:, :])
        ph.wait("gpsimd", ev2)
        src_c = bass.AP(c2r_d.tensor, 0, [[1, DC], [DC, 128]])
        ev3 = ph.op("gpsimd", lambda e: e.dma_start(
            out=c2row_d[:].rearrange("(d p) -> d p", d=DC), in_=src_c,
            allow_slow_non_contiguous=True), inc="st", incby=16)
        ph.wait("gpsimd", ev3)
        ev4 = ph.dma("gpsimd", c2t[:, :],
                     c2row_d[None, :].partition_broadcast(128))
        src_b = bass.AP(b2qr_d.tensor, 0, [[1, DC], [DC, 128]])
        ev5a = ph.op("gpsimd", lambda e: e.dma_start(
            out=b2row_d[:].rearrange("(d p) -> d p", d=DC),
            in_=src_b, allow_slow_non_contiguous=True), inc="st", incby=16)
        ph.wait("gpsimd", ev5a)
        ev5 = ph.dma("gpsimd", rows_f[0:1, 0:1024], b2row_d[None, :])
        ph.wait("vector", ev5)
        ev_b2row = ph.op("vector", lambda e: e.tensor_copy(
            b2q_row[0:1, :], rows_f[0:1, 0:1024]), inc="v")
        ph.wait("gpsimd", ev4)
        ph.run(nc)
        t_es.close()

        d_es = ExitStack()
        sbd = lambda n, s, d: d_es.enter_context(nc.sbuf_tensor(n, s, d))
        r1res = [sbd("r1res0", [128, 1024], F32), sbd("r1res1", [128, 1024], F32)]
        t2 = sbd("t2", [128, 1024], F32)
        rpre2 = [sbd("rpre20", [128, 1024], F32), sbd("rpre21", [128, 1024], F32)]
        yt = [sbd("yt0", [128, 1024], F32), sbd("yt1", [128, 1024], F32)]

        ph = Ph(k)
        for eng in ENGS:
            ph.wait(eng, ev_b2row)
            ph.wait(eng, w2T_ready)
            ph.wait(eng, ev_rs2)
        r1r_cons = {}
        yt_store = {}
        mm_t = {}
        mm_v = {}
        pend2 = None

        def flush2(ph):
            nonlocal pend2
            if pend2 is None:
                return
            ptt, ppar, pev_var = pend2
            ev_rstd = ln_b(ph, ppar, pev_var)
            pbb, ptl = ptt // NT, ptt % NT
            ph.wait("vector", yt_store.get(ptt - 2))
            ev_y = ln_c(ph, ppar, rpre2[ppar][:, :], yt[ptt % 2][:, :],
                        ev_rstd)
            ph.wait("gpsimd", ev_y)
            yt_store[ptt] = ph.dma(
                "gpsimd", y_d[pbb, ptl * 128:(ptl + 1) * 128, :],
                yt[ptt % 2][:, :])
            pend2 = None

        for it in range(8):
            hqb = hq[it % 2]
            for fc in (range(FC) if it > 0 else ()):
                gidx = it * FC + fc
                eng = "sync" if fc % 2 == 0 else "gpsimd"
                buf = hld[fc % 6]
                ph.wait(eng, hq_cons.get(gidx - 6))
                if it >= 2 and fc in (0, 1, 2, 3, 4, 5):
                    ph.wait(eng, mm_t.get((it - 2, 3)))
                ev_l = ph.dma(eng, buf[:, :],
                              h_d[fc, :, it * 512:(it + 1) * 512])
                ph.wait("scalar", ev_l)
                if it >= 2 and fc == 0:
                    ph.wait("scalar", mm_t.get((it - 2, 3)))
                def cq8(e, fc=fc, buf=buf, hqb=hqb):
                    e.activation(h_u8[:, :], buf[:, :], AF.Identity,
                                 scale=rs2[:, 0:1])
                    return e.activation(hqb[:, fc * 512:(fc + 1) * 512],
                                        h_u8[:, :], AF.Identity)
                hq_cons[gidx] = ph.op("scalar", cq8, inc="s")
            hq_ready = k.evt("s")
            flush2(ph)
            for qp in range(4):
                tt = it * 4 + qp
                par = tt % 2
                pp = pq if qp % 2 == 0 else pv2
                if tt >= 2:
                    ph.wait("sync", r1r_cons[tt - 2])
                ev_r = ph.dma("sync", r1res[par][:, :],
                              r1_d[tt * 128:(tt + 1) * 128, :])
                ph.wait("tensor", hq_ready)
                ph.wait("tensor", mm_v.get((it, qp - 2)) if qp >= 2
                        else mm_v.get((it - 1, qp + 2)))
                def mm2(e, qp=qp, pp=pp, hqb=hqb):
                    for n2 in range(2):
                        pslc = pp[:, n2 * 512:(n2 + 1) * 512]
                        e.matmul(pslc, ones_bf[0:1, 0:128],
                                 b2q_row[0:1, n2 * 512:(n2 + 1) * 512],
                                 start=True, stop=False)
                        for fc in range(FC):
                            ins = e.matmul(
                                pslc,
                                hqb[:, fc * 512 + qp * 128:fc * 512 + (qp + 1) * 128],
                                w2T[:, fc * 1024 + n2 * 512:fc * 1024 + (n2 + 1) * 512],
                                start=False, stop=(fc == FC - 1))
                    return ins
                ev_m = ph.op("tensor", mm2, inc="t")
                mm_t[(it, qp)] = ev_m
                flush2(ph)
                ph.wait("vector", ev_m)
                ph.wait("vector", ev_r)
                ph.wait("vector", yt_store.get(tt - 2))
                def ev2f(e, pp=pp, par=par):
                    e.scalar_tensor_tensor(t2[:, :], pp[:, :], 1.0, c2t[:, :],
                                           ALU.mult, ALU.mult)
                    return e.tensor_add(rpre2[par][:, :], t2[:, :],
                                        r1res[par][:, :])
                ev_rp = ph.op("vector", ev2f, inc="v")
                mm_v[(it, qp)] = ev_rp
                r1r_cons[tt] = ev_rp
                ev_var = ln_a(ph, par, rpre2[par][:, :], 1024, ev_rp)
                pend2 = (tt, par, ev_var)
        flush2(ph)
        ph.wait("gpsimd", k.evt("st"))
        ph.run(nc)
        d_es.close()
        c_es.close()

    return nc


_NC_CACHE = None


def kernel(L0_attn_x, L0_attn_Wq, L0_attn_bq, L0_attn_Wk, L0_attn_bk,
           L0_attn_Wv, L0_attn_bv, L0_attn_Wo, L0_attn_bo,
           L0_ffn_w1, L0_ffn_b1, L0_ffn_w2, L0_ffn_b2):
    global _NC_CACHE
    x = np.ascontiguousarray(np.asarray(L0_attn_x, dtype=np.float32))
    wq = np.ascontiguousarray(np.asarray(L0_attn_Wq, dtype=np.float32))
    wk = np.ascontiguousarray(np.asarray(L0_attn_Wk, dtype=np.float32))
    wv = np.ascontiguousarray(np.asarray(L0_attn_Wv, dtype=np.float32))
    wo = np.ascontiguousarray(np.asarray(L0_attn_Wo, dtype=np.float32))
    rows = np.concatenate([
        np.asarray(L0_attn_bq, dtype=np.float32).reshape(-1),
        np.asarray(L0_attn_bo, dtype=np.float32).reshape(-1),
        np.asarray(L0_attn_bk, dtype=np.float32).reshape(-1),
        np.asarray(L0_attn_bv, dtype=np.float32).reshape(-1)]).reshape(1, 2560)
    rows = np.ascontiguousarray(rows)
    w1 = np.ascontiguousarray(np.asarray(L0_ffn_w1, dtype=np.float32))
    w2 = np.ascontiguousarray(np.asarray(L0_ffn_w2, dtype=np.float32))
    b1f = np.ascontiguousarray(
        np.asarray(L0_ffn_b1, dtype=np.float32).reshape(FC, 128).T)
    b2f = np.ascontiguousarray(
        np.asarray(L0_ffn_b2, dtype=np.float32).reshape(DC, 128).T)

    if _NC_CACHE is None:
        _NC_CACHE = build_nc()
    nc = _NC_CACHE

    in_maps = []
    for c in range(8):
        in_maps.append({
            "x": np.ascontiguousarray(x[2 * c:2 * c + 2]),
            "wq": wq, "wk": wk, "wv": wv, "wo": wo, "rows": rows,
            "w1": w1, "w2": w2, "b1f": b1f, "b2f": b2f,
        })
    res = run_bass_kernel_spmd(nc, in_maps, core_ids=list(range(8)))
    out = np.concatenate([r["y"] for r in res.results], axis=0)
    return np.ascontiguousarray(out.astype(np.float32))
